# version 05
# baseline (speedup 1.0000x reference)
import contextlib
import numpy as np
import ml_dtypes
import concourse.bass as bass
import concourse.mybir as mybir
from concourse.bass_utils import run_bass_kernel_spmd

F32 = mybir.dt.float32
BF16 = mybir.dt.bfloat16
I32 = mybir.dt.int32
AF = mybir.ActivationFunctionType
ALU = mybir.AluOpType
AX = mybir.AxisListType
BF = ml_dtypes.bfloat16

SEQ = 8192
D = 1024
NCORE = 8
DBG_SSD = 8
DBG_ML = 0



def host_consts():
    c = {}
    c["ones_col_bf"] = np.ones((128, 1), BF)
    sel = np.zeros((32, 32 * 128), np.float32)
    for k in range(32):
        sel[k, 128 * k:128 * (k + 1)] = 1.0
    c["sel32"] = sel
    c["ident128"] = np.eye(128, dtype=np.float32)
    c["ident128b"] = np.eye(128, dtype=np.float32).astype(BF)
    c["ident32f"] = np.eye(32, dtype=np.float32)
    c["maskT"] = np.triu(np.ones((128, 128), np.float32)).astype(BF)
    c["mask8"] = np.ascontiguousarray(np.tile(np.triu(np.ones((128, 128), np.float32)), (1, 8))).astype(BF)
    c["ident64f"] = np.eye(64, dtype=np.float32)
    c["ones_f"] = np.ones((128, 1), np.float32)
    c["c44"] = np.full((64, 1), 44.0, np.float32)
    c["c44_128"] = np.full((128, 1), 44.0, np.float32)
    c["c44n"] = np.full((64, 1), -44.0, np.float32)
    c["eps1"] = np.full((1, 1), 1e-5, np.float32)
    p64 = np.zeros((64, 64), np.float32)
    for n in range(64):
        p64[4 * (n % 16) + n // 16, n] = 1.0
    c["perm64"] = p64
    return c



PRE_SEMS = ("s_ld", "s_cast", "s_sq", "s_sum", "s_bc", "s_v1", "s_sc",
            "s_row", "s_ap", "s_st")


def build_pre(nc=None, block=None, g=None, io=None, bar=None, ph=0):
    standalone = nc is None
    if standalone:
        nc = bass.Bass("TRN2", target_bir_lowering=False)
    def dram(name, shape, dt, kind):
        if io is not None:
            return io[name]
        return nc.dram_tensor(name, shape, dt, kind=kind).ap()
    xT = dram("xT", [D, SEQ], F32, "ExternalInput")
    ones_col = dram("ones_col_bf", [128, 1], BF16, "ExternalInput")
    sel32 = dram("sel32", [32, 32 * 128], F32, "ExternalInput")
    xn0T = dram("xn0T", [D, SEQ], BF16, "ExternalOutput")

    NT = SEQ // 512
    KT = D // 128

    xTr = xT.rearrange("(k p) n -> p k n", p=128)
    xnr = xn0T.rearrange("(k p) n -> p k n", p=128)

    with contextlib.ExitStack() as ctx:
        ent = ctx.enter_context
        xf = ent(nc.sbuf_tensor("xf", [128, 2, KT, 512], F32))
        xb = ent(nc.sbuf_tensor("xb", [128, 2, KT, 512], BF16))
        xsq = ent(nc.sbuf_tensor("xsq", [128, KT, 512], BF16))
        xn = ent(nc.sbuf_tensor("xn", [128, 2, KT, 512], BF16))
        xg = ent(nc.sbuf_tensor("xg", [128, 2, KT, 512], BF16))
        onesc = ent(nc.sbuf_tensor("onesc", [128, 1], BF16))
        selc = ent(nc.sbuf_tensor("selc", [32, 32 * 128], F32))
        rows = ent(nc.sbuf_tensor("rows", [32, 8 * 512], F32))
        sum_a = ent(nc.psum_tensor("sum_a", [1, 512], F32))
        sum_b = ent(nc.psum_tensor("sum_b", [1, 512], F32))
        bca = ent(nc.psum_tensor("bca", [128, 512], F32))
        bcb = ent(nc.psum_tensor("bcb", [128, 512], F32))
        if standalone:
            g = {}
            for name in PRE_SEMS:
                g[name] = ent(nc.semaphore(name))
            block = ent(nc.Block())
        (s_ld, s_cast, s_sq, s_sum, s_bc, s_v1, s_sc, s_row, s_ap,
         s_st) = (g[n] for n in PRE_SEMS)

        @block.sync
        def _(sync):
            if bar is not None:
                sync.wait_ge(bar, ph)
            sync.dma_start(out=onesc[:], in_=ones_col).then_inc(s_ld, 16)
            sync.dma_start(out=selc[:], in_=sel32).then_inc(s_ld, 16)
            for t in range(NT):
                b = t % 2
                if t >= 2:
                    sync.wait_ge(s_ap, t - 1)
                    sync.dma_start(out=xnr[:, :, 512 * (t - 2):512 * (t - 1)],
                                   in_=xn[:, t % 2]).then_inc(s_st, 16)
                sync.dma_start(out=xf[:, b], in_=xTr[:, :, 512 * t:512 * (t + 1)]).then_inc(s_ld, 16)
            for t in (NT - 2, NT - 1):
                sync.wait_ge(s_ap, t + 1)
                sync.dma_start(out=xnr[:, :, 512 * t:512 * (t + 1)], in_=xn[:, t % 2]).then_inc(s_st, 16)
            sync.wait_ge(s_st, 16 * NT)

        @block.gpsimd
        def _(gpsimd):
            if bar is not None:
                gpsimd.wait_ge(bar, ph)
            for t in range(NT):
                b = t % 2
                if t >= 2:
                    gpsimd.wait_ge(s_sum, t - 1)
                gpsimd.dma_start(out=xb[:, b], in_=xTr[:, :, 512 * t:512 * (t + 1)]).then_inc(s_cast, 16)

        @block.scalar
        def _(scalar):
            if bar is not None:
                scalar.wait_ge(bar, ph)
            for t in range(NT):
                b = t % 2
                scalar.wait_ge(s_ld, 32 + 16 * (t + 1))
                if t >= 1:
                    scalar.wait_ge(s_sum, t)
                last = None
                for k in range(KT):
                    last = scalar.activation(xsq[:, k], xf[:, b, k], AF.Square)
                last.then_inc(s_sq, 1)
                scalar.wait_ge(s_v1, t + 1)
                scalar.activation(rows[0:1, 6 * 512:7 * 512], rows[0:1, 5 * 512:6 * 512], AF.Ln)
                scalar.activation(rows[0:1, 0:512], rows[0:1, 6 * 512:7 * 512], AF.Exp,
                                  scale=-0.5).then_inc(s_sc, 1)

        @block.tensor
        def _(tensor):
            if bar is not None:
                tensor.wait_ge(bar, ph)
            tensor.wait_ge(s_ld, 32)
            for t in range(NT):
                b = t % 2
                tensor.wait_ge(s_cast, 16 * (t + 1))
                tensor.wait_ge(s_sq, t + 1)
                if t >= 1:
                    tensor.wait_ge(s_v1, t)
                for k in range(KT):
                    tensor.matmul(sum_a[:], onesc[:], xb[:, b, k], start=(k == 0), stop=(k == KT - 1))
                for k in range(KT):
                    last = tensor.matmul(sum_b[:], onesc[:], xsq[:, k], start=(k == 0), stop=(k == KT - 1))
                last.then_inc(s_sum, 1)
                tensor.wait_ge(s_row, t + 1)
                if t >= 1:
                    tensor.wait_ge(s_ap, t)
                tensor.matmul(bca[:], selc[:, 0:128], rows[:, 0:512], start=True, stop=True)
                tensor.matmul(bcb[:], selc[:, 0:128], rows[:, 512:1024], start=True, stop=True).then_inc(s_bc, 1)

        @block.vector
        def _(vector):
            if bar is not None:
                vector.wait_ge(bar, ph)
            vector.memset(rows[:], 0.0)
            for t in range(NT):
                b = t % 2
                vector.wait_ge(s_sum, t + 1)
                if t >= 1:
                    vector.wait_ge(s_bc, t)
                r = lambda j: rows[0:1, 512 * j:512 * (j + 1)]
                vector.tensor_scalar_mul(r(2), sum_a[:], 1.0 / D)
                vector.tensor_scalar_mul(r(3), sum_b[:], 1.0 / D)
                vector.tensor_tensor(r(4), r(2), r(2), ALU.mult)
                vector.tensor_tensor(r(5), r(3), r(4), ALU.subtract)
                vector.tensor_scalar_add(r(5), r(5), 1e-5).then_inc(s_v1, 1)
                vector.wait_ge(s_sc, t + 1)
                vector.tensor_tensor(r(7), r(2), r(0), ALU.mult)
                vector.tensor_scalar_mul(r(1), r(7), -1.0).then_inc(s_row, 1)
                vector.wait_ge(s_bc, t + 1)
                if t >= 2:
                    vector.wait_ge(s_st, 16 * (t - 1))
                last = None
                for k in range(KT):
                    vector.tensor_tensor(xn[:, b, k], xf[:, b, k], bca[:], ALU.mult)
                    last = vector.scalar_tensor_tensor(xn[:, b, k], xn[:, b, k], 1.0, bcb[:],
                                                       ALU.mult, ALU.add)
                last.then_inc(s_ap, 1)

    return nc



MF_SEMS = ("s_w", "s_wg", "s_ld", "s_mm", "s_evz", "s_evx", "s_conv", "s_silu",
           "s_dtv", "s_dts", "s_dtd", "s_st", "s_a")


def build_mamba_front(nc=None, block=None, g=None, io=None, bar=None, ph=0):
    standalone = nc is None
    if standalone:
        nc = bass.Bass("TRN2", target_bir_lowering=False)
    def dram(name, shape, dt, kind):
        if io is not None:
            return io[name]
        return nc.dram_tensor(name, shape, dt, kind=kind).ap()
    xn0T = dram("xn0T", [D, SEQ], BF16, "ExternalInput")
    in_proj_w = dram("in_proj_w", [D, 4384], F32, "ExternalInput")
    conv_w = dram("conv_wr", [128, 18, 4], F32, "ExternalInput")
    conv_b = dram("conv_br", [128, 18], F32, "ExternalInput")
    dt_bias = dram("dt_biasr", [32, 1], F32, "ExternalInput")
    A_log = dram("A_logr", [32, 1], F32, "ExternalInput")
    cmask = dram("cmask32", [32, 512], F32, "ExternalInput")

    szT = dram("szT", [2048, SEQ], BF16, "ExternalOutput")
    xcT = dram("xcT", [2048, SEQ], BF16, "ExternalOutput")
    BT = dram("BT", [128, SEQ], BF16, "ExternalOutput")
    CT = dram("CT", [128, SEQ], BF16, "ExternalOutput")
    dtq = dram("dtq", [32, SEQ], F32, "ExternalOutput")
    Acq = dram("Acq", [32, SEQ], F32, "ExternalOutput")

    NT = SEQ // 512
    KT = D // 128
    NB = 6
    MCH = 35

    xnr = xn0T.rearrange("(k p) n -> p k n", p=128)
    wr = in_proj_w.rearrange("(k p) m -> p k m", p=128)
    szr = szT.rearrange("(k p) n -> p k n", p=128)
    xcr = xcT.rearrange("(k p) n -> p k n", p=128)

    def prior_wait(eng, t, m):
        gi = t * MCH + m
        if gi < NB:
            return
        pg = gi - NB
        pt, pm = divmod(pg, MCH)
        if pm < 16:
            eng.wait_ge(S["s_evz"], 16 * pt + pm + 1)
        else:
            eng.wait_ge(S["s_evx"], 19 * pt + (pm - 16) + 1)

    S = {}
    with contextlib.ExitStack() as ctx:
        ent = ctx.enter_context
        wsb = ent(nc.sbuf_tensor("wsb", [128, KT, 4384], BF16))
        xn = ent(nc.sbuf_tensor("xn", [128, 2, KT, 512], BF16))
        xg = ent(nc.sbuf_tensor("xg", [128, 2, KT, 512], BF16))
        stage = ent(nc.sbuf_tensor("stage", [128, 18, 515], BF16))
        conv_y = ent(nc.sbuf_tensor("conv_y", [128, 18, 512], BF16))
        xbcs = ent(nc.sbuf_tensor("xbcs", [128, 18, 512], BF16))
        zs = ent(nc.sbuf_tensor("zs", [128, 16, 512], BF16))
        cwsb = ent(nc.sbuf_tensor("cwsb", [128, 18, 4], F32))
        cbsb = ent(nc.sbuf_tensor("cbsb", [128, 18], F32))
        dtb = ent(nc.sbuf_tensor("dtb", [32, 1], F32))
        alog = ent(nc.sbuf_tensor("alog", [32, 1], F32))
        acol = ent(nc.sbuf_tensor("acol", [32, 2], F32))
        cmsk = ent(nc.sbuf_tensor("cmsk", [32, 512], F32))
        dts = ent(nc.sbuf_tensor("dts", [32, 6, 512], F32))
        acum = ent(nc.sbuf_tensor("acum", [32, 2, 512], F32))
        psum = ent(nc.psum_tensor("psum", [128, NB, 512], F32))
        psdt = ent(nc.psum_tensor("psdt", [32, 512], F32))
        if standalone:
            for name in MF_SEMS:
                S[name] = ent(nc.semaphore(name))
            block = ent(nc.Block())
        else:
            S = g
        s_w, s_wg, s_ld, s_mm = S["s_w"], S["s_wg"], S["s_ld"], S["s_mm"]
        s_evz, s_evx, s_conv, s_silu = S["s_evz"], S["s_evx"], S["s_conv"], S["s_silu"]
        s_dtv, s_dts, s_dtd, s_st, s_a = S["s_dtv"], S["s_dts"], S["s_dtd"], S["s_st"], S["s_a"]

        @block.gpsimd
        def _(gpsimd):
            if bar is not None:
                gpsimd.wait_ge(bar, ph)
            gpsimd.dma_start(out=wsb[:], in_=wr).then_inc(s_wg, 16)
            for t in range(NT):
                b = t % 2
                if t >= 2:
                    gpsimd.wait_ge(s_mm, MCH * (t - 1))
                gpsimd.dma_start(out=xn[:, b], in_=xnr[:, :, 512 * t:512 * (t + 1)]).then_inc(s_ld, 16)

        @block.sync
        def _(sync):
            if bar is not None:
                sync.wait_ge(bar, ph)
            sync.dma_start(out=cwsb[:], in_=conv_w).then_inc(s_w, 16)
            sync.dma_start(out=cbsb[:], in_=conv_b).then_inc(s_w, 16)
            sync.dma_start(out=dtb[:], in_=dt_bias).then_inc(s_w, 16)
            sync.dma_start(out=alog[:], in_=A_log).then_inc(s_w, 16)
            sync.dma_start(out=cmsk[:], in_=cmask).then_inc(s_w, 16)
            for t in range(NT):
                sync.wait_ge(s_evz, 16 * (t + 1))
                sync.dma_start(out=szr[:, :, 512 * t:512 * (t + 1)], in_=zs[:]).then_inc(s_st, 16)
                sync.wait_ge(s_silu, t + 1)
                sync.dma_start(out=xcr[:, :, 512 * t:512 * (t + 1)], in_=xbcs[:, 0:16]).then_inc(s_st, 16)
                sync.dma_start(out=BT[:, 512 * t:512 * (t + 1)], in_=xbcs[:, 16]).then_inc(s_st, 16)
                sync.dma_start(out=CT[:, 512 * t:512 * (t + 1)], in_=xbcs[:, 17]).then_inc(s_st, 16)
                sync.wait_ge(s_dtd, t + 1)
                sync.dma_start(out=dtq[:, 512 * t:512 * (t + 1)], in_=dts[:, 5]).then_inc(s_st, 16)
                sync.dma_start(out=Acq[:, 512 * t:512 * (t + 1)], in_=acum[:, 1]).then_inc(s_st, 16)
            sync.wait_ge(s_st, 16 * 6 * NT)

        @block.tensor
        def _(tensor):
            if bar is not None:
                tensor.wait_ge(bar, ph)
            tensor.wait_ge(s_wg, 16)
            tensor.wait_ge(s_w, 80)
            for t in range(NT):
                b = t % 2
                tensor.wait_ge(s_ld, 16 * (t + 1))
                for m in range(MCH):
                    prior_wait(tensor, t, m)
                    if m == 34:
                        dst = psdt[:]
                        lo, hi = 4352, 4384
                    else:
                        dst = psum[:, m % NB]
                        lo, hi = 128 * m, 128 * (m + 1)
                    last = None
                    for k in range(KT):
                        last = tensor.matmul(dst, wsb[:, k, lo:hi], xn[:, b, k],
                                             start=(k == 0), stop=(k == KT - 1))
                    last.then_inc(s_mm, 1)

        @block.scalar
        def _(scalar):
            if bar is not None:
                scalar.wait_ge(bar, ph)
            scalar.wait_ge(s_w, 80)
            scalar.activation(acol[:, 0:1], alog[:], AF.Exp).then_inc(s_a, 1)
            for t in range(NT):
                if t >= 1:
                    scalar.wait_ge(s_st, 16 * 6 * (t - 1) + 16)
                for m in range(16):
                    scalar.wait_ge(s_mm, t * MCH + m + 1)
                    scalar.activation(zs[:, m], psum[:, m % NB], AF.Silu).then_inc(s_evz, 1)
                scalar.wait_ge(s_conv, t + 1)
                if t >= 1:
                    scalar.wait_ge(s_st, 16 * 6 * (t - 1) + 64)
                scalar.activation(xbcs[:], conv_y[:], AF.Silu).then_inc(s_silu, 1)
                scalar.wait_ge(s_dtv, 2 * t + 1)
                scalar.activation(dts[:, 3], dts[:, 2], AF.Exp, scale=-1.0).then_inc(s_dts, 1)
                scalar.wait_ge(s_dtv, 2 * t + 2)
                scalar.activation(dts[:, 4], dts[:, 3], AF.Ln).then_inc(s_dts, 1)

        @block.vector
        def _(vector):
            if bar is not None:
                vector.wait_ge(bar, ph)
            vector.memset(stage[:, :, 0:3], 0.0)
            vector.wait_ge(s_a, 1)
            vector.tensor_scalar_mul(acol[:, 1:2], acol[:, 0:1], -1.0)
            for t in range(NT):
                for m in range(16, 34):
                    vector.wait_ge(s_mm, t * MCH + m + 1)
                    vector.tensor_copy(stage[:, m - 16, 3:515], psum[:, m % NB]).then_inc(s_evx, 1)
                vector.wait_ge(s_silu, t)
                for j in range(18):
                    vector.tensor_scalar(conv_y[:, j], stage[:, j, 3:515], cwsb[:, j, 3:4],
                                         cbsb[:, j:j + 1], ALU.mult, ALU.add)
                    for i in (2, 1, 0):
                        r = vector.scalar_tensor_tensor(conv_y[:, j], stage[:, j, i:512 + i],
                                                        cwsb[:, j, i:i + 1], conv_y[:, j],
                                                        ALU.mult, ALU.add)
                    if j == 17:
                        r.then_inc(s_conv, 1)
                vector.tensor_copy(stage[:, :, 0:3], stage[:, :, 512:515])
                vector.wait_ge(s_mm, t * MCH + 35)
                if t >= 1:
                    vector.wait_ge(s_st, 16 * 6 * (t - 1) + 96)
                vector.tensor_scalar_add(dts[:, 0], psdt[:], dtb[:]).then_inc(s_evx, 1)
                vector.tensor_scalar_max(dts[:, 1], dts[:, 0], 0.0)
                vector.tensor_scalar_mul(dts[:, 2], dts[:, 0], -1.0)
                vector.tensor_tensor(dts[:, 2], dts[:, 2], dts[:, 0], ALU.max).then_inc(s_dtv, 1)
                vector.wait_ge(s_dts, 2 * t + 1)
                vector.tensor_scalar_add(dts[:, 3], dts[:, 3], 1.0).then_inc(s_dtv, 1)
                vector.wait_ge(s_dts, 2 * t + 2)
                vector.tensor_tensor(dts[:, 5], dts[:, 1], dts[:, 4], ALU.add)
                vector.tensor_scalar_mul(acum[:, 0], dts[:, 5], acol[:, 1:2])
                vector.tensor_tensor_scan(acum[:, 1], cmsk[:], acum[:, 0], 0.0,
                                          ALU.mult, ALU.add).then_inc(s_dtd, 1)

    return nc



SSD_SEMS = ("s_w", "s_wg", "s_wf", "s_pv", "s_pvv", "s_pa", "s_z", "s_dp1", "s_dp2", "s_dp3", "s_ld", "s_ldz", "s_tb", "s_tbe", "s_dtq", "s_dece1", "s_dmm", "s_dece3", "s_sc", "s_scm", "s_d0", "s_dcl", "s_de", "s_m", "s_xt", "s_xe", "s_xtl", "s_yd", "s_yo", "s_yoe", "s_ct", "s_up", "s_rbf", "s_xd", "s_ya", "s_yt", "s_ge", "s_gq", "s_s2", "s_r1", "s_r2", "s_rb", "s_rq", "s_opm", "s_ope", "s_st")


def build_ssd(nc=None, block=None, g=None, io=None, bar=None, ph=0):
    standalone = nc is None
    if standalone:
        nc = bass.Bass("TRN2", target_bir_lowering=False)
    def dram(name, shape, dt, kind):
        if io is not None:
            return io[name]
        return nc.dram_tensor(name, shape, dt, kind=kind).ap()
    xcT = dram("xcT", [2048, SEQ], BF16, "ExternalInput")
    szT = dram("szT", [2048, SEQ], BF16, "ExternalInput")
    BTd = dram("BT", [128, SEQ], BF16, "ExternalInput")
    CTd = dram("CT", [128, SEQ], BF16, "ExternalInput")
    dtq = dram("dtq", [32, SEQ], F32, "ExternalInput")
    Acq = dram("Acq", [32, SEQ], F32, "ExternalInput")
    opw = dram("out_proj_w", [2048, D], F32, "ExternalInput")
    nwr = dram("norm_wr", [128, 16], F32, "ExternalInput")
    Dpr = dram("D_paramr", [32, 1], F32, "ExternalInput")
    sel32 = dram("sel32", [32, 32 * 128], F32, "ExternalInput")
    ident = dram("ident128", [128, 128], F32, "ExternalInput")
    identb = dram("ident128b", [128, 128], BF16, "ExternalInput")
    ident32 = dram("ident32f", [32, 32], F32, "ExternalInput")
    maskTd = dram("maskT", [128, 128], BF16, "ExternalInput")
    onesd = dram("ones_col_bf", [128, 1], BF16, "ExternalInput")

    s0T = dram("s0T", [D, SEQ], BF16, "ExternalOutput")

    NC_ = SEQ // 128
    NG = SEQ // 512
    xcr = xcT.rearrange("(k p) n -> p k n", p=128)
    szr = szT.rearrange("(k p) n -> p k n", p=128)
    opr = opw.rearrange("(k p) m -> p k m", p=128)
    s0r = s0T.rearrange("(k p) n -> p k n", p=128)

    def batch_end(bb):
        c2, k2 = divmod(bb, 5)
        if k2 <= 2:
            return ("s_xe", 17 * c2 + (8, 16, 17)[k2])
        return ("s_ge", 16 * c2 + (8, 16)[k2 - 3])

    def xt_slot(c2, kind, i):
        if kind == "x":
            bb = 5 * c2 + i // 8
            pos = i % 8
        elif kind == "b":
            bb = 5 * c2 + 2
            pos = 0
        else:
            bb = 5 * c2 + 3 + i // 8
            pos = i % 8
        return bb, bb % 2, pos

    with contextlib.ExitStack() as ctx:
        ent = ctx.enter_context
        opwsb = ent(nc.sbuf_tensor("opwsb", [128, 16, D], BF16))
        nwsb = ent(nc.sbuf_tensor("nwsb", [128, 16], F32))
        dpsb = ent(nc.sbuf_tensor("dpsb", [32, 1], F32))
        dbc = ent(nc.sbuf_tensor("dbc", [128, 32], F32))
        selc = ent(nc.sbuf_tensor("selc", [32, 32 * 128], F32))
        idc = ent(nc.sbuf_tensor("idc", [128, 128], F32))
        idcb = ent(nc.sbuf_tensor("idcb", [128, 128], BF16))
        id32 = ent(nc.sbuf_tensor("id32", [32, 32], F32))
        mskc = ent(nc.sbuf_tensor("mskc", [128, 128], BF16))
        onesc = ent(nc.sbuf_tensor("onesc", [128, 1], BF16))
        acg = ent(nc.sbuf_tensor("acg", [32, 512], F32))
        dtg = ent(nc.sbuf_tensor("dtg", [32, 512], F32))
        lndtg = ent(nc.sbuf_tensor("lndtg", [32, 512], F32))
        a2g = ent(nc.sbuf_tensor("a2g", [32, 512], F32))
        e2g = ent(nc.sbuf_tensor("e2g", [32, 512], F32))
        nalg = ent(nc.sbuf_tensor("nalg", [32, 512], F32))
        sc1g = ent(nc.sbuf_tensor("sc1g", [32, 512], F32))
        eacg = ent(nc.sbuf_tensor("eacg", [32, 512], F32))
        z32 = ent(nc.sbuf_tensor("z32", [32, 32], F32))
        z32b = ent(nc.sbuf_tensor("z32b", [32, 128], F32))
        run = ent(nc.sbuf_tensor("run", [128, 2048], F32))
        run_bf = ent(nc.sbuf_tensor("run_bf", [128, 2048], BF16))
        xc_sb = ent(nc.sbuf_tensor("xc_sb", [128, 16, 512], BF16))
        sz_sb = ent(nc.sbuf_tensor("sz_sb", [128, 16, 128], BF16))
        bt_sb = ent(nc.sbuf_tensor("bt_sb", [128, 512], BF16))
        ct_sb = ent(nc.sbuf_tensor("ct_sb", [128, 512], BF16))
        tbs = ent(nc.sbuf_tensor("tbs", [128, 4, 128], F32))
        decq = ent(nc.sbuf_tensor("decq", [128, 128], F32))
        scMq = ent(nc.sbuf_tensor("scMq", [128, 512], BF16))
        bsmq = ent(nc.sbuf_tensor("bsmq", [128, 4, 128], BF16))
        msb = ent(nc.sbuf_tensor("msb", [128, 32, 512], BF16))
        expd = ent(nc.sbuf_tensor("expd", [128, 2, 512], BF16))
        xsm = ent(nc.sbuf_tensor("xsm", [128, 2048], BF16))
        xtl = ent(nc.sbuf_tensor("xtl", [128, 2048], BF16))
        yo_sb = ent(nc.sbuf_tensor("yo_sb", [128, 512], F32))
        xd_sb = ent(nc.sbuf_tensor("xd_sb", [128, 4, 512], BF16))
        y_sb = ent(nc.sbuf_tensor("y_sb", [128, 2048], BF16))
        g_cm = ent(nc.sbuf_tensor("g_cm", [128, 16, 512], BF16))
        gsq = ent(nc.sbuf_tensor("gsq", [128, 16, 128], BF16))
        rows = ent(nc.sbuf_tensor("rows", [32, 2 * 128], F32))
        rstdq = ent(nc.sbuf_tensor("rstdq", [128, 512], F32))
        s0_sb = ent(nc.sbuf_tensor("s0_sb", [128, 8, 512], BF16))
        ps_m = ent(nc.psum_tensor("ps_m", [128, 512], F32))
        ps_s = ent(nc.psum_tensor("ps_s", [128, 512], F32))
        ps_xt = ent(nc.psum_tensor("ps_xt", [128, 2, 8, 128], BF16))
        ps_d0 = ent(nc.psum_tensor("ps_d0", [128, 512], F32))
        ps_yd = ent(nc.psum_tensor("ps_yd", [128, 512], F32))
        ps_yo = ent(nc.psum_tensor("ps_yo", [128, 512], F32))
        ps_op = ent(nc.psum_tensor("ps_op", [128, 512], F32))

        if standalone:
            g = {}
            for name in SSD_SEMS:
                g[name] = ent(nc.semaphore(name))
            block = ent(nc.Block())

        @block.sync
        def _(sync):
            if bar is not None:
                sync.wait_ge(bar, ph)
            for ap_out, ap_in in ((selc[:], sel32), (idc[:], ident), (idcb[:], identb),
                                  (id32[:], ident32), (mskc[:], maskTd), (onesc[:], onesd),
                                  (nwsb[:], nwr), (dpsb[:], Dpr)):
                sync.dma_start(out=ap_out, in_=ap_in).then_inc(g["s_w"], 16)
            for gi in range(NG):
                if gi >= 1:
                    if DBG_SSD >= 8:
                        sync.wait_ge(g["s_opm"], 8 * gi)
                    elif DBG_SSD >= 5:
                        sync.wait_ge(g["s_xt"], 68 * gi)
                    if DBG_SSD >= 7:
                        sync.wait_ge(g["s_ge"], 64 * gi)
                sync.dma_start(out=xc_sb[:], in_=xcr[:, :, 512 * gi:512 * (gi + 1)]).then_inc(g["s_ld"], 16)
                sync.dma_start(out=bt_sb[:], in_=BTd[:, 512 * gi:512 * (gi + 1)]).then_inc(g["s_ld"], 16)
                sync.dma_start(out=ct_sb[:], in_=CTd[:, 512 * gi:512 * (gi + 1)]).then_inc(g["s_ld"], 16)
                if gi >= 1:
                    sync.wait_ge(g["s_pa"], 3 * gi)
                    if DBG_SSD >= 4:
                        sync.wait_ge(g["s_d0"], 32 * gi)
                sync.dma_start(out=dtg[:], in_=dtq[:, 512 * gi:512 * (gi + 1)]).then_inc(g["s_ld"], 16)
                sync.dma_start(out=acg[:], in_=Acq[:, 512 * gi:512 * (gi + 1)]).then_inc(g["s_ld"], 16)
                if DBG_SSD >= 8 and gi >= 1:
                    sync.wait_ge(g["s_ope"], 8 * gi)
                    sync.dma_start(out=s0r[:, :, 512 * (gi - 1):512 * gi],
                                   in_=s0_sb[:]).then_inc(g["s_st"], 16)
            if DBG_SSD >= 8:
                sync.wait_ge(g["s_ope"], 8 * NG)
                sync.dma_start(out=s0r[:, :, 512 * (NG - 1):512 * NG], in_=s0_sb[:]).then_inc(g["s_st"], 16)
                sync.wait_ge(g["s_st"], 16 * NG)
            else:
                if DBG_SSD == 1:
                    sync.wait_ge(g["s_pa"], 3 * NG)
                elif DBG_SSD == 2:
                    sync.wait_ge(g["s_dece3"], NG)
                    sync.wait_ge(g["s_tbe"], NC_)
                elif DBG_SSD == 3:
                    sync.wait_ge(g["s_scm"], NC_)
                elif DBG_SSD == 4:
                    sync.wait_ge(g["s_m"], 32 * NG)
                elif DBG_SSD == 5:
                    sync.wait_ge(g["s_xtl"], 16 * NC_)
                    sync.wait_ge(g["s_xd"], 4 * NC_)
                elif DBG_SSD == 6:
                    sync.wait_ge(g["s_rbf"], NC_ + 1)
                elif DBG_SSD in (6.4, 6.45, 6.5, 7):
                    sync.wait_ge(g["s_ge"], 16 * NC_)
                elif DBG_SSD == 7.1:
                    sync.wait_ge(g["s_s2"], NC_)
                    sync.wait_ge(g["s_r1"], NC_)
                elif DBG_SSD == 7.2:
                    sync.wait_ge(g["s_rq"], NC_)

        @block.gpsimd
        def _(gpsimd):
            if bar is not None:
                gpsimd.wait_ge(bar, ph)
            gpsimd.dma_start(out=opwsb[:], in_=opr).then_inc(g["s_wg"], 16)
            gpsimd.wait_ge(g["s_w"], 129)
            for c in range(NC_):
                if DBG_SSD < 7:
                    break
                if c >= 1:
                    gpsimd.wait_ge(g["s_ge"], 16 * c)
                gpsimd.dma_start(out=sz_sb[:], in_=szr[:, :, 128 * c:128 * (c + 1)]).then_inc(g["s_ldz"], 16)
                if DBG_SSD < 7.05:
                    pass

        @block.tensor
        def _(tensor):
            if bar is not None:
                tensor.wait_ge(bar, ph)
            tensor.wait_ge(g["s_w"], 128)
            tensor.wait_ge(g["s_wf"], 1)
            tensor.wait_ge(g["s_z"], 1)
            tensor.transpose(ps_m[0:1, 384:416], dpsb[:], id32[:]).then_inc(g["s_dp1"], 1)
            tensor.wait_ge(g["s_dp2"], 1)
            tensor.matmul(ps_m[:, 416:448], selc[:, 0:128], z32[:, 0:32],
                          start=True, stop=True).then_inc(g["s_dp3"], 1)
            tensor.wait_ge(g["s_w"], 129)
            for gi in range(NG):
                tensor.wait_ge(g["s_ld"], 80 * (gi + 1))
                tensor.wait_ge(g["s_pa"], 3 * (gi + 1))
                for j in range(4):
                    if DBG_SSD < 2:
                        break
                    c = 4 * gi + j
                    if c >= 1:
                        tensor.wait_ge(g["s_tbe"], c)
                        tensor.wait_ge(g["s_scm"], c)
                    if j == 0 and gi >= 1:
                        tensor.wait_ge(g["s_dece1"], gi)
                    tensor.transpose(ps_m[:, 0:32], nalg[:, 128 * j:128 * (j + 1)], id32[:])
                    tensor.transpose(ps_m[:, 32:64], sc1g[:, 128 * j:128 * (j + 1)], id32[:])
                    tensor.transpose(ps_m[:, 64:96], eacg[:, 128 * j:128 * (j + 1)],
                                     id32[:]).then_inc(g["s_tb"], 1)
                    if DBG_SSD >= 3:
                        tensor.matmul(ps_s[:, 0:128], bt_sb[:, 128 * j:128 * (j + 1)],
                                      ct_sb[:, 128 * j:128 * (j + 1)], start=True,
                                      stop=True).then_inc(g["s_sc"], 1)
                if DBG_SSD < 2:
                    continue
                if gi >= 1:
                    tensor.wait_ge(g["s_dece3"], gi)
                    if DBG_SSD >= 6:
                        tensor.wait_ge(g["s_up"], 16 * gi)
                    if DBG_SSD >= 4:
                        tensor.wait_ge(g["s_de"], 32 * gi)
                    if DBG_SSD >= 7.1:
                        tensor.wait_ge(g["s_r1"], 4 * gi)
                    if DBG_SSD >= 7.2:
                        tensor.wait_ge(g["s_rq"], 4 * gi)
                tensor.wait_ge(g["s_tbe"], 4 * (gi + 1))
                tensor.transpose(ps_m[0:4, 96:128],
                                 eacg.rearrange("p (a b) -> p a b", b=128)[:, :, 127],
                                 id32[:]).then_inc(g["s_dtq"], 1)
                tensor.wait_ge(g["s_dece1"], gi + 1)
                last = None
                for r in range(4):
                    last = tensor.matmul(ps_d0[:, 32 * r:32 * (r + 1)], selc[:, 128 * r:128 * (r + 1)],
                                         z32[:, 0:32], start=True, stop=True)
                last.then_inc(g["s_dmm"], 1)
                tensor.wait_ge(g["s_tbe"], 4 * (gi + 1))
                tensor.wait_ge(g["s_dece3"], gi + 1)
                for h in range(32):
                    if DBG_SSD < 4:
                        break
                    if 32 * gi + h >= 1:
                        tensor.wait_ge(g["s_de"], 32 * gi + h)
                    tensor.matmul(ps_d0[:], selc[:, 128 * h:128 * (h + 1)],
                                  acg[:], start=True, stop=False)
                    tensor.matmul(ps_d0[:], idc[:],
                                  tbs[:, :, h].unsqueeze(2).broadcast_to([128, 4, 128]),
                                  start=False, stop=True).then_inc(g["s_d0"], 1)
                for j in range(4):
                    if DBG_SSD < 5:
                        break
                    c = 4 * gi + j
                    for i in range(16):
                        bb, bank, pos = xt_slot(c, "x", i)
                        if pos == 0 and bb >= 2:
                            if DBG_SSD >= 6.4:
                                if c >= 1 and i == 0:
                                    tensor.wait_ge(g["s_ge"], 16 * c)
                                w = batch_end(bb - 2)
                                tensor.wait_ge(g[w[0]], w[1])
                            elif c >= 1:
                                tensor.wait_ge(g["s_xe"], 17 * (c - 1) + (16 if i < 8 else 17))
                        tensor.transpose(ps_xt[:, bank, pos], xc_sb[:, i, 128 * j:128 * (j + 1)],
                                         idcb[:]).then_inc(g["s_xt"], 1)
                    bb, bank, pos = xt_slot(c, "b", 0)
                    if bb >= 2:
                        if DBG_SSD >= 7:
                            w = batch_end(bb - 2)
                            tensor.wait_ge(g[w[0]], w[1])
                        else:
                            tensor.wait_ge(g["s_xe"], 17 * c + 8)
                    tensor.transpose(ps_xt[:, bank, pos], bt_sb[:, 128 * j:128 * (j + 1)],
                                     idcb[:]).then_inc(g["s_xt"], 1)
                    if DBG_SSD < 6:
                        continue
                    tensor.wait_ge(g["s_m"], 32 * (gi + 1))
                    for q in range(4):
                        tensor.wait_ge(g["s_ya"], 4 * c + q)
                        tensor.wait_ge(g["s_xe"], 17 * c + 4 * (q + 1))
                        last = None
                        for hh in range(8 * q, 8 * q + 8):
                            last = tensor.matmul(ps_yd[:, 64 * (hh - 8 * q):64 * (hh - 8 * q + 1)],
                                                 msb[:, hh, 128 * j:128 * (j + 1)],
                                                 xsm[:, 64 * hh:64 * (hh + 1)],
                                                 start=True, stop=True)
                        last.then_inc(g["s_yd"], 1)
                        tensor.wait_ge(g["s_rbf"], c + 1)
                        tensor.wait_ge(g["s_up"], 4 * c + q)
                        tensor.matmul(ps_yo[:], ct_sb[:, 128 * j:128 * (j + 1)],
                                      run_bf[:, 512 * q:512 * (q + 1)],
                                      start=True, stop=True).then_inc(g["s_yo"], 1)
                        tensor.wait_ge(g["s_yoe"], 4 * c + q + 1)
                        tensor.wait_ge(g["s_xtl"], 16 * c + 4 * (q + 1))
                        tensor.wait_ge(g["s_xe"], 17 * (c + 1))
                        tensor.matmul(ps_yo[:], bsmq[:, j], xtl[:, 512 * q:512 * (q + 1)],
                                      start=True, stop=True).then_inc(g["s_ct"], 1)
                    if DBG_SSD < 6.4:
                        continue
                    tensor.wait_ge(g["s_ya"], 4 * (c + 1))
                    for i in range(16):
                        bb, bank, pos = xt_slot(c, "y", i)
                        if pos == 0:
                            w = batch_end(bb - 2)
                            tensor.wait_ge(g[w[0]], w[1])
                        tensor.transpose(ps_xt[:, bank, pos], y_sb[:, 128 * i:128 * (i + 1)],
                                         idcb[:]).then_inc(g["s_yt"], 1)
                    if DBG_SSD < 7.1:
                        continue
                    tensor.wait_ge(g["s_gq"], 16 * (c + 1))
                    if c >= 1:
                        tensor.wait_ge(g["s_r1"], c)
                        if DBG_SSD >= 7.2:
                            tensor.wait_ge(g["s_rq"], c)
                    last = None
                    for i in range(16):
                        last = tensor.matmul(ps_d0[0:1, 0:128], onesc[:], gsq[:, i],
                                             start=(i == 0), stop=(i == 15))
                    last.then_inc(g["s_s2"], 1)
                    if DBG_SSD >= 7.2:
                        tensor.wait_ge(g["s_r2"], c + 1)
                        if c >= 1:
                            tensor.wait_ge(g["s_rq"], c)
                        tensor.matmul(ps_d0[:, 128:256], selc[:, 0:128], z32b[:, 0:128],
                                      start=True, stop=True).then_inc(g["s_rb"], 1)
                if DBG_SSD < 8:
                    continue
                tensor.wait_ge(g["s_ge"], 64 * (gi + 1))
                tensor.wait_ge(g["s_rq"], 4 * (gi + 1))
                for m in range(8):
                    if 8 * gi + m >= 1:
                        tensor.wait_ge(g["s_ope"], 8 * gi + m)
                    last = None
                    for kt in range(16):
                        last = tensor.matmul(ps_op[:], opwsb[:, kt, 128 * m:128 * (m + 1)],
                                             g_cm[:, kt, :], start=(kt == 0), stop=(kt == 15))
                    last.then_inc(g["s_opm"], 1)

        @block.scalar
        def _(scalar):
            if bar is not None:
                scalar.wait_ge(bar, ph)
            scalar.wait_ge(g["s_w"], 128)
            scalar.wait_ge(g["s_dp1"], 1)
            scalar.copy(z32[0:1, 0:32], ps_m[0:1, 384:416]).then_inc(g["s_dp2"], 1)
            scalar.wait_ge(g["s_dp3"], 1)
            scalar.copy(dbc[:], ps_m[:, 416:448]).then_inc(g["s_w"], 1)
            for gi in range(NG):
                scalar.wait_ge(g["s_ld"], 80 * (gi + 1))
                if gi >= 1:
                    scalar.wait_ge(g["s_pa"], 3 * gi)
                scalar.activation(lndtg[:], dtg[:], AF.Ln).then_inc(g["s_pv"], 1)
                scalar.wait_ge(g["s_pvv"], gi + 1)
                scalar.activation(e2g[:], a2g[:], AF.Exp).then_inc(g["s_pv"], 1)
                scalar.activation(eacg[:], acg[:], AF.Exp).then_inc(g["s_pa"], 1)
                for j in range(4):
                    if DBG_SSD < 2:
                        break
                    c = 4 * gi + j
                    scalar.wait_ge(g["s_tb"], c + 1)
                    if gi >= 1 and j == 0:
                        if DBG_SSD >= 6:
                            scalar.wait_ge(g["s_up"], 16 * gi)
                        if DBG_SSD >= 4:
                            scalar.wait_ge(g["s_d0"], 32 * gi)
                    scalar.copy(tbs[:, j, 0:96], ps_m[:, 0:96]).then_inc(g["s_tbe"], 1)
                if DBG_SSD < 2:
                    continue
                scalar.wait_ge(g["s_dtq"], gi + 1)
                if gi >= 1:
                    scalar.wait_ge(g["s_dmm"], gi)
                scalar.copy(z32[0:4, 0:32], ps_m[0:4, 96:128]).then_inc(g["s_dece1"], 1)
                scalar.wait_ge(g["s_dmm"], gi + 1)
                scalar.copy(decq[:], ps_d0[:, 0:128]).then_inc(g["s_dece3"], 1)
                for h in range(32):
                    if DBG_SSD < 4:
                        break
                    scalar.wait_ge(g["s_dcl"], 32 * gi + h + 1)
                    if 32 * gi + h >= 2:
                        scalar.wait_ge(g["s_m"], 32 * gi + h - 1)
                    scalar.activation(expd[:, h % 2], ps_d0[:], AF.Exp).then_inc(g["s_de"], 1)
                for j in range(4):
                    if DBG_SSD < 7:
                        break
                    c = 4 * gi + j
                    if DBG_SSD < 7.2:
                        continue
                    scalar.wait_ge(g["s_r1"], c + 1)
                    scalar.wait_ge(g["s_z"], 2)
                    if c >= 1:
                        scalar.wait_ge(g["s_rb"], c)
                    scalar.activation(rows[0:1, 128:256], rows[0:1, 0:128], AF.Ln)
                    scalar.activation(z32b[0:1, 0:128], rows[0:1, 128:256], AF.Exp,
                                      scale=-0.5).then_inc(g["s_r2"], 1)
                    scalar.wait_ge(g["s_rb"], c + 1)
                    if gi >= 1 and j == 0:
                        scalar.wait_ge(g["s_ope"], 8 * gi)
                    scalar.copy(rstdq[:, 128 * j:128 * (j + 1)],
                                ps_d0[:, 128:256]).then_inc(g["s_rq"], 1)

        @block.vector
        def _(vector):
            if bar is not None:
                vector.wait_ge(bar, ph)
            vector.wait_ge(g["s_w"], 128)
            vector.wait_ge(g["s_wg"], 16)
            for kt in range(16):
                r = vector.tensor_scalar_mul(opwsb[:, kt, :], opwsb[:, kt, :], nwsb[:, kt:kt + 1])
            r.then_inc(g["s_wf"], 1)
            vector.memset(z32[:], 0.0).then_inc(g["s_z"], 1)
            vector.memset(z32b[:], 0.0).then_inc(g["s_z"], 1)
            vector.memset(run[:], 0.0)
            vector.memset(run_bf[:], 0.0).then_inc(g["s_rbf"], 1)
            for gi in range(NG):
                vector.wait_ge(g["s_ld"], 80 * (gi + 1))
                if gi >= 1:
                    vector.wait_ge(g["s_pa"], 3 * gi)
                sl_ac = acg[:]
                ce = acg[:].rearrange("p (a b) -> p a b", b=128)[:, :, 127]
                vector.tensor_tensor(a2g[:].rearrange("p (a b) -> p a b", a=4),
                                     ce.unsqueeze(2).broadcast_to([32, 4, 128]),
                                     sl_ac.rearrange("p (a b) -> p a b", a=4),
                                     ALU.subtract).then_inc(g["s_pvv"], 1)
                vector.wait_ge(g["s_pv"], 2 * gi + 1)
                vector.tensor_tensor(nalg[:], lndtg[:], sl_ac, ALU.subtract).then_inc(g["s_pa"], 1)
                vector.wait_ge(g["s_pv"], 2 * gi + 2)
                vector.tensor_tensor(sc1g[:], dtg[:], e2g[:], ALU.mult).then_inc(g["s_pa"], 1)
                for j in range(4):
                    if DBG_SSD < 3:
                        break
                    c = 4 * gi + j
                    vector.wait_ge(g["s_sc"], c + 1)
                    if gi >= 1 and j == 0 and DBG_SSD >= 4:
                        vector.wait_ge(g["s_m"], 32 * gi)
                    vector.tensor_tensor(scMq[:, 128 * j:128 * (j + 1)], ps_s[:, 0:128],
                                         mskc[:], ALU.mult).then_inc(g["s_scm"], 1)
                for h in range(32):
                    if DBG_SSD < 4:
                        break
                    vector.wait_ge(g["s_d0"], 32 * gi + h + 1)
                    vector.tensor_scalar_min(ps_d0[:], ps_d0[:],
                                             20.0).then_inc(g["s_dcl"], 1)
                    vector.wait_ge(g["s_de"], 32 * gi + h + 1)
                    vector.wait_ge(g["s_scm"], 4 * (gi + 1))
                    vector.tensor_tensor(msb[:, h], expd[:, h % 2], scMq[:],
                                         ALU.mult).then_inc(g["s_m"], 1)
                for j in range(4):
                    if DBG_SSD < 5:
                        break
                    c = 4 * gi + j
                    for i in range(16):
                        vector.wait_ge(g["s_xt"], 17 * c + 8 * (i // 8 + 1))
                        if c >= 1 and DBG_SSD >= 6:
                            vector.wait_ge(g["s_yd"], 4 * (c - 1) + i // 4 + 1)
                        bb, bank, pos = xt_slot(c, "x", i)
                        vector.tensor_copy(xsm[:, 128 * i:128 * (i + 1)],
                                           ps_xt[:, bank, pos]).then_inc(g["s_xe"], 1)
                        if c >= 1 and DBG_SSD >= 6:
                            vector.wait_ge(g["s_ct"], 4 * (c - 1) + i // 4 + 1)
                        vector.tensor_tensor(
                            xtl[:, 128 * i:128 * (i + 1)].rearrange("p (a b) -> p a b", a=2),
                            xsm[:, 128 * i:128 * (i + 1)].rearrange("p (a b) -> p a b", a=2),
                            tbs[:, j, 32 + 2 * i:32 + 2 * i + 2].unsqueeze(2).broadcast_to([128, 2, 64]),
                            ALU.mult).then_inc(g["s_xtl"], 1)
                    vector.wait_ge(g["s_xt"], 17 * c + 17)
                    if c >= 4 and DBG_SSD >= 6:
                        vector.wait_ge(g["s_ct"], 4 * (c - 4) + 4)
                    bb, bank, pos = xt_slot(c, "b", 0)
                    vector.tensor_copy(bsmq[:, j], ps_xt[:, bank, pos]).then_inc(g["s_xe"], 1)
                    for q in range(4):
                        vector.tensor_tensor(
                            xd_sb[:, q].rearrange("p (a b) -> p a b", a=8),
                            xsm[:, 512 * q:512 * (q + 1)].rearrange("p (a b) -> p a b", a=8),
                            dbc[:, 8 * q:8 * (q + 1)].unsqueeze(2).broadcast_to([128, 8, 64]),
                            ALU.mult).then_inc(g["s_xd"], 1)
                    for q in range(4):
                        if DBG_SSD < 6:
                            break
                        vector.wait_ge(g["s_yo"], 4 * c + q + 1)
                        vector.tensor_tensor(
                            yo_sb[:].rearrange("p (a b) -> p a b", a=8),
                            ps_yo[:].rearrange("p (a b) -> p a b", a=8),
                            tbs[:, j, 64 + 8 * q:64 + 8 * q + 8].unsqueeze(2).broadcast_to([128, 8, 64]),
                            ALU.mult).then_inc(g["s_yoe"], 1)
                        vector.wait_ge(g["s_yd"], 4 * c + q + 1)
                        vector.wait_ge(g["s_xd"], 4 * c + q + 1)
                        vector.tensor_tensor(y_sb[:, 512 * q:512 * (q + 1)], ps_yd[:], yo_sb[:], ALU.add)
                        vector.tensor_tensor(y_sb[:, 512 * q:512 * (q + 1)], y_sb[:, 512 * q:512 * (q + 1)],
                                             xd_sb[:, q], ALU.add).then_inc(g["s_ya"], 1)
                        vector.wait_ge(g["s_ct"], 4 * c + q + 1)
                        vector.tensor_tensor(
                            run[:, 512 * q:512 * (q + 1)].rearrange("p (a b) -> p a b", a=8),
                            run[:, 512 * q:512 * (q + 1)].rearrange("p (a b) -> p a b", a=8),
                            decq[:, 32 * j + 8 * q:32 * j + 8 * (q + 1)].unsqueeze(2).broadcast_to([128, 8, 64]),
                            ALU.mult)
                        vector.tensor_tensor(run[:, 512 * q:512 * (q + 1)], run[:, 512 * q:512 * (q + 1)],
                                             ps_yo[:], ALU.add).then_inc(g["s_up"], 1)
                    if DBG_SSD >= 6:
                        vector.tensor_copy(run_bf[:], run[:]).then_inc(g["s_rbf"], 1)
                    if DBG_SSD < 6.4:
                        continue
                    if DBG_SSD >= 7:
                        vector.wait_ge(g["s_ldz"], 16 * (c + 1))
                    for i in range(16):
                        vector.wait_ge(g["s_yt"], 16 * c + 8 * (i // 8 + 1))
                        if gi >= 1 and j == 0 and i == 0 and DBG_SSD >= 8:
                            vector.wait_ge(g["s_opm"], 8 * gi)
                        bb, bank, pos = xt_slot(c, "y", i)
                        if DBG_SSD >= 7:
                            vector.tensor_tensor(g_cm[:, i, 128 * j:128 * (j + 1)],
                                                 ps_xt[:, bank, pos],
                                                 sz_sb[:, i, :],
                                                 ALU.mult).then_inc(g["s_ge"], 1)
                        elif DBG_SSD >= 6.5:
                            vector.tensor_copy(g_cm[:, i, 128 * j:128 * (j + 1)],
                                               ps_xt[:, bank, pos]).then_inc(g["s_ge"], 1)
                        elif DBG_SSD >= 6.45:
                            bbx, bankx, posx = xt_slot(c, "x", i)
                            vector.tensor_copy(g_cm[:, i, 128 * j:128 * (j + 1)],
                                               ps_xt[:, bankx, posx]).then_inc(g["s_ge"], 1)
                        else:
                            vector.tensor_copy(g_cm[:, i, 128 * j:128 * (j + 1)],
                                               mskc[:]).then_inc(g["s_ge"], 1)
                    if DBG_SSD < 7.1:
                        continue
                    for i in range(16):
                        vector.wait_ge(g["s_yt"], 16 * c + 8 * (i // 8 + 1))
                        if c >= 1 and i == 0:
                            vector.wait_ge(g["s_s2"], c)
                        vector.tensor_tensor(gsq[:, i], g_cm[:, i, 128 * j:128 * (j + 1)],
                                             g_cm[:, i, 128 * j:128 * (j + 1)],
                                             ALU.mult).then_inc(g["s_gq"], 1)
                    vector.wait_ge(g["s_s2"], c + 1)
                    if c >= 1 and DBG_SSD >= 7.2:
                        vector.wait_ge(g["s_r2"], c)
                    vector.tensor_scalar(rows[0:1, 0:128], ps_d0[0:1, 0:128],
                                         1.0 / 2048.0, 1e-5, ALU.mult, ALU.add).then_inc(g["s_r1"], 1)
                for m in range(8):
                    if DBG_SSD < 8:
                        break
                    vector.wait_ge(g["s_opm"], 8 * gi + m + 1)
                    vector.wait_ge(g["s_rq"], 4 * (gi + 1))
                    if m == 0:
                        vector.wait_ge(g["s_st"], 16 * gi)
                    vector.tensor_tensor(s0_sb[:, m], ps_op[:], rstdq[:],
                                         ALU.mult).then_inc(g["s_ope"], 1)

    return nc



L0F_SEMS = ("s_w", "s_wg", "s_ld", "s_ldg", "s_mix", "s_mm", "s_ev", "s_vmm", "s_f0", "s_g0", "s_cmb", "s_sum", "s_v1", "s_sc", "s_row", "s_bc", "s_ap", "s_st")


def build_l0ffn(nc=None, block=None, g=None, io=None, bar=None, ph=0):
    standalone = nc is None
    if standalone:
        nc = bass.Bass("TRN2", target_bir_lowering=False)
    def dram(name, shape, dt, kind):
        if io is not None:
            return io[name]
        return nc.dram_tensor(name, shape, dt, kind=kind).ap()
    xT = dram("xT", [D, SEQ], F32, "ExternalInput")
    xn0T = dram("xn0T", [D, SEQ], BF16, "ExternalInput")
    s0T = dram("s0T", [D, SEQ], BF16, "ExternalInput")
    WrT = dram("WrT", [D, D], F32, "ExternalInput")
    WkT = dram("WkT", [D, D], F32, "ExternalInput")
    WvT = dram("WvT", [D, D], F32, "ExternalInput")
    mur = dram("murr", [128, 8], F32, "ExternalInput")
    muk = dram("mukr", [128, 8], F32, "ExternalInput")
    ones_col = dram("ones_col_bf", [128, 1], BF16, "ExternalInput")
    sel32 = dram("sel32", [32, 32 * 128], F32, "ExternalInput")

    x1T = dram("x1T", [D, SEQ], F32, "ExternalOutput")
    xn1T = dram("xn1T", [D, SEQ], BF16, "ExternalOutput")

    NT = SEQ // 512
    KT = 8
    xr_ = xT.rearrange("(k p) n -> p k n", p=128)
    xnr = xn0T.rearrange("(k p) n -> p k n", p=128)
    s0r = s0T.rearrange("(k p) n -> p k n", p=128)
    x1r = x1T.rearrange("(k p) n -> p k n", p=128)
    xn1r = xn1T.rearrange("(k p) n -> p k n", p=128)
    wrr = WrT.rearrange("(k p) m -> p k m", p=128)
    wkr = WkT.rearrange("(k p) m -> p k m", p=128)
    wvr = WvT.rearrange("(k p) m -> p k m", p=128)

    with contextlib.ExitStack() as ctx:
        ent = ctx.enter_context
        wr_sb = ent(nc.sbuf_tensor("wr_sb", [128, KT, D], BF16))
        wk_sb = ent(nc.sbuf_tensor("wk_sb", [128, KT, D], BF16))
        wv_sb = ent(nc.sbuf_tensor("wv_sb", [128, KT, D], BF16))
        mur_sb = ent(nc.sbuf_tensor("mur_sb", [128, 8], F32))
        muk_sb = ent(nc.sbuf_tensor("muk_sb", [128, 8], F32))
        onesc = ent(nc.sbuf_tensor("onesc", [128, 1], BF16))
        selc = ent(nc.sbuf_tensor("selc", [32, 256], F32))
        xs = ent(nc.sbuf_tensor("xs", [128, 2, KT, 513], BF16))
        xf = ent(nc.sbuf_tensor("xf", [128, 1, KT, 512], F32))
        s0_sb = ent(nc.sbuf_tensor("s0_sb", [128, 1, KT, 512], BF16))
        dd = ent(nc.sbuf_tensor("dd", [128, KT, 512], BF16))
        xr = ent(nc.sbuf_tensor("xrb", [128, KT, 512], BF16))
        xk = ent(nc.sbuf_tensor("xkb", [128, KT, 512], BF16))
        r_sb = ent(nc.sbuf_tensor("r_sb", [128, KT, 512], BF16))
        ka_sb = ent(nc.sbuf_tensor("ka_sb", [128, KT, 512], BF16))
        krl = ent(nc.sbuf_tensor("krl", [128, 512], F32))
        f0 = ent(nc.sbuf_tensor("f0", [128, KT, 512], BF16))
        g0 = ent(nc.sbuf_tensor("g0", [128, KT, 512], BF16))
        x1 = ent(nc.sbuf_tensor("x1", [128, KT, 512], F32))
        x1b = ent(nc.sbuf_tensor("x1b", [128, KT, 512], BF16))
        xn1 = ent(nc.sbuf_tensor("xn1", [128, KT, 512], BF16))
        rows = ent(nc.sbuf_tensor("rows", [32, 5 * 512], F32))
        ps = ent(nc.psum_tensor("ps", [128, 4, 512], F32))
        sum_a = ent(nc.psum_tensor("sum_a", [1, 512], F32))
        sum_b = ent(nc.psum_tensor("sum_b", [1, 512], F32))
        bca = ent(nc.psum_tensor("bca", [128, 512], F32))
        bcb = ent(nc.psum_tensor("bcb", [128, 512], F32))
        if standalone:
            g = {}
            for name in L0F_SEMS:
                g[name] = ent(nc.semaphore(name))
            block = ent(nc.Block())


        @block.gpsimd
        def _(gpsimd):
            if bar is not None:
                gpsimd.wait_ge(bar, ph)
            gpsimd.dma_start(out=wr_sb[:], in_=wrr).then_inc(g["s_wg"], 16)
            gpsimd.dma_start(out=wk_sb[:], in_=wkr).then_inc(g["s_wg"], 16)
            gpsimd.dma_start(out=wv_sb[:], in_=wvr).then_inc(g["s_wg"], 16)
            for t in range(NT):
                if t >= 1:
                    gpsimd.wait_ge(g["s_cmb"], t)
                gpsimd.dma_start(out=xf[:, 0], in_=xr_[:, :, 512 * t:512 * (t + 1)]).then_inc(g["s_ldg"], 16)
                gpsimd.dma_start(out=s0_sb[:, 0], in_=s0r[:, :, 512 * t:512 * (t + 1)]).then_inc(g["s_ldg"], 16)

        @block.sync
        def _(sync):
            if bar is not None:
                sync.wait_ge(bar, ph)
            sync.dma_start(out=mur_sb[:], in_=mur).then_inc(g["s_w"], 16)
            sync.dma_start(out=muk_sb[:], in_=muk).then_inc(g["s_w"], 16)
            sync.dma_start(out=onesc[:], in_=ones_col).then_inc(g["s_w"], 16)
            sync.dma_start(out=selc[:], in_=sel32[:, 0:256]).then_inc(g["s_w"], 16)
            for t in range(NT):
                b = t % 2
                if t >= 2:
                    sync.wait_ge(g["s_mix"], 3 * (t - 1))
                sync.dma_start(out=xs[:, b, :, 1:513],
                               in_=xnr[:, :, 512 * t:512 * (t + 1)]).then_inc(g["s_ld"], 16)
                if t >= 1:
                    sync.wait_ge(g["s_cmb"], t)
                    sync.dma_start(out=x1r[:, :, 512 * (t - 1):512 * t], in_=x1[:]).then_inc(g["s_st"], 16)
                    sync.wait_ge(g["s_ap"], t)
                    sync.dma_start(out=xn1r[:, :, 512 * (t - 1):512 * t], in_=xn1[:]).then_inc(g["s_st"], 16)
            sync.wait_ge(g["s_cmb"], NT)
            sync.dma_start(out=x1r[:, :, 512 * (NT - 1):512 * NT], in_=x1[:]).then_inc(g["s_st"], 16)
            sync.wait_ge(g["s_ap"], NT)
            sync.dma_start(out=xn1r[:, :, 512 * (NT - 1):512 * NT], in_=xn1[:]).then_inc(g["s_st"], 16)
            sync.wait_ge(g["s_st"], 16 * 2 * NT)

        @block.tensor
        def _(tensor):
            if bar is not None:
                tensor.wait_ge(bar, ph)
            tensor.wait_ge(g["s_wg"], 48)
            tensor.wait_ge(g["s_w"], 64)
            for t in range(NT):
                b = t % 2
                tensor.wait_ge(g["s_mix"], 3 * t + 3)
                for br, (wsb, rhs) in enumerate(((wr_sb, xr), (wk_sb, xk))):
                    for m in range(8):
                        ci = 24 * t + 8 * br + m
                        if ci >= 4:
                            tensor.wait_ge(g["s_ev"], ci - 3)
                        last = None
                        for k in range(KT):
                            last = tensor.matmul(ps[:, ci % 4], wsb[:, k, 128 * m:128 * (m + 1)],
                                                 rhs[:, k], start=(k == 0), stop=(k == KT - 1))
                        last.then_inc(g["s_mm"], 1)
                tensor.wait_ge(g["s_ev"], 24 * t + 16)
                for m in range(8):
                    ci = 24 * t + 16 + m
                    if ci >= 4:
                        tensor.wait_ge(g["s_ev"], ci - 3)
                    last = None
                    for k in range(KT):
                        last = tensor.matmul(ps[:, ci % 4], wv_sb[:, k, 128 * m:128 * (m + 1)],
                                             ka_sb[:, k], start=(k == 0), stop=(k == KT - 1))
                    last.then_inc(g["s_mm"], 1)
                tensor.wait_ge(g["s_v1"], 3 * t + 2)
                if t >= 1:
                    tensor.wait_ge(g["s_row"], t)
                for k in range(KT):
                    tensor.matmul(sum_a[:], onesc[:], x1b[:, k], start=(k == 0), stop=(k == KT - 1))
                for k in range(KT):
                    last = tensor.matmul(sum_b[:], onesc[:], g0[:, k], start=(k == 0), stop=(k == KT - 1))
                last.then_inc(g["s_sum"], 1)
                tensor.wait_ge(g["s_row"], t + 1)
                if t >= 1:
                    tensor.wait_ge(g["s_ap"], t)
                tensor.matmul(bca[:], selc[:, 0:128], rows[:, 0:512], start=True, stop=True)
                tensor.matmul(bcb[:], selc[:, 0:128], rows[:, 512:1024], start=True,
                              stop=True).then_inc(g["s_bc"], 1)

        @block.scalar
        def _(scalar):
            if bar is not None:
                scalar.wait_ge(bar, ph)
            for t in range(NT):
                for m in range(8):
                    ci = 24 * t + m
                    scalar.wait_ge(g["s_mm"], ci + 1)
                    if t >= 1 and m == 0:
                        scalar.wait_ge(g["s_ev"], 24 * t)
                    scalar.activation(r_sb[:, m], ps[:, ci % 4], AF.Sigmoid).then_inc(g["s_ev"], 1)
                for m in range(8):
                    ci = 24 * t + 8 + m
                    scalar.wait_ge(g["s_mm"], ci + 1)
                    scalar.activation(krl[:], ps[:, ci % 4], AF.Relu)
                    scalar.activation(ka_sb[:, m], krl[:], AF.Square).then_inc(g["s_ev"], 1)
                scalar.wait_ge(g["s_ev"], 24 * (t + 1))
                if t >= 1:
                    scalar.wait_ge(g["s_sum"], t)
                scalar.activation(g0[:], f0[:], AF.Silu).then_inc(g["s_g0"], 1)
                scalar.wait_ge(g["s_cmb"], t + 1)
                scalar.activation(g0[:], x1[:], AF.Square).then_inc(g["s_v1"], 1)
                scalar.wait_ge(g["s_v1"], 3 * t + 3)
                scalar.activation(rows[0:1, 4 * 512:5 * 512], rows[0:1, 3 * 512:4 * 512], AF.Ln)
                scalar.activation(rows[0:1, 0:512], rows[0:1, 4 * 512:5 * 512], AF.Exp,
                                  scale=-0.5).then_inc(g["s_sc"], 1)

        @block.vector
        def _(vector):
            if bar is not None:
                vector.wait_ge(bar, ph)
            vector.memset(xs[:, 0, :, 0:1], 0.0)
            vector.memset(rows[:], 0.0)
            for t in range(NT):
                b = t % 2
                vector.wait_ge(g["s_ld"], 16 * (t + 1))
                r1 = None
                for k in range(KT):
                    r1 = vector.tensor_tensor(dd[:, k], xs[:, b, k, 1:513], xs[:, b, k, 0:512],
                                              ALU.subtract)
                r1.then_inc(g["s_mix"], 1)
                for k in range(KT):
                    r1 = vector.scalar_tensor_tensor(xr[:, k], dd[:, k], mur_sb[:, k:k + 1],
                                                     xs[:, b, k, 0:512], ALU.mult, ALU.add)
                r1.then_inc(g["s_mix"], 1)
                for k in range(KT):
                    r2 = vector.scalar_tensor_tensor(xk[:, k], dd[:, k], muk_sb[:, k:k + 1],
                                                     xs[:, b, k, 0:512], ALU.mult, ALU.add)
                if t + 1 < NT:
                    nb = (t + 1) % 2
                    r2 = vector.tensor_tensor(xs[:, nb, :, 0:1], xs[:, b, :, 512:513],
                                              xs[:, b, :, 512:513], ALU.max)
                r2.then_inc(g["s_mix"], 1)
                for m in range(8):
                    ci = 24 * t + 16 + m
                    vector.wait_ge(g["s_mm"], ci + 1)
                    vector.tensor_tensor(f0[:, m], ps[:, ci % 4], r_sb[:, m],
                                         ALU.mult).then_inc(g["s_ev"], 1)
                vector.wait_ge(g["s_g0"], t + 1)
                vector.wait_ge(g["s_ldg"], 32 * (t + 1))
                if t >= 1:
                    vector.wait_ge(g["s_st"], 32 * t - 16)
                    vector.wait_ge(g["s_sc"], t)
                for k in range(KT):
                    vector.tensor_tensor(x1b[:, k], s0_sb[:, 0, k], g0[:, k], ALU.mult)
                    r3 = vector.tensor_tensor(x1[:, k], xf[:, 0, k], x1b[:, k], ALU.add)
                r3.then_inc(g["s_cmb"], 1)
                for k in range(KT):
                    r4 = vector.tensor_copy(x1b[:, k], x1[:, k])
                r4.then_inc(g["s_v1"], 1)
                vector.wait_ge(g["s_sum"], t + 1)
                if t >= 1:
                    vector.wait_ge(g["s_bc"], t)
                rr = lambda j: rows[0:1, 512 * j:512 * (j + 1)]
                vector.tensor_scalar_mul(rr(2), sum_a[:], 1.0 / D)
                vector.tensor_scalar_mul(rr(3), sum_b[:], 1.0 / D)
                vector.tensor_tensor(rr(4), rr(2), rr(2), ALU.mult)
                vector.tensor_tensor(rr(3), rr(3), rr(4), ALU.subtract)
                vector.tensor_scalar_add(rr(3), rr(3), 1e-5).then_inc(g["s_v1"], 1)
                vector.wait_ge(g["s_sc"], t + 1)
                vector.tensor_tensor(rr(4), rr(2), rr(0), ALU.mult)
                vector.tensor_scalar_mul(rr(1), rr(4), -1.0).then_inc(g["s_row"], 1)
                vector.wait_ge(g["s_bc"], t + 1)
                if t >= 1:
                    vector.wait_ge(g["s_st"], 32 * t)
                for k in range(KT):
                    vector.tensor_tensor(xn1[:, k], x1[:, k], bca[:], ALU.mult)
                    r5 = vector.scalar_tensor_tensor(xn1[:, k], xn1[:, k], 1.0, bcb[:],
                                                     ALU.mult, ALU.add)
                r5.then_inc(g["s_ap"], 1)

    return nc



L1F_SEMS = ("s_w", "s_wg", "s_wf", "s_ld", "s_xg", "s_mm", "s_eva", "s_evd", "s_if", "s_bc", "s_bce", "s_fiv", "s_fis", "s_fisc", "s_fid", "s_st")


def build_l1front(nc=None, block=None, g=None, io=None, bar=None, ph=0):
    standalone = nc is None
    if standalone:
        nc = bass.Bass("TRN2", target_bir_lowering=False)
    def dram(name, shape, dt, kind):
        if io is not None:
            return io[name]
        return nc.dram_tensor(name, shape, dt, kind=kind).ap()
    xn1T = dram("xn1T", [D, SEQ], BF16, "ExternalInput")
    qwT = dram("qwT", [D, D], F32, "ExternalInput")
    kwT = dram("kwT", [D, D], F32, "ExternalInput")
    vwT = dram("vwT", [D, D], F32, "ExternalInput")
    iwT = dram("iwT", [D, 64], F32, "ExternalInput")
    fwT = dram("fwT", [D, 64], F32, "ExternalInput")
    gmr = dram("gmr", [128, 8], F32, "ExternalInput")
    btr = dram("btr", [128, 8], F32, "ExternalInput")
    qbr = dram("qbr", [128, 8], F32, "ExternalInput")
    kbr2 = dram("kbr2", [128, 8], F32, "ExternalInput")
    vbr2 = dram("vbr2", [128, 8], F32, "ExternalInput")
    kbrow = dram("kbrow", [32, D], F32, "ExternalInput")
    vbrow = dram("vbrow", [32, D], F32, "ExternalInput")
    ibr = dram("ibr", [64, 1], F32, "ExternalInput")
    fbr = dram("fbr", [64, 1], F32, "ExternalInput")
    c44 = dram("c44", [64, 1], F32, "ExternalInput")
    c44n = dram("c44n", [64, 1], F32, "ExternalInput")
    sel32 = dram("sel32", [32, 32 * 128], F32, "ExternalInput")

    qT = dram("qT", [D, SEQ], BF16, "ExternalOutput")
    kT = dram("kT", [D, SEQ], BF16, "ExternalOutput")
    vT = dram("vT", [D, SEQ], BF16, "ExternalOutput")
    ksm = dram("ksm", [SEQ, D], BF16, "ExternalOutput")
    vsm = dram("vsm", [SEQ, D], BF16, "ExternalOutput")
    atil = dram("atil", [64, SEQ], F32, "ExternalOutput")
    btil = dram("btil", [64, SEQ], F32, "ExternalOutput")
    decd = dram("decd", [64, SEQ // 128], F32, "ExternalOutput")

    NT = SEQ // 512
    KT = 8
    NCH = 40
    xnr = xn1T.rearrange("(k p) n -> p k n", p=128)
    qwr = qwT.rearrange("(k p) m -> p k m", p=128)
    kwr = kwT.rearrange("(k p) m -> p k m", p=128)
    vwr = vwT.rearrange("(k p) m -> p k m", p=128)
    iwr = iwT.rearrange("(k p) m -> p k m", p=128)
    fwr = fwT.rearrange("(k p) m -> p k m", p=128)
    qspr = qT.rearrange("(k p) n -> p k n", p=128)
    kspr = kT.rearrange("(k p) n -> p k n", p=128)
    vspr = vT.rearrange("(k p) n -> p k n", p=128)
    ksmr = ksm.rearrange("(a p) d -> p a d", p=128)
    vsmr = vsm.rearrange("(a p) d -> p a d", p=128)

    with contextlib.ExitStack() as ctx:
        ent = ctx.enter_context
        qw_sb = ent(nc.sbuf_tensor("qw_sb", [128, KT, D], BF16))
        kw_sb = ent(nc.sbuf_tensor("kw_sb", [128, KT, D], BF16))
        vw_sb = ent(nc.sbuf_tensor("vw_sb", [128, KT, D], BF16))
        iw_sb = ent(nc.sbuf_tensor("iw_sb", [128, KT, 64], BF16))
        fw_sb = ent(nc.sbuf_tensor("fw_sb", [128, KT, 64], BF16))
        gm_sb = ent(nc.sbuf_tensor("gm_sb", [128, 8], F32))
        bt_sb = ent(nc.sbuf_tensor("bt_sb", [128, 8], F32))
        qb_sb = ent(nc.sbuf_tensor("qb_sb", [128, 8], F32))
        kb_sb = ent(nc.sbuf_tensor("kb_sb", [128, 8], F32))
        vb_sb2 = ent(nc.sbuf_tensor("vb_sb2", [128, 8], F32))
        kbr_sb = ent(nc.sbuf_tensor("kbr_sb", [32, D], F32))
        vbr_sb = ent(nc.sbuf_tensor("vbr_sb", [32, D], F32))
        ib_sb = ent(nc.sbuf_tensor("ib_sb", [64, 1], F32))
        fb_sb = ent(nc.sbuf_tensor("fb_sb", [64, 1], F32))
        c44_sb = ent(nc.sbuf_tensor("c44_sb", [64, 1], F32))
        c44n_sb = ent(nc.sbuf_tensor("c44n_sb", [64, 1], F32))
        selc = ent(nc.sbuf_tensor("selc", [32, 256], F32))
        kbb = ent(nc.sbuf_tensor("kbb", [128, D], F32))
        vbb = ent(nc.sbuf_tensor("vbb", [128, D], F32))
        xn = ent(nc.sbuf_tensor("xn", [128, 2, KT, 512], BF16))
        xg = ent(nc.sbuf_tensor("xg", [128, 2, KT, 512], BF16))
        qsp = ent(nc.sbuf_tensor("qsp", [128, 8, 512], BF16))
        ksp = ent(nc.sbuf_tensor("ksp", [128, 8, 512], BF16))
        vsp = ent(nc.sbuf_tensor("vsp", [128, 8, 512], BF16))
        ksm_sb = ent(nc.sbuf_tensor("ksm_sb", [128, 4, D], BF16))
        vsm_sb = ent(nc.sbuf_tensor("vsm_sb", [128, 4, D], BF16))
        fi = ent(nc.sbuf_tensor("fi", [64, 8, 512], F32))
        carry = ent(nc.sbuf_tensor("carry", [64, 4], F32))
        decs = ent(nc.sbuf_tensor("decs", [64, 16], F32))
        zrow = ent(nc.sbuf_tensor("zrow", [64, 512], F32))
        ps = ent(nc.psum_tensor("ps", [128, 4, 512], F32))
        ps_if = ent(nc.psum_tensor("ps_if", [64, 2, 512], F32))
        ps_bc = ent(nc.psum_tensor("ps_bc", [128, 2, 512], F32))
        if standalone:
            g = {}
            for name in L1F_SEMS:
                g[name] = ent(nc.semaphore(name))
            block = ent(nc.Block())

        def bank_wait(eng, t, ci):
            gi = NCH * t + ci
            if gi < 4:
                return
            pg = gi - 4
            pt, pc = divmod(pg, NCH)
            if pc < 24:
                eng.wait_ge(g["s_eva"], 24 * pt + pc + 1)
            else:
                eng.wait_ge(g["s_evd"], 16 * pt + (pc - 24) + 1)

        @block.gpsimd
        def _(gpsimd):
            if bar is not None:
                gpsimd.wait_ge(bar, ph)
            gpsimd.dma_start(out=qw_sb[:], in_=qwr).then_inc(g["s_wg"], 16)
            gpsimd.dma_start(out=kw_sb[:], in_=kwr).then_inc(g["s_wg"], 16)
            gpsimd.dma_start(out=vw_sb[:], in_=vwr).then_inc(g["s_wg"], 16)
            gpsimd.dma_start(out=iw_sb[:], in_=iwr).then_inc(g["s_wg"], 16)
            gpsimd.dma_start(out=fw_sb[:], in_=fwr).then_inc(g["s_wg"], 16)

        @block.sync
        def _(sync):
            if bar is not None:
                sync.wait_ge(bar, ph)
            for ap_out, ap_in in ((gm_sb[:], gmr), (bt_sb[:], btr),
                                  (qb_sb[:], qbr), (kb_sb[:], kbr2), (vb_sb2[:], vbr2),
                                  (kbr_sb[:], kbrow),
                                  (vbr_sb[:], vbrow), (ib_sb[:], ibr), (fb_sb[:], fbr),
                                  (c44_sb[:], c44), (c44n_sb[:], c44n),
                                  (selc[:], sel32[:, 0:256])):
                sync.dma_start(out=ap_out, in_=ap_in).then_inc(g["s_w"], 16)
            for t in range(NT + 1):
                if t < NT:
                    b = t % 2
                    if t >= 2:
                        sync.wait_ge(g["s_mm"], NCH * (t - 1))
                    sync.dma_start(out=xn[:, b], in_=xnr[:, :, 512 * t:512 * (t + 1)]).then_inc(g["s_ld"], 16)
                if t >= 1:
                    sync.wait_ge(g["s_eva"], 24 * t)
                    sync.wait_ge(g["s_evd"], 16 * t)
                    sync.dma_start(out=qspr[:, :, 512 * (t - 1):512 * t], in_=qsp[:]).then_inc(g["s_st"], 16)
                    sync.dma_start(out=kspr[:, :, 512 * (t - 1):512 * t], in_=ksp[:]).then_inc(g["s_st"], 16)
                    sync.dma_start(out=vspr[:, :, 512 * (t - 1):512 * t], in_=vsp[:]).then_inc(g["s_st"], 16)
                    sync.dma_start(out=ksmr[:, 4 * (t - 1):4 * t], in_=ksm_sb[:]).then_inc(g["s_st"], 16)
                    sync.dma_start(out=vsmr[:, 4 * (t - 1):4 * t], in_=vsm_sb[:]).then_inc(g["s_st"], 16)
                    sync.wait_ge(g["s_fiv"], t)
                    sync.dma_start(out=atil[:, 512 * (t - 1):512 * t], in_=fi[:, 5]).then_inc(g["s_st"], 16)
                    sync.dma_start(out=btil[:, 512 * (t - 1):512 * t], in_=fi[:, 6]).then_inc(g["s_st"], 16)
                    sync.dma_start(out=decd[:, 4 * (t - 1):4 * t], in_=decs[:, 12:16]).then_inc(g["s_st"], 16)
            sync.wait_ge(g["s_st"], 16 * 8 * NT)

        @block.tensor
        def _(tensor):
            if bar is not None:
                tensor.wait_ge(bar, ph)
            tensor.wait_ge(g["s_wg"], 80)
            tensor.wait_ge(g["s_wf"], 1)
            tensor.wait_ge(g["s_w"], 192)
            tensor.matmul(ps_bc[:, 0], selc[:, 0:128], kbr_sb[:, 0:512], start=True, stop=True)
            tensor.matmul(ps_bc[:, 1], selc[:, 0:128], kbr_sb[:, 512:1024],
                          start=True, stop=True).then_inc(g["s_bc"], 1)
            tensor.wait_ge(g["s_bce"], 1)
            tensor.matmul(ps_bc[:, 0], selc[:, 0:128], vbr_sb[:, 0:512], start=True, stop=True)
            tensor.matmul(ps_bc[:, 1], selc[:, 0:128], vbr_sb[:, 512:1024],
                          start=True, stop=True).then_inc(g["s_bc"], 2)
            for t in range(NT):
                b = t % 2
                tensor.wait_ge(g["s_xg"], t + 1)
                ci = 0
                for wsb in (qw_sb, kw_sb, vw_sb):
                    for m in range(8):
                        bank_wait(tensor, t, ci)
                        last = None
                        for k in range(KT):
                            last = tensor.matmul(ps[:, (NCH * t + ci) % 4],
                                                 wsb[:, k, 128 * m:128 * (m + 1)],
                                                 xg[:, b, k], start=(k == 0), stop=(k == KT - 1))
                        last.then_inc(g["s_mm"], 1)
                        ci += 1
                for wsb in (kw_sb, vw_sb):
                    for a in range(4):
                        for half in range(2):
                            bank_wait(tensor, t, ci)
                            last = None
                            for k in range(KT):
                                last = tensor.matmul(ps[:, (NCH * t + ci) % 4],
                                                     xg[:, b, k, 128 * a:128 * (a + 1)],
                                                     wsb[:, k, 512 * half:512 * (half + 1)],
                                                     start=(k == 0), stop=(k == KT - 1))
                            last.then_inc(g["s_mm"], 1)
                            ci += 1
                if t >= 1:
                    tensor.wait_ge(g["s_fid"], t)
                for k in range(KT):
                    tensor.matmul(ps_if[:, 0], iw_sb[:, k], xg[:, b, k],
                                  start=(k == 0), stop=(k == KT - 1))
                for k in range(KT):
                    last = tensor.matmul(ps_if[:, 1], fw_sb[:, k], xg[:, b, k],
                                         start=(k == 0), stop=(k == KT - 1))
                last.then_inc(g["s_if"], 1)

        @block.scalar
        def _(scalar):
            if bar is not None:
                scalar.wait_ge(bar, ph)
            scalar.wait_ge(g["s_w"], 192)
            scalar.mul(kb_sb[:], kb_sb[:], 0.25)
            scalar.wait_ge(g["s_bc"], 1)
            scalar.copy(kbb[:, 0:512], ps_bc[:, 0])
            scalar.mul(kbb[:, 0:512], kbb[:, 0:512], 0.25)
            scalar.copy(kbb[:, 512:1024], ps_bc[:, 1])
            scalar.mul(kbb[:, 512:1024], kbb[:, 512:1024], 0.25).then_inc(g["s_bce"], 1)
            scalar.wait_ge(g["s_bc"], 2)
            scalar.copy(vbb[:, 0:512], ps_bc[:, 0])
            scalar.copy(vbb[:, 512:1024], ps_bc[:, 1]).then_inc(g["s_bce"], 2)
            for t in range(NT):
                for ci in range(24):
                    scalar.wait_ge(g["s_mm"], NCH * t + ci + 1)
                    if t >= 1 and ci == 0:
                        scalar.wait_ge(g["s_st"], 16 * 8 * t)
                    tau = ci % 8
                    dst = (qsp, ksp, vsp)[ci // 8]
                    bias = (qb_sb, kb_sb, vb_sb2)[ci // 8]
                    scalar.add(dst[:, tau], ps[:, (NCH * t + ci) % 4],
                               bias[:, tau:tau + 1]).then_inc(g["s_eva"], 1)
                scalar.wait_ge(g["s_fis"], 3 * t + 1)
                scalar.activation(fi[:, 5], fi[:, 7], AF.Exp,
                                  bias=c44_sb[:]).then_inc(g["s_fisc"], 1)
                scalar.wait_ge(g["s_fis"], 3 * t + 2)
                scalar.activation(fi[:, 6], fi[:, 7], AF.Exp, scale=-1.0,
                                  bias=c44n_sb[:]).then_inc(g["s_fisc"], 1)
                scalar.wait_ge(g["s_fis"], 3 * t + 3)
                scalar.activation(decs[:, 12:16], decs[:, 8:12], AF.Exp).then_inc(g["s_fiv"], 1)

        @block.vector
        def _(vector):
            if bar is not None:
                vector.wait_ge(bar, ph)
            vector.wait_ge(g["s_wg"], 32)
            vector.tensor_scalar_mul(kw_sb[:], kw_sb[:], 0.25).then_inc(g["s_wf"], 1)
            vector.memset(carry[:], 0.0)
            vector.memset(zrow[:], 0.0)
            for t in range(NT):
                b2_ = t % 2
                vector.wait_ge(g["s_ld"], 16 * (t + 1))
                vector.wait_ge(g["s_w"], 192)
                if t >= 2:
                    vector.wait_ge(g["s_mm"], NCH * (t - 1))
                for k in range(KT):
                    op = vector.scalar_tensor_tensor(
                        xg[:, b2_, k].unsqueeze(1), xn[:, b2_, k].unsqueeze(1),
                        gm_sb[:, k:k + 1],
                        bt_sb[:, k:k + 1].unsqueeze(2).broadcast_to([128, 1, 512]),
                        ALU.mult, ALU.add)
                    if k == KT - 1:
                        op.then_inc(g["s_xg"], 1)
                for ci in range(24, 40):
                    vector.wait_ge(g["s_mm"], NCH * t + ci + 1)
                    if t >= 1 and ci == 24:
                        vector.wait_ge(g["s_st"], 16 * 8 * t)
                    pss = ps[:, (NCH * t + ci) % 4]
                    a = (ci - 24) % 8 // 2
                    half = (ci - 24) % 2
                    if ci < 32:
                        vector.tensor_tensor(ksm_sb[:, a, 512 * half:512 * (half + 1)], pss,
                                             kbb[:, 512 * half:512 * (half + 1)],
                                             ALU.add).then_inc(g["s_evd"], 1)
                    else:
                        vector.tensor_tensor(vsm_sb[:, a, 512 * half:512 * (half + 1)], pss,
                                             vbb[:, 512 * half:512 * (half + 1)],
                                             ALU.add).then_inc(g["s_evd"], 1)
                vector.wait_ge(g["s_if"], t + 1)
                vector.tensor_scalar_add(fi[:, 0], ps_if[:, 0], ib_sb[:])
                vector.tensor_scalar_add(fi[:, 1], ps_if[:, 1], fb_sb[:]).then_inc(g["s_fid"], 1)
                vector.tensor_tensor_scan(fi[:, 2], fi[:, 1], zrow[:], carry[:, 0:1],
                                          ALU.add, ALU.add)
                vector.tensor_copy(decs[:, 0:1], carry[:, 1:2])
                vector.tensor_copy(carry[:, 0:1], fi[:, 2, 511:512])
                vector.tensor_tensor(fi[:, 3], fi[:, 0], fi[:, 2], ALU.subtract)
                vector.tensor_tensor_scan(fi[:, 4], zrow[:], fi[:, 3], carry[:, 1:2],
                                          ALU.add, ALU.max)
                gce = fi[:, 4].rearrange("p (c l) -> p c l", l=128)[:, :, 127]
                for cc in range(4):
                    vector.tensor_tensor(decs[:, 1 + cc:2 + cc], gce[:, cc:cc + 1],
                                         gce[:, cc:cc + 1], ALU.max)
                vector.tensor_copy(carry[:, 1:2], fi[:, 4, 511:512])
                if t >= 1:
                    vector.wait_ge(g["s_fiv"], t)
                    vector.wait_ge(g["s_fisc"], 2 * t)
                gceb = gce.unsqueeze(2).broadcast_to([64, 4, 128])
                vector.tensor_tensor(fi[:, 7].rearrange("p (c l) -> p c l", l=128),
                                     fi[:, 3].rearrange("p (c l) -> p c l", l=128),
                                     gceb, ALU.subtract).then_inc(g["s_fis"], 1)
                vector.wait_ge(g["s_fisc"], 2 * t + 1)
                vector.tensor_tensor(fi[:, 7].rearrange("p (c l) -> p c l", l=128),
                                     fi[:, 4].rearrange("p (c l) -> p c l", l=128),
                                     gceb, ALU.subtract)
                vector.tensor_scalar_max(fi[:, 7], fi[:, 7],
                                         -124.0).then_inc(g["s_fis"], 1)
                vector.wait_ge(g["s_fisc"], 2 * t + 2)
                vector.tensor_tensor(decs[:, 8:12], decs[:, 0:4], decs[:, 1:5],
                                     ALU.subtract).then_inc(g["s_fis"], 1)

    return nc



ML_SEMS = ("s_w", "s_wg", "s_ld", "s_ldg", "s_ab", "s_abe", "s_dt", "s_dte", "s_dmm", "s_dece", "s_sb", "s_sbe", "s_ske", "s_ktl", "s_pv", "s_den", "s_scl", "s_yint", "s_ctm", "s_up", "s_yb", "s_hn", "s_ht", "s_hte", "s_opm", "s_ope", "s_st", "s_dbg", "s_dnr", "s_rcp", "s_z0")


def build_mlstm(nc=None, block=None, g=None, io=None, bar=None, ph=0):
    standalone = nc is None
    if standalone:
        nc = bass.Bass("TRN2", target_bir_lowering=False)
    def dram(name, shape, dt, kind):
        if io is not None:
            return io[name]
        return nc.dram_tensor(name, shape, dt, kind=kind).ap()
    qT = dram("qT", [D, SEQ], BF16, "ExternalInput")
    kT = dram("kT", [D, SEQ], BF16, "ExternalInput")
    vT = dram("vT", [D, SEQ], BF16, "ExternalInput")
    ksmd = dram("ksm", [SEQ, D], BF16, "ExternalInput")
    vsmd = dram("vsm", [SEQ, D], BF16, "ExternalInput")
    atil = dram("atil", [64, SEQ], F32, "ExternalInput")
    btil = dram("btil", [64, SEQ], F32, "ExternalInput")
    decd = dram("decd", [64, SEQ // 128], F32, "ExternalInput")
    owT = dram("owT", [D, D], F32, "ExternalInput")
    obr = dram("obr", [128, 8], F32, "ExternalInput")
    xn1T = dram("xn1T", [D, SEQ], BF16, "ExternalInput")
    sel32 = dram("sel32", [32, 32 * 128], F32, "ExternalInput")
    id64 = dram("ident64f", [64, 64], F32, "ExternalInput")
    pm64 = dram("perm64", [64, 64], F32, "ExternalInput")
    c44q = dram("c44_128", [128, 1], F32, "ExternalInput")
    idf = dram("ident128", [128, 128], F32, "ExternalInput")
    mask8 = dram("mask8", [128, 8 * 128], BF16, "ExternalInput")

    s1T = dram("s1T", [D, SEQ], BF16, "ExternalOutput")
    if DBG_ML:
        d_abT = nc.dram_tensor("d_abT", [128, 128], F32, kind="ExternalOutput").ap()
        d_abTb = nc.dram_tensor("d_abTb", [128, 64], BF16, kind="ExternalOutput").ap()
        d_ssb = nc.dram_tensor("d_ssb", [128, 128], BF16, kind="ExternalOutput").ap()
        d_skb = nc.dram_tensor("d_skb", [128, 128], BF16, kind="ExternalOutput").ap()
        d_dn = nc.dram_tensor("d_dn", [128, 64], F32, kind="ExternalOutput").ap()
        d_dn2 = nc.dram_tensor("d_dn2", [128, 64], F32, kind="ExternalOutput").ap()
        d_csp = nc.dram_tensor("d_csp", [128, 272], F32, kind="ExternalOutput").ap()
        d_cspb = nc.dram_tensor("d_cspb", [128, 272], BF16, kind="ExternalOutput").ap()
        d_hsm = nc.dram_tensor("d_hsm", [128, 1024], F32, kind="ExternalOutput").ap()
        d_hcm = nc.dram_tensor("d_hcm", [128, 8, 512], BF16, kind="ExternalOutput").ap()
        d_vtl = nc.dram_tensor("d_vtl", [128, D], BF16, kind="ExternalOutput").ap()
        d_dbq = nc.dram_tensor("d_dbq", [128, 256], F32, kind="ExternalOutput").ap()

    NC_ = SEQ // 128
    NG = SEQ // 512
    qsp_view = qT.rearrange("(tau g e) n -> g e tau n", g=4, e=16)
    ksp_view = kT.rearrange("(tau g e) n -> g e tau n", g=4, e=16)
    vsp_view = vT.rearrange("(tau g e) n -> g e tau n", g=4, e=16)
    ksmr = ksmd.rearrange("(a p) d -> p a d", p=128)
    vsmr = vsmd.rearrange("(a p) d -> p a d", p=128)
    owr = owT.rearrange("(k p) m -> p k m", p=128)
    xnr = xn1T.rearrange("(k p) n -> p k n", p=128)
    s1r = s1T.rearrange("(k p) n -> p k n", p=128)

    with contextlib.ExitStack() as ctx:
        ent = ctx.enter_context
        ow_sb = ent(nc.sbuf_tensor("ow_sb", [128, 8, D], BF16))
        ob_sb = ent(nc.sbuf_tensor("ob_sb", [128, 8], F32))
        selc = ent(nc.sbuf_tensor("selc", [32, 512], F32))
        id64c = ent(nc.sbuf_tensor("id64c", [64, 64], F32))
        pm64c = ent(nc.sbuf_tensor("pm64c", [64, 64], F32))
        c44c = ent(nc.sbuf_tensor("c44c", [128, 1], F32))
        idfc = ent(nc.sbuf_tensor("idfc", [128, 128], F32))
        m8c = ent(nc.sbuf_tensor("m8c", [128, 8 * 128], BF16))
        qsp = ent(nc.sbuf_tensor("qsp", [128, 2, 16, 128], BF16))
        ksp = ent(nc.sbuf_tensor("ksp", [128, 2, 16, 128], BF16))
        vsp = ent(nc.sbuf_tensor("vsp", [128, 2, 16, 128], BF16))
        ksm_sb = ent(nc.sbuf_tensor("ksm_sb", [128, 2, D], BF16))
        vsm_sb = ent(nc.sbuf_tensor("vsm_sb", [128, 2, D], BF16))
        vtl = ent(nc.sbuf_tensor("vtl", [128, D], BF16))
        absb = ent(nc.sbuf_tensor("absb", [64, 2, 2, 512], F32))
        decg = ent(nc.sbuf_tensor("decg", [64, 2, 4], F32))
        zdec = ent(nc.sbuf_tensor("zdec", [32, 64], F32))
        abT = ent(nc.sbuf_tensor("abT", [128, 128], F32))
        abTb = ent(nc.sbuf_tensor("abTb", [128, 64], BF16))
        abTs = ent(nc.sbuf_tensor("abTs", [128, 64], F32))
        rcpb = ent(nc.sbuf_tensor("rcpb", [128, 64], F32))
        rcpb2 = ent(nc.sbuf_tensor("rcpb2", [128, 64], F32))
        decbq = ent(nc.sbuf_tensor("decbq", [128, 256], F32))
        ssb = ent(nc.sbuf_tensor("ssb", [128, 64, 128], BF16))
        skb = ent(nc.sbuf_tensor("skb", [128, 64, 128], BF16))
        csp = ent(nc.sbuf_tensor("csp", [128, 272], F32))
        cspb = ent(nc.sbuf_tensor("cspb", [128, 272], BF16))
        dnall = ent(nc.sbuf_tensor("dnall", [128, 64], F32))
        dnneg = ent(nc.sbuf_tensor("dnneg", [128, 64], F32))
        hsm = ent(nc.sbuf_tensor("hsm", [128, 1024], F32))
        hcm = ent(nc.sbuf_tensor("hcm", [128, 8, 512], BF16))
        tmp512 = ent(nc.sbuf_tensor("tmp512", [128, 512], F32))
        s1_sb = ent(nc.sbuf_tensor("s1_sb", [128, 8, 512], BF16))
        xng = ent(nc.sbuf_tensor("xng", [128, 8, 512], BF16))
        ps_sb = ent(nc.psum_tensor("ps_sb", [128, 2, 512], F32))
        ps_y = ent(nc.psum_tensor("ps_y", [128, 4, 512], F32))
        ps_c = ent(nc.psum_tensor("ps_c", [128, 512], F32))
        ps_m = ent(nc.psum_tensor("ps_m", [128, 512], F32))
        if standalone:
            g = {}
            for name in ML_SEMS:
                g[name] = ent(nc.semaphore(name))
            block = ent(nc.Block())

        def batch_end6(bb):
            c2, k2 = divmod(bb, 34)
            if k2 < 16:
                return ("s_sbe", 16 * c2 + k2 + 1)
            if k2 < 32:
                return ("s_ske", 16 * c2 + (k2 - 16) + 1)
            return ("s_hte", 2 * c2 + (k2 - 32) + 1)

        def aview(sigma, qr, w):
            return abT[:, 0:64].rearrange("p (t2 four) -> p t2 four", four=4)[
                :, 4 * qr:4 * qr + 4, sigma].unsqueeze(2).broadcast_to([128, 4, w])

        @block.gpsimd
        def _(gpsimd):
            if bar is not None:
                gpsimd.wait_ge(bar, ph)
            gpsimd.dma_start(out=ow_sb[:], in_=owr).then_inc(g["s_wg"], 16)
            for c in range(NC_):
                cb = c % 2
                if c >= 2:
                    gpsimd.wait_ge(g["s_pv"], c - 1)
                    gpsimd.wait_ge(g["s_ktl"], c - 1)
                    gpsimd.wait_ge(g["s_yint"], 4 * (c - 1))
                    gpsimd.wait_ge(g["s_ctm"], 4 * (c - 1))
                for dst, view in ((qsp, qsp_view), (ksp, ksp_view), (vsp, vsp_view)):
                    for gg in range(4):
                        gpsimd.dma_start(out=dst[32 * gg:32 * gg + 16, cb],
                                         in_=view[gg, :, :, 128 * c:128 * (c + 1)]).then_inc(g["s_ldg"], 16)
                gpsimd.dma_start(out=ksm_sb[:, cb], in_=ksmr[:, c]).then_inc(g["s_ldg"], 16)
                gpsimd.dma_start(out=vsm_sb[:, cb], in_=vsmr[:, c]).then_inc(g["s_ldg"], 16)

        @block.sync
        def _(sync):
            if bar is not None:
                sync.wait_ge(bar, ph)
            for ap_out, ap_in in ((ob_sb[:], obr), (selc[:], sel32[:, 0:512]),
                                  (id64c[:], id64), (pm64c[:], pm64), (c44c[:], c44q),
                                  (idfc[:], idf), (m8c[:], mask8)):
                sync.dma_start(out=ap_out, in_=ap_in).then_inc(g["s_w"], 16)
            for gi in range(NG):
                gb = gi % 2
                if gi >= 2:
                    sync.wait_ge(g["s_hn"], 16 * (gi - 1))
                    sync.wait_ge(g["s_up"], 16 * (gi - 1))
                sync.dma_start(out=absb[0:64, 0, gb], in_=atil[:, 512 * gi:512 * (gi + 1)]).then_inc(g["s_ld"], 16)
                sync.dma_start(out=absb[0:64, 1, gb], in_=btil[:, 512 * gi:512 * (gi + 1)]).then_inc(g["s_ld"], 16)
                sync.dma_start(out=decg[:, gb], in_=decd[:, 4 * gi:4 * (gi + 1)]).then_inc(g["s_ld"], 16)
                if gi >= 1:
                    sync.wait_ge(g["s_ope"], 8 * gi)
                    sync.dma_start(out=s1r[:, :, 512 * (gi - 1):512 * gi], in_=s1_sb[:]).then_inc(g["s_st"], 16)
                sync.dma_start(out=xng[:], in_=xnr[:, :, 512 * gi:512 * (gi + 1)]).then_inc(g["s_ld"], 16)
            sync.wait_ge(g["s_ope"], 8 * NG)
            sync.dma_start(out=s1r[:, :, 512 * (NG - 1):512 * NG], in_=s1_sb[:]).then_inc(g["s_st"], 16)
            if DBG_ML:
                nd = 0
                for ap_out, src in ((d_abT, abT[:]), (d_abTb, abTb[:]),
                                    (d_ssb, ssb[:, 63, :]), (d_skb, skb[:, 63, :]),
                                    (d_dn, dnall[:]), (d_dn2, rcpb[:]),
                                    (d_csp, csp[:]), (d_cspb, cspb[:]),
                                    (d_hsm, hsm[:]), (d_hcm, hcm[:]),
                                    (d_vtl, vtl[:]), (d_dbq, decbq[:])):
                    sync.dma_start(out=ap_out, in_=src).then_inc(g["s_dbg"], 16)
                    nd += 1
                sync.wait_ge(g["s_dbg"], 16 * nd)
            sync.wait_ge(g["s_st"], 16 * NG)

        @block.tensor
        def _(tensor):
            if bar is not None:
                tensor.wait_ge(bar, ph)
            tensor.wait_ge(g["s_w"], 112)
            tensor.wait_ge(g["s_wg"], 16)
            tensor.wait_ge(g["s_z0"], 1)
            for gi in range(NG):
                gb = gi % 2
                tensor.wait_ge(g["s_ld"], 64 * gi + 48)
                if gi >= 1:
                    tensor.wait_ge(g["s_dece"], gi)
                    tensor.wait_ge(g["s_up"], 16 * gi)
                    tensor.wait_ge(g["s_ope"], 8 * gi)
                tensor.transpose(ps_m[0:4, 128:192], decg[:, gb], id64c[:]).then_inc(g["s_dt"], 1)
                tensor.wait_ge(g["s_dte"], gi + 1)
                last = None
                for r in range(4):
                    last = tensor.matmul(ps_m[:, 192 + 64 * r:256 + 64 * r],
                                         selc[:, 128 * r:128 * (r + 1)], zdec[:, 0:64],
                                         start=True, stop=True)
                last.then_inc(g["s_dmm"], 1)
                for j in range(4):
                    c = 4 * gi + j
                    cb = c % 2
                    tensor.wait_ge(g["s_ldg"], 224 * (c + 1))
                    if j == 0:
                        tensor.wait_ge(g["s_dece"], gi + 1)
                    if c >= 1:
                        tensor.wait_ge(g["s_abe"], c)
                    tensor.transpose(ps_m[:, 0:64], absb[0:64, 0, gb, 128 * j:128 * (j + 1)], id64c[:])
                    tensor.transpose(ps_m[:, 128:192], absb[0:64, 1, gb, 128 * j:128 * (j + 1)],
                                     pm64c[:])
                    tensor.transpose(ps_m[:, 64:128], absb[0:64, 1, gb, 128 * j:128 * (j + 1)],
                                     id64c[:]).then_inc(g["s_ab"], 1)
                    bb0 = 34 * c
                    for kind in range(2):
                        lsp = vsp if kind == 0 else ksp
                        for sg in range(4):
                            tensor.drain()
                            for qr in range(4):
                                bb = bb0 + 16 * kind + 4 * sg + qr
                                if bb >= 2:
                                    w = batch_end6(bb - 2)
                                    tensor.wait_ge(g[w[0]], w[1])
                                last = None
                                for tt_ in range(4):
                                    tau = 4 * qr + tt_
                                    tp = (32 * sg, 0) if sg > 0 else None
                                    last = tensor.matmul(
                                        ps_sb[:, bb % 2, 128 * tt_:128 * (tt_ + 1)],
                                        lsp[32 * sg:32 * sg + 16, cb, tau],
                                        qsp[32 * sg:32 * sg + 16, cb, tau],
                                        start=True, stop=True, tile_position=tp)
                                last.then_inc(g["s_sb"], 1)
                    tensor.wait_ge(g["s_abe"], c + 1)
                    for sg in range(4):
                        tensor.drain()
                        tensor.wait_ge(g["s_sbe"], 16 * c + 4 * (sg + 1))
                        if c >= 1:
                            tensor.wait_ge(g["s_hn"], 4 * (c - 1) + sg + 1)
                            if sg == 0:
                                tensor.wait_ge(g["s_pv"], c)
                        last = None
                        for tau in range(16):
                            h = 4 * tau + sg
                            last = tensor.matmul(ps_y[:, sg, 16 * tau:16 * (tau + 1)],
                                                 ssb[:, 16 * sg + tau, :],
                                                 ksm_sb[:, cb, 16 * h:16 * (h + 1)],
                                                 start=(tau == 0), stop=False)
                        if sg == 3:
                            last.then_inc(g["s_pv"], 1)
                        tensor.wait_ge(g["s_ske"], 16 * c + 4 * (sg + 1))
                        last = None
                        for tau in range(16):
                            h = 4 * tau + sg
                            last = tensor.matmul(ps_y[:, sg, 256 + tau:257 + tau],
                                                 skb[:, 16 * sg + tau, :],
                                                 abTb[:, h:h + 1],
                                                 start=False, stop=False)
                        if sg == 3:
                            last.then_inc(g["s_den"], 1)
                        tensor.drain()
                        tensor.wait_ge(g["s_scl"], 4 * c + sg + 1)
                        last = None
                        for tau in range(16):
                            tp = (32 * sg, 0) if sg > 0 else None
                            tensor.matmul(ps_y[:, sg, 16 * tau:16 * (tau + 1)],
                                          qsp[32 * sg:32 * sg + 16, cb, tau],
                                          cspb[32 * sg:32 * sg + 16, 17 * tau:17 * tau + 16],
                                          start=False, stop=False, tile_position=tp)
                            last = tensor.matmul(ps_y[:, sg, 256 + tau:257 + tau],
                                                 qsp[32 * sg:32 * sg + 16, cb, tau],
                                                 cspb[32 * sg:32 * sg + 16, 17 * tau + 16:17 * tau + 17],
                                                 start=False, stop=(tau == 15), tile_position=tp)
                        last.then_inc(g["s_yint"], 1)
                    tensor.wait_ge(g["s_ktl"], c + 1)
                    tensor.wait_ge(g["s_up"], 4 * c)
                    for sg in range(4):
                        tensor.drain()
                        last = None
                        for tau in range(16):
                            h = 4 * tau + sg
                            tp = (0, 32 * sg) if sg > 0 else None
                            last = tensor.matmul(ps_c[32 * sg:32 * sg + 16, 17 * tau:17 * tau + 16],
                                                 vtl[:, 16 * h:16 * (h + 1)],
                                                 ksm_sb[:, cb, 16 * h:16 * (h + 1)],
                                                 start=True, stop=True, tile_position=tp)
                            last = tensor.matmul(ps_c[32 * sg:32 * sg + 16, 17 * tau + 16:17 * tau + 17],
                                                 ksm_sb[:, cb, 16 * h:16 * (h + 1)],
                                                 abTb[:, h:h + 1],
                                                 start=True, stop=True, tile_position=tp)
                        last.then_inc(g["s_ctm"], 1)
                    tensor.drain()
                    tensor.wait_ge(g["s_hn"], 4 * (c + 1))
                    for tb in range(2):
                        bb = 34 * c + 32 + tb
                        w = batch_end6(bb - 2)
                        tensor.wait_ge(g[w[0]], w[1])
                        last = None
                        for i in range(4):
                            last = tensor.transpose(
                                ps_sb[:, bb % 2, 128 * i:128 * (i + 1)],
                                hsm[:, 512 * tb + 128 * i:512 * tb + 128 * (i + 1)], idfc[:])
                        last.then_inc(g["s_ht"], 1)
                tensor.wait_ge(g["s_hte"], 8 * (gi + 1))
                tensor.wait_ge(g["s_abe"], 4 * (gi + 1))
                tensor.wait_ge(g["s_dece"], gi + 1)
                for m in range(8):
                    if 8 * gi + m >= 1:
                        tensor.wait_ge(g["s_ope"], 8 * gi + m)
                    last = None
                    for k in range(8):
                        last = tensor.matmul(ps_m[:], ow_sb[:, k, 128 * m:128 * (m + 1)],
                                             hcm[:, k, :], start=(k == 0), stop=(k == 7))
                    last.then_inc(g["s_opm"], 1)

        @block.scalar
        def _(scalar):
            if bar is not None:
                scalar.wait_ge(bar, ph)
            scalar.wait_ge(g["s_w"], 112)
            for gi in range(NG):
                scalar.wait_ge(g["s_dt"], gi + 1)
                if gi >= 1:
                    scalar.wait_ge(g["s_dmm"], gi)
                scalar.copy(zdec[0:4, 0:64], ps_m[0:4, 128:192]).then_inc(g["s_dte"], 1)
                scalar.wait_ge(g["s_dmm"], gi + 1)
                if gi >= 1:
                    scalar.wait_ge(g["s_scl"], 16 * gi)
                scalar.copy(decbq[:], ps_m[:, 192:448]).then_inc(g["s_dece"], 1)
                for j in range(4):
                    c = 4 * gi + j
                    scalar.wait_ge(g["s_ab"], c + 1)
                    if c >= 1:
                        scalar.wait_ge(g["s_hn"], 4 * c)
                        scalar.wait_ge(g["s_ktl"], c)
                        scalar.wait_ge(g["s_sbe"], 16 * c)
                        scalar.wait_ge(g["s_den"], c)
                        scalar.wait_ge(g["s_ctm"], 4 * c)
                    scalar.copy(abT[:], ps_m[:, 0:128])
                    scalar.copy(abTs[:], ps_m[:, 128:192])
                    scalar.copy(abTb[:], ps_m[:, 0:64]).then_inc(g["s_abe"], 1)
                    scalar.wait_ge(g["s_dnr"], c + 1)
                    scalar.activation(rcpb[:], dnneg[:], AF.Ln)
                    scalar.activation(rcpb[:], rcpb[:], AF.Exp,
                                      scale=-1.0).then_inc(g["s_rcp"], 1)

        @block.vector
        def _(vector):
            if bar is not None:
                vector.wait_ge(bar, ph)
            vector.wait_ge(g["s_w"], 112)
            vector.memset(csp[:], 0.0)
            vector.memset(cspb[:], 0.0)
            vector.memset(zdec[:], 0.0).then_inc(g["s_z0"], 1)
            for gi in range(NG):
                gb = gi % 2
                for j in range(4):
                    c = 4 * gi + j
                    cb = c % 2
                    vector.wait_ge(g["s_dece"], gi + 1)
                    for sg in range(4):
                        if c >= 1:
                            vector.wait_ge(g["s_yint"], 4 * (c - 1) + sg + 1)
                        cslice = csp[32 * sg:32 * sg + 16, :]
                        decview = decbq[32 * sg:32 * sg + 16, 64 * j:64 * (j + 1)].rearrange(
                            "p (t2 four) -> p t2 four", four=4)[:, :, sg].unsqueeze(2) \
                            .broadcast_to([16, 16, 17])
                        vector.tensor_tensor(
                            cspb[32 * sg:32 * sg + 16, :].rearrange("p (t2 s) -> p t2 s", s=17),
                            cslice.rearrange("p (t2 s) -> p t2 s", s=17),
                            decview, ALU.mult).then_inc(g["s_scl"], 1)
                        vector.tensor_tensor(cslice.rearrange("p (t2 s) -> p t2 s", s=17),
                                             cslice.rearrange("p (t2 s) -> p t2 s", s=17),
                                             decview, ALU.mult)
                    vector.wait_ge(g["s_abe"], c + 1)
                    vector.wait_ge(g["s_ldg"], 224 * (c + 1))
                    if c >= 1:
                        vector.wait_ge(g["s_ctm"], 4 * c)
                    vector.tensor_tensor(vtl[:].rearrange("p (h e) -> p h e", e=16),
                                         vsm_sb[:, cb].rearrange("p (h e) -> p h e", e=16),
                                         abT[:, 0:64].unsqueeze(2).broadcast_to([128, 64, 16]),
                                         ALU.mult).then_inc(g["s_ktl"], 1)
                    for k2 in range(16):
                        bb = 34 * c + k2
                        vector.wait_ge(g["s_sb"], 32 * c + k2 + 1)
                        if c >= 1 and k2 == 0:
                            vector.wait_ge(g["s_pv"], c)
                        sg, qr = divmod(k2, 4)
                        dst = ssb[:, 16 * sg + 4 * qr:16 * sg + 4 * qr + 4, :]
                        vector.tensor_tensor(dst,
                                             ps_sb[:, bb % 2].rearrange("p (a b) -> p a b", a=4),
                                             m8c[:, 0:512].rearrange("p (a b) -> p a b", a=4),
                                             ALU.mult)
                        vector.tensor_tensor(dst, dst, aview(sg, qr, 128),
                                             ALU.mult).then_inc(g["s_sbe"], 1)
                    for k2 in range(16):
                        bb = 34 * c + 16 + k2
                        vector.wait_ge(g["s_sb"], 32 * c + 16 + k2 + 1)
                        if c >= 1 and k2 == 0:
                            vector.wait_ge(g["s_den"], c)
                        sg, qr = divmod(k2, 4)
                        dst = skb[:, 16 * sg + 4 * qr:16 * sg + 4 * qr + 4, :]
                        vector.tensor_tensor(dst,
                                             ps_sb[:, bb % 2].rearrange("p (a b) -> p a b", a=4),
                                             m8c[:, 0:512].rearrange("p (a b) -> p a b", a=4),
                                             ALU.mult).then_inc(g["s_ske"], 1)
                    for sg in range(4):
                        vector.wait_ge(g["s_yint"], 4 * c + sg + 1)
                        vector.tensor_scalar_add(dnall[:, 16 * sg:16 * (sg + 1)],
                                                 ps_y[:, sg, 256:272], 0.0)
                    vector.wait_ge(g["s_ctm"], 4 * (c + 1))
                    for sg in range(4):
                        cslice = csp[32 * sg:32 * sg + 16, :]
                        vector.tensor_tensor(cslice.rearrange("p (t2 s) -> p t2 s", s=17),
                                             cslice.rearrange("p (t2 s) -> p t2 s", s=17),
                                             ps_c[32 * sg:32 * sg + 16, 0:272].rearrange(
                                                 "p (t2 s) -> p t2 s", s=17),
                                             ALU.add).then_inc(g["s_up"], 1)
                    vector.tensor_scalar_mul(dnneg[:], dnall[:], -1.0)
                    for _n in range(4):
                        vector.engine_nop()
                    vector.tensor_tensor(dnneg[:], dnneg[:], dnall[:], ALU.max)
                    for _n in range(4):
                        vector.engine_nop()
                    vector.tensor_tensor(dnneg[:], dnneg[:], abTs[:], ALU.mult)
                    for _n in range(4):
                        vector.engine_nop()
                    vector.tensor_scalar_max(dnneg[:], dnneg[:], 1.0).then_inc(g["s_dnr"], 1)
                    vector.wait_ge(g["s_rcp"], c + 1)
                    vector.tensor_tensor(rcpb2[:], rcpb[:], abTs[:], ALU.mult)
                    for _n in range(5):
                        vector.engine_nop()
                    for sg in range(4):
                        vector.tensor_tensor(
                            hsm[:].rearrange("p (t2 four e) -> p t2 four e", four=4, e=16)[:, :, sg, :],
                            ps_y[:, sg, 0:256].rearrange("p (t2 e) -> p t2 e", e=16),
                            rcpb2[:, 16 * sg:16 * (sg + 1)].unsqueeze(2).broadcast_to([128, 16, 16]),
                            ALU.mult).then_inc(g["s_hn"], 1)
                    for tb in range(2):
                        vector.wait_ge(g["s_ht"], 2 * c + tb + 1)
                        bb = 34 * c + 32 + tb
                        last = None
                        for i in range(4):
                            last = vector.tensor_copy(hcm[:, 4 * tb + i, 128 * j:128 * (j + 1)],
                                                      ps_sb[:, bb % 2, 128 * i:128 * (i + 1)])
                        last.then_inc(g["s_hte"], 1)
                for m in range(8):
                    vector.wait_ge(g["s_opm"], 8 * gi + m + 1)
                    vector.wait_ge(g["s_ld"], 64 * (gi + 1))
                    if m == 0 and gi >= 1:
                        vector.wait_ge(g["s_st"], 16 * gi)
                    vector.tensor_scalar_add(tmp512[:], ps_m[:], ob_sb[:, m:m + 1])
                    vector.tensor_tensor(s1_sb[:, m], tmp512[:], xng[:, m],
                                         ALU.add).then_inc(g["s_ope"], 1)

    return nc





QF_SEMS = ("s_w", "s_l1", "s_l2", "s_l3", "s_l4", "s_ab1", "s_ab3", "s_p1m", "s_p3m", "s_r2", "s_r4", "s_qs", "s_tt", "s_z", "s_bc", "s_gb", "s_gl", "s_c1", "s_c2", "s_lx", "s_lx1", "s_mmh", "s_he", "s_om", "s_sl", "s_oe", "s_st")


def build_qffn(nc=None, block=None, g=None, io=None, bar=None, ph=0):
    standalone = nc is None
    if standalone:
        nc = bass.Bass("TRN2", target_bir_lowering=False)
    def dram(name, shape, dt, kind):
        if io is not None:
            return io[name]
        return nc.dram_tensor(name, shape, dt, kind=kind).ap()
    x1T = dram("x1T", [D, SEQ], F32, "ExternalInput")
    xn1T = dram("xn1T", [D, SEQ], BF16, "ExternalInput")
    s1T = dram("s1T", [D, SEQ], BF16, "ExternalInput")
    W1T = dram("W1T", [D, 4096], F32, "ExternalInput")
    W2T = dram("W2T", [4096, D], F32, "ExternalInput")
    b1r = dram("b1r", [128, 32], F32, "ExternalInput")
    b2r = dram("b2r", [128, 8], F32, "ExternalInput")
    ones_f = dram("ones_f", [128, 1], F32, "ExternalInput")
    eps1 = dram("eps1", [1, 1], F32, "ExternalInput")
    sel32 = dram("sel32", [32, 32 * 128], F32, "ExternalInput")

    x2T = dram("x2T", [D, SEQ], F32, "ExternalOutput")

    NT2 = SEQ // 256
    NEL = float(D * 4096)
    w1r = W1T.rearrange("(k p) f -> p k f", p=128)
    w2r = W2T.rearrange("(k p) f -> p k f", p=128)
    xnr = xn1T.rearrange("(k p) n -> p k n", p=128)
    x1r = x1T.rearrange("(k p) n -> p k n", p=128)
    s1r = s1T.rearrange("(k p) n -> p k n", p=128)
    x2r = x2T.rearrange("(k p) n -> p k n", p=128)

    with contextlib.ExitStack() as ctx:
        ent = ctx.enter_context
        wq1 = ent(nc.sbuf_tensor("wq1", [128, 8, 4096], BF16))
        wq2 = ent(nc.sbuf_tensor("wq2", [128, 32, 1024], BF16))
        wpc = ent(nc.sbuf_tensor("wpc", [128, 2, 2048], F32))
        wi32 = ent(nc.sbuf_tensor("wi32", [128, 2048], mybir.dt.int32))
        qs = ent(nc.sbuf_tensor("qs", [128, 2], F32))
        gbc = ent(nc.sbuf_tensor("gbc", [128, 2], F32))
        z32q = ent(nc.sbuf_tensor("z32q", [32, 2], F32))
        gtmp = ent(nc.sbuf_tensor("gtmp", [1, 4], F32))
        selc = ent(nc.sbuf_tensor("selc", [32, 128], F32))
        onec = ent(nc.sbuf_tensor("onec", [128, 1], F32))
        epsc = ent(nc.sbuf_tensor("epsc", [1, 1], F32))
        b1_sb = ent(nc.sbuf_tensor("b1_sb", [128, 32], F32))
        b2_sb = ent(nc.sbuf_tensor("b2_sb", [128, 8], F32))
        xnt = ent(nc.sbuf_tensor("xnt", [128, 2, 8, 256], BF16))
        x1t = ent(nc.sbuf_tensor("x1t", [128, 8, 256], F32))
        s1t = ent(nc.sbuf_tensor("s1t", [128, 8, 256], BF16))
        hsb = ent(nc.sbuf_tensor("hsb", [128, 32, 256], BF16))
        htmp = ent(nc.sbuf_tensor("htmp", [128, 256], F32))
        otmp = ent(nc.sbuf_tensor("otmp", [128, 256], F32))
        g1b = ent(nc.sbuf_tensor("g1b", [128, 8, 256], BF16))
        tmp2 = ent(nc.sbuf_tensor("tmp2", [128, 8, 256], BF16))
        ps_h = ent(nc.psum_tensor("ps_h", [128, 2, 512], F32))
        ps_o = ent(nc.psum_tensor("ps_o", [128, 2, 512], F32))
        ps_q = ent(nc.psum_tensor("ps_q", [128, 512], F32))
        if standalone:
            g = {}
            for name in QF_SEMS:
                g[name] = ent(nc.semaphore(name))
            block = ent(nc.Block())

        @block.sync
        def _(sync):
            if bar is not None:
                sync.wait_ge(bar, ph)
            for ap_out, ap_in in ((b1_sb[:], b1r), (b2_sb[:], b2r),
                                  (onec[:], ones_f), (epsc[:], eps1),
                                  (selc[:], sel32[:, 0:128])):
                sync.dma_start(out=ap_out, in_=ap_in).then_inc(g["s_w"], 16)
            for p in range(16):
                if p >= 2:
                    sync.wait_ge(g["s_p1m"], 16 * (p - 1))
                sync.dma_start(out=wpc[:, p % 2],
                               in_=w1r[:, :, 256 * p:256 * (p + 1)]).then_inc(g["s_l1"], 16)
            for p in range(16):
                sync.wait_ge(g["s_p1m"], 256 if p < 2 else 256)
                if p >= 2:
                    sync.wait_ge(g["s_r2"], p - 1)
                sync.dma_start(out=wpc[:, p % 2],
                               in_=w1r[:, :, 256 * p:256 * (p + 1)]).then_inc(g["s_l2"], 16)
            for p in range(16):
                if p < 2:
                    sync.wait_ge(g["s_r2"], 15 + p)
                else:
                    sync.wait_ge(g["s_p3m"], 16 * (p - 1))
                sync.dma_start(out=wpc[:, p % 2],
                               in_=w2r[:, 2 * p:2 * (p + 1), :]).then_inc(g["s_l3"], 16)
            for p in range(16):
                sync.wait_ge(g["s_p3m"], 256)
                if p >= 2:
                    sync.wait_ge(g["s_r4"], p - 1)
                sync.dma_start(out=wpc[:, p % 2],
                               in_=w2r[:, 2 * p:2 * (p + 1), :]).then_inc(g["s_l4"], 16)
            for t in range(NT2):
                b = t % 2
                if t >= 2:
                    sync.wait_ge(g["s_mmh"], 256 * (t - 1))
                sync.dma_start(out=xnt[:, b],
                               in_=xnr[:, :, 256 * t:256 * (t + 1)]).then_inc(g["s_lx"], 16)
                sync.wait_ge(g["s_oe"], 8 * t)
                if t >= 1:
                    sync.dma_start(out=x2r[:, :, 256 * (t - 1):256 * t],
                                   in_=x1t[:]).then_inc(g["s_st"], 16)
                    sync.wait_ge(g["s_st"], 16 * t)
                sync.dma_start(out=x1t[:], in_=x1r[:, :, 256 * t:256 * (t + 1)]).then_inc(g["s_lx1"], 16)
                sync.dma_start(out=s1t[:], in_=s1r[:, :, 256 * t:256 * (t + 1)]).then_inc(g["s_lx1"], 16)
            sync.wait_ge(g["s_oe"], 8 * NT2)
            sync.dma_start(out=x2r[:, :, 256 * (NT2 - 1):256 * NT2],
                           in_=x1t[:]).then_inc(g["s_st"], 16)
            sync.wait_ge(g["s_st"], 16 * NT2)

        @block.tensor
        def _(tensor):
            if bar is not None:
                tensor.wait_ge(bar, ph)
            tensor.wait_ge(g["s_w"], 80)
            for p in range(16):
                tensor.wait_ge(g["s_ab1"], p + 1)
                for i in range(16):
                    mm = tensor.matmul(ps_q[:, 0:1],
                                       wpc[:, p % 2].rearrange("p (a m) -> p a m", m=128)[:, i],
                                       onec[:],
                                       start=(p == 0 and i == 0), stop=(p == 15 and i == 15))
                    mm.then_inc(g["s_p1m"], 1)
            tensor.wait_ge(g["s_qs"], 1)
            tensor.matmul(ps_q[0:1, 1:2], qs[:, 0:1], onec[:],
                          start=True, stop=True).then_inc(g["s_tt"], 1)
            tensor.wait_ge(g["s_z"], 1)
            tensor.matmul(ps_q[:, 2:3], selc[:], z32q[:, 0:1],
                          start=True, stop=True).then_inc(g["s_bc"], 1)
            for p in range(16):
                tensor.wait_ge(g["s_ab3"], p + 1)
                if p == 0:
                    tensor.wait_ge(g["s_gb"], 1)
                for i in range(16):
                    mm = tensor.matmul(ps_q[:, 0:1],
                                       wpc[:, p % 2].rearrange("p (a m) -> p a m", m=128)[:, i],
                                       onec[:],
                                       start=(p == 0 and i == 0), stop=(p == 15 and i == 15))
                    mm.then_inc(g["s_p3m"], 1)
            tensor.wait_ge(g["s_qs"], 2)
            tensor.matmul(ps_q[0:1, 1:2], qs[:, 1:2], onec[:],
                          start=True, stop=True).then_inc(g["s_tt"], 2)
            tensor.wait_ge(g["s_z"], 2)
            tensor.matmul(ps_q[:, 2:3], selc[:], z32q[:, 1:2],
                          start=True, stop=True).then_inc(g["s_bc"], 2)
            tensor.wait_ge(g["s_r2"], 16)
            tensor.wait_ge(g["s_r4"], 16)
            for t in range(NT2):
                b = t % 2
                tensor.wait_ge(g["s_lx"], 16 * (t + 1))
                for m in range(32):
                    gm = 32 * t + m
                    if gm >= 2:
                        tensor.wait_ge(g["s_he"], gm - 1)
                    last = None
                    for k in range(8):
                        last = tensor.matmul(ps_h[:, m % 2, 0:256],
                                             wq1[:, k, 128 * m:128 * (m + 1)],
                                             xnt[:, b, k], start=(k == 0), stop=(k == 7))
                    last.then_inc(g["s_mmh"], 8)
                tensor.wait_ge(g["s_he"], 32 * (t + 1))
                for mo in range(8):
                    go = 8 * t + mo
                    if go >= 2:
                        tensor.wait_ge(g["s_oe"], go - 1)
                    last = None
                    for k in range(32):
                        last = tensor.matmul(ps_o[:, mo % 2, 0:256],
                                             wq2[:, k, 128 * mo:128 * (mo + 1)],
                                             hsb[:, k], start=(k == 0), stop=(k == 31))
                    last.then_inc(g["s_om"], 1)

        @block.scalar
        def _(scalar):
            if bar is not None:
                scalar.wait_ge(bar, ph)
            scalar.wait_ge(g["s_w"], 80)
            for w_i in range(2):
                scalar.wait_ge(g["s_p1m" if w_i == 0 else "s_p3m"], 256)
                scalar.copy(qs[:, w_i:w_i + 1], ps_q[:, 0:1]).then_inc(g["s_qs"], 1)
                scalar.wait_ge(g["s_tt"], w_i + 1)
                scalar.mul(gtmp[0:1, 0:1], ps_q[0:1, 1:2], 1.0 / NEL)
                scalar.add(gtmp[0:1, 0:1], gtmp[0:1, 0:1], epsc[:])
                scalar.activation(gtmp[0:1, 0:1], gtmp[0:1, 0:1], AF.Ln)
                scalar.activation(gtmp[0:1, 1:2], gtmp[0:1, 0:1], AF.Exp,
                                  scale=-1.0).then_inc(g["s_gl"], 1)
                scalar.wait_ge(g["s_bc"], w_i + 1)
                scalar.copy(gbc[:, w_i:w_i + 1], ps_q[:, 2:3]).then_inc(g["s_gb"], 1)
                for p in range(16):
                    scalar.wait_ge(g["s_c1"], 16 * w_i + p + 1)
                    scalar.copy(wpc[:, p % 2], wi32[:]).then_inc(g["s_c2"], 1)
            for t in range(NT2):
                for m in range(32):
                    scalar.wait_ge(g["s_mmh"], 256 * t + 8 * (m + 1))
                    scalar.add(htmp[:], ps_h[:, m % 2, 0:256], b1_sb[:, m:m + 1])
                    scalar.activation(hsb[:, m], htmp[:], AF.Relu).then_inc(g["s_he"], 1)
                for mo in range(8):
                    scalar.wait_ge(g["s_om"], 8 * t + mo + 1)
                    scalar.add(otmp[:], ps_o[:, mo % 2, 0:256], b2_sb[:, mo:mo + 1])
                    scalar.activation(g1b[:, mo], otmp[:], AF.Silu).then_inc(g["s_sl"], 1)

        @block.vector
        def _(vector):
            if bar is not None:
                vector.wait_ge(bar, ph)
            vector.wait_ge(g["s_w"], 80)
            vector.memset(z32q[:], 0.0)
            for w_i in range(2):
                l_sem = "s_l1" if w_i == 0 else "s_l3"
                a_sem = "s_ab1" if w_i == 0 else "s_ab3"
                for p in range(16):
                    vector.wait_ge(g[l_sem], 16 * (p + 1))
                    v = wpc[:, p % 2].unsqueeze(1)
                    vector.scalar_tensor_tensor(v, v, -1.0, v, ALU.mult,
                                                ALU.max).then_inc(g[a_sem], 1)
                vector.wait_ge(g["s_gl"], w_i + 1)
                vector.tensor_tensor(z32q[0:1, w_i:w_i + 1], gtmp[0:1, 1:2],
                                     gtmp[0:1, 1:2], ALU.max).then_inc(g["s_z"], 1)
                l2_sem = "s_l2" if w_i == 0 else "s_l4"
                r_sem = "s_r2" if w_i == 0 else "s_r4"
                def minmax(p2):
                    wbp = wpc[:, p2 % 2]
                    vector.tensor_scalar_min(wbp, wbp, 1.0)
                    if w_i == 0:
                        dst = wq1[:, :, 256 * p2:256 * (p2 + 1)]
                        src = wbp.rearrange("p (a f) -> p a f", f=256)
                    else:
                        dst = wq2[:, 2 * p2:2 * (p2 + 1), :]
                        src = wbp.rearrange("p (a f) -> p a f", f=1024)
                    vector.tensor_scalar_max(dst, src, -1.0).then_inc(g[r_sem], 1)

                for p in range(16):
                    vector.wait_ge(g[l2_sem], 16 * (p + 1))
                    if p == 0:
                        vector.wait_ge(g["s_gb"], w_i + 1)
                    wb = wpc[:, p % 2]
                    if p >= 1:
                        vector.wait_ge(g["s_c2"], 16 * w_i + p)
                    vector.tensor_scalar_mul(wb, wb, gbc[:, w_i:w_i + 1])
                    if p >= 1:
                        minmax(p - 1)
                    vector.tensor_copy(wi32[:], wb).then_inc(g["s_c1"], 1)
                vector.wait_ge(g["s_c2"], 16 * w_i + 16)
                minmax(15)
            for t in range(NT2):
                vector.wait_ge(g["s_lx1"], 32 * (t + 1))
                for mo in range(9):
                    if mo < 8:
                        vector.wait_ge(g["s_sl"], 8 * t + mo + 1)
                        vector.tensor_tensor(tmp2[:, mo], g1b[:, mo], s1t[:, mo], ALU.mult)
                    if mo >= 1:
                        vector.tensor_tensor(x1t[:, mo - 1], x1t[:, mo - 1], tmp2[:, mo - 1],
                                             ALU.add).then_inc(g["s_oe"], 1)

    return nc


_TALLY = [None]
_PATCHED = [False]


def _install_tally():
    if _PATCHED[0]:
        return
    nc2 = bass.Bass("TRN2", target_bir_lowering=False)
    captured = []
    with contextlib.ExitStack() as c2:
        t2 = c2.enter_context(nc2.sbuf_tensor("t2", [1, 4], F32))
        d2 = nc2.dram_tensor("d2", [1, 4], F32, kind="ExternalInput").ap()
        b2 = c2.enter_context(nc2.Block())

        @b2.vector
        def _(vector):
            captured.append(vector.memset(t2[:], 0.0))

        @b2.sync
        def _(sync):
            captured.append(sync.dma_start(out=t2[:], in_=d2))
    classes = {type(op) for op in captured}
    for cls in classes:
        orig = cls.then_inc

        def make(orig):
            def wrapper(self, sem, n):
                rec = _TALLY[0]
                if rec is not None:
                    tally, names = rec
                    nm = names.get(id(sem))
                    if nm is not None:
                        tally[nm] = tally.get(nm, 0) + n
                return orig(self, sem, n)
            return wrapper
        cls.then_inc = make(orig)
    _PATCHED[0] = True


FUSED_PHASES = None


def _fused_phases():
    return [
        (build_pre, PRE_SEMS, ("s_cast",)),
        (build_mamba_front, MF_SEMS, ("s_ld", "s_wg")),
        (build_ssd, SSD_SEMS, ("s_ldz", "s_wg")),
        (build_l0ffn, L0F_SEMS, ("s_ldg", "s_wg")),
        (build_l1front, L1F_SEMS, ("s_wg",)),
        (build_mlstm, ML_SEMS, ("s_ldg", "s_wg")),
        (build_qffn, QF_SEMS, ()),
    ]


class _NCProxy:
    def __init__(self, nc, prefix):
        self._nc = nc
        self._p = prefix

    def sbuf_tensor(self, name, *a, **k):
        return self._nc.sbuf_tensor(self._p + name, *a, **k)

    def psum_tensor(self, name, *a, **k):
        return self._nc.psum_tensor(self._p + name, *a, **k)

    def __getattr__(self, a):
        return getattr(self._nc, a)


def build_fused(debug_mids=False):
    _install_tally()
    nc = bass.Bass("TRN2", target_bir_lowering=False)
    io = {}

    def ext(name, shape, dt):
        io[name] = nc.dram_tensor(name, shape, dt, kind="ExternalInput").ap()

    def mid(name, shape, dt):
        kind = "ExternalOutput" if debug_mids else "Internal"
        io[name] = nc.dram_tensor(name, shape, dt, kind=kind).ap()

    ext("xT", [D, SEQ], F32)
    for name, shape in (("ones_col_bf", [128, 1]), ("ident128b", [128, 128]),
                        ("maskT", [128, 128]), ("mask8", [128, 8 * 128])):
        ext(name, shape, BF16)
    for name, shape in (("sel32", [32, 32 * 128]), ("ident128", [128, 128]),
                        ("ident32f", [32, 32]), ("ident64f", [64, 64]),
                        ("perm64", [64, 64]), ("c44", [64, 1]), ("c44n", [64, 1]),
                        ("c44_128", [128, 1]), ("ones_f", [128, 1]), ("eps1", [1, 1]),
                        ("in_proj_w", [D, 4384]), ("conv_wr", [128, 18, 4]),
                        ("conv_br", [128, 18]), ("dt_biasr", [32, 1]),
                        ("A_logr", [32, 1]), ("cmask32", [32, 512]),
                        ("out_proj_w", [2048, D]), ("norm_wr", [128, 16]),
                        ("D_paramr", [32, 1]),
                        ("WrT", [D, D]), ("WkT", [D, D]), ("WvT", [D, D]),
                        ("murr", [128, 8]), ("mukr", [128, 8]),
                        ("gmr", [128, 8]), ("btr", [128, 8]),
                        ("qwT", [D, D]), ("kwT", [D, D]), ("vwT", [D, D]),
                        ("iwT", [D, 64]), ("fwT", [D, 64]),
                        ("qbr", [128, 8]), ("kbr2", [128, 8]), ("vbr2", [128, 8]),
                        ("kbrow", [32, D]), ("vbrow", [32, D]),
                        ("ibr", [64, 1]), ("fbr", [64, 1]),
                        ("owT", [D, D]), ("obr", [128, 8]),
                        ("W1T", [D, 4096]), ("W2T", [4096, D]),
                        ("b1r", [128, 32]), ("b2r", [128, 8])):
        ext(name, shape, F32)
    for name, shape, dt in (("xn0T", [D, SEQ], BF16), ("szT", [2048, SEQ], BF16),
                            ("xcT", [2048, SEQ], BF16), ("BT", [128, SEQ], BF16),
                            ("CT", [128, SEQ], BF16), ("dtq", [32, SEQ], F32),
                            ("Acq", [32, SEQ], F32), ("s0T", [D, SEQ], BF16),
                            ("x1T", [D, SEQ], F32), ("xn1T", [D, SEQ], BF16),
                            ("qT", [D, SEQ], BF16), ("kT", [D, SEQ], BF16),
                            ("vT", [D, SEQ], BF16), ("ksm", [SEQ, D], BF16),
                            ("vsm", [SEQ, D], BF16), ("atil", [64, SEQ], F32),
                            ("btil", [64, SEQ], F32), ("decd", [64, SEQ // 128], F32),
                            ("s1T", [D, SEQ], BF16)):
        mid(name, shape, dt)
    io["x2T"] = nc.dram_tensor("x2T", [D, SEQ], F32, kind="ExternalOutput").ap()

    phases = _fused_phases()
    npool = max(len(sems) for _, sems, _ in phases)
    with contextlib.ExitStack() as ctx:
        ent = ctx.enter_context
        pool = [ent(nc.semaphore(f"pool{i}")) for i in range(npool)]
        bar = ent(nc.semaphore("bar"))
        done = {e: ent(nc.semaphore(f"done_{e}")) for e in
                ("sync", "gpsimd", "tensor", "scalar")}
        block = ent(nc.Block())
        for ph, (fn, names, swdge) in enumerate(phases):
            g = {}
            pi = 0
            for n in names:
                if n in swdge:
                    g[n] = ent(nc.semaphore(f"dg{ph}_{n}"))
                else:
                    g[n] = pool[pi]
                    pi += 1
            tally = {}
            _TALLY[0] = (tally, {id(g[n]): n for n in names})
            fn(nc=_NCProxy(nc, f"p{ph}_"), block=block, g=g, io=io, bar=bar, ph=ph)
            _TALLY[0] = None

            last = ph == len(phases) - 1

            @block.sync
            def _(sync):
                sync.sem_inc(done["sync"], 1)

            @block.gpsimd
            def _(gpsimd):
                gpsimd.sem_inc(done["gpsimd"], 1)

            @block.tensor
            def _(tensor):
                tensor.sem_inc(done["tensor"], 1)

            @block.scalar
            def _(scalar):
                scalar.sem_inc(done["scalar"], 1)

            @block.vector
            def _(vector, tally=dict(tally), g=dict(g), names=tuple(names),
                  swdge=tuple(swdge), last=last, ph=ph):
                for n in sorted(tally):
                    vector.wait_ge(g[n], tally[n])
                for e in ("sync", "gpsimd", "tensor", "scalar"):
                    vector.wait_ge(done[e], ph + 1)
                if not last:
                    for n in names:
                        if n not in swdge:
                            vector.sem_clear(g[n])
                    vector.sem_inc(bar, 1)
    return nc



def kernel(**inputs):
    consts = host_consts()
    AC = np.ascontiguousarray
    x = inputs["L0_attn_u"]

    cmask = np.ones((32, 512), np.float32)
    cmask[:, ::128] = 0.0

    ipw = AC(inputs["L0_attn_in_proj_w"].astype(np.float32))
    conv_wr = AC(inputs["L0_attn_conv_w"].reshape(18, 128, 4).transpose(1, 0, 2).astype(np.float32))
    conv_br = AC(inputs["L0_attn_conv_b"].reshape(18, 128).T.astype(np.float32))
    dt_biasr = AC(inputs["L0_attn_dt_bias"].reshape(32, 1).astype(np.float32))
    A_logr = AC(inputs["L0_attn_A_log"].reshape(32, 1).astype(np.float32))
    D_paramr = AC(inputs["L0_attn_D_param"].reshape(32, 1).astype(np.float32))
    norm_wr = AC(inputs["L0_attn_norm_w"].reshape(16, 128).T.astype(np.float32))
    opw = AC(inputs["L0_attn_out_proj_w"].astype(np.float32))
    WrT = AC(inputs["L0_ffn_W_r"].T.astype(np.float32))
    WkT = AC(inputs["L0_ffn_W_k"].T.astype(np.float32))
    WvT = AC(inputs["L0_ffn_W_v"].T.astype(np.float32))
    murr = AC(inputs["L0_ffn_mu_r"].reshape(8, 128).T.astype(np.float32))
    mukr = AC(inputs["L0_ffn_mu_k"].reshape(8, 128).T.astype(np.float32))
    gmr = AC(inputs["L1_attn_gamma"].reshape(8, 128).T.astype(np.float32))
    btr = AC(inputs["L1_attn_beta"].reshape(8, 128).T.astype(np.float32))
    qwT = AC(inputs["L1_attn_qw"].T.astype(np.float32))
    kwT = AC(inputs["L1_attn_kw"].T.astype(np.float32))
    vwT = AC(inputs["L1_attn_vw"].T.astype(np.float32))
    iwT = AC(inputs["L1_attn_iw"].T.astype(np.float32))
    fwT = AC(inputs["L1_attn_fw"].T.astype(np.float32))
    qbr = AC(inputs["L1_attn_qb"].reshape(8, 128).T.astype(np.float32))
    kbr2 = AC(inputs["L1_attn_kb"].reshape(8, 128).T.astype(np.float32))
    vbr2 = AC(inputs["L1_attn_vb"].reshape(8, 128).T.astype(np.float32))
    kbrow = np.zeros((32, D), np.float32)
    kbrow[0] = inputs["L1_attn_kb"]
    vbrow = np.zeros((32, D), np.float32)
    vbrow[0] = inputs["L1_attn_vb"]
    ibr = AC(inputs["L1_attn_ib"].reshape(64, 1).astype(np.float32))
    fbr = AC(inputs["L1_attn_fb"].reshape(64, 1).astype(np.float32))
    owT = AC(inputs["L1_attn_ow"].T.astype(np.float32))
    obr = AC(inputs["L1_attn_ob"].reshape(8, 128).T.astype(np.float32))
    W1T = AC(inputs["L1_ffn_W1"].T.astype(np.float32))
    W2T = AC(inputs["L1_ffn_W2"].T.astype(np.float32))
    b1r = AC(inputs["L1_ffn_b1"].reshape(32, 128).T.astype(np.float32))
    b2r = AC(inputs["L1_ffn_b2"].reshape(8, 128).T.astype(np.float32))

    cores = list(range(NCORE))
    xT = [AC(x[b].T) for b in cores]

    shared = dict(
        ones_col_bf=consts["ones_col_bf"], sel32=consts["sel32"],
        ident128=consts["ident128"], ident128b=consts["ident128b"],
        ident32f=consts["ident32f"], ident64f=consts["ident64f"],
        perm64=consts["perm64"], maskT=consts["maskT"], mask8=consts["mask8"],
        c44=consts["c44"], c44n=consts["c44n"], c44_128=consts["c44_128"],
        ones_f=consts["ones_f"], eps1=consts["eps1"],
        in_proj_w=ipw, conv_wr=conv_wr, conv_br=conv_br, dt_biasr=dt_biasr,
        A_logr=A_logr, cmask32=cmask, out_proj_w=opw, norm_wr=norm_wr,
        D_paramr=D_paramr, WrT=WrT, WkT=WkT, WvT=WvT, murr=murr, mukr=mukr,
        gmr=gmr, btr=btr, qwT=qwT, kwT=kwT, vwT=vwT, iwT=iwT, fwT=fwT,
        qbr=qbr, kbr2=kbr2, vbr2=vbr2, kbrow=kbrow, vbrow=vbrow, ibr=ibr,
        fbr=fbr, owT=owT, obr=obr, W1T=W1T, W2T=W2T, b1r=b1r, b2r=b2r,
    )
    nc = build_fused()
    res = run_bass_kernel_spmd(nc, [dict(xT=xT[b], **shared) for b in cores],
                               core_ids=cores)
    out = np.stack([np.asarray(res.results[b]["x2T"], np.float32).T for b in cores])
    return out


# version 06
# speedup vs baseline: 1.0249x; 1.0249x over previous
import contextlib
import numpy as np
import ml_dtypes
import concourse.bass as bass
import concourse.mybir as mybir
from concourse.bass_utils import run_bass_kernel_spmd

F32 = mybir.dt.float32
BF16 = mybir.dt.bfloat16
I32 = mybir.dt.int32
AF = mybir.ActivationFunctionType
ALU = mybir.AluOpType
AX = mybir.AxisListType
BF = ml_dtypes.bfloat16

SEQ = 8192
D = 1024
NCORE = 8
DBG_SSD = 8
DBG_ML = 0



def host_consts():
    c = {}
    c["ones_col_bf"] = np.ones((128, 1), BF)
    sel = np.zeros((32, 32 * 128), np.float32)
    for k in range(32):
        sel[k, 128 * k:128 * (k + 1)] = 1.0
    c["sel32"] = sel
    c["ident128"] = np.eye(128, dtype=np.float32)
    c["ident128b"] = np.eye(128, dtype=np.float32).astype(BF)
    c["ident32f"] = np.eye(32, dtype=np.float32)
    c["maskT"] = np.triu(np.ones((128, 128), np.float32)).astype(BF)
    c["mask8"] = np.ascontiguousarray(np.tile(np.triu(np.ones((128, 128), np.float32)), (1, 8))).astype(BF)
    c["ident64f"] = np.eye(64, dtype=np.float32)
    c["ones_f"] = np.ones((128, 1), np.float32)
    c["c44"] = np.full((64, 1), 44.0, np.float32)
    c["c44_128"] = np.full((128, 1), 44.0, np.float32)
    c["c44n"] = np.full((64, 1), -44.0, np.float32)
    c["eps1"] = np.full((1, 1), 1e-5, np.float32)
    p64 = np.zeros((64, 64), np.float32)
    for n in range(64):
        p64[4 * (n % 16) + n // 16, n] = 1.0
    c["perm64"] = p64
    return c



PRE_SEMS = ("s_ld", "s_cast", "s_sq", "s_sum", "s_bc", "s_v1", "s_sc",
            "s_row", "s_ap", "s_st")


def build_pre(nc=None, block=None, g=None, io=None, bar=None, ph=0):
    standalone = nc is None
    if standalone:
        nc = bass.Bass("TRN2", target_bir_lowering=False)
    def dram(name, shape, dt, kind):
        if io is not None:
            return io[name]
        return nc.dram_tensor(name, shape, dt, kind=kind).ap()
    xT = dram("xT", [D, SEQ], F32, "ExternalInput")
    ones_col = dram("ones_col_bf", [128, 1], BF16, "ExternalInput")
    sel32 = dram("sel32", [32, 32 * 128], F32, "ExternalInput")
    xn0T = dram("xn0T", [D, SEQ], BF16, "ExternalOutput")

    NT = SEQ // 512
    KT = D // 128

    xTr = xT.rearrange("(k p) n -> p k n", p=128)
    xnr = xn0T.rearrange("(k p) n -> p k n", p=128)

    with contextlib.ExitStack() as ctx:
        ent = ctx.enter_context
        xf = ent(nc.sbuf_tensor("xf", [128, 2, KT, 512], F32))
        xb = ent(nc.sbuf_tensor("xb", [128, 2, KT, 512], BF16))
        xsq = ent(nc.sbuf_tensor("xsq", [128, KT, 512], BF16))
        xn = ent(nc.sbuf_tensor("xn", [128, 2, KT, 512], BF16))
        xg = ent(nc.sbuf_tensor("xg", [128, 2, KT, 512], BF16))
        onesc = ent(nc.sbuf_tensor("onesc", [128, 1], BF16))
        selc = ent(nc.sbuf_tensor("selc", [32, 32 * 128], F32))
        rows = ent(nc.sbuf_tensor("rows", [32, 8 * 512], F32))
        sum_a = ent(nc.psum_tensor("sum_a", [1, 512], F32))
        sum_b = ent(nc.psum_tensor("sum_b", [1, 512], F32))
        bca = ent(nc.psum_tensor("bca", [128, 512], F32))
        bcb = ent(nc.psum_tensor("bcb", [128, 512], F32))
        if standalone:
            g = {}
            for name in PRE_SEMS:
                g[name] = ent(nc.semaphore(name))
            block = ent(nc.Block())
        (s_ld, s_cast, s_sq, s_sum, s_bc, s_v1, s_sc, s_row, s_ap,
         s_st) = (g[n] for n in PRE_SEMS)

        @block.sync
        def _(sync):
            if bar is not None:
                sync.wait_ge(bar, ph)
            sync.dma_start(out=onesc[:], in_=ones_col).then_inc(s_ld, 16)
            sync.dma_start(out=selc[:], in_=sel32).then_inc(s_ld, 16)
            for t in range(NT):
                b = t % 2
                if t >= 2:
                    sync.wait_ge(s_ap, t - 1)
                    sync.dma_start(out=xnr[:, :, 512 * (t - 2):512 * (t - 1)],
                                   in_=xn[:, t % 2]).then_inc(s_st, 16)
                sync.dma_start(out=xf[:, b], in_=xTr[:, :, 512 * t:512 * (t + 1)]).then_inc(s_ld, 16)
            for t in (NT - 2, NT - 1):
                sync.wait_ge(s_ap, t + 1)
                sync.dma_start(out=xnr[:, :, 512 * t:512 * (t + 1)], in_=xn[:, t % 2]).then_inc(s_st, 16)
            sync.wait_ge(s_st, 16 * NT)

        @block.gpsimd
        def _(gpsimd):
            if bar is not None:
                gpsimd.wait_ge(bar, ph)
            for t in range(NT):
                b = t % 2
                if t >= 2:
                    gpsimd.wait_ge(s_sum, t - 1)
                gpsimd.dma_start(out=xb[:, b], in_=xTr[:, :, 512 * t:512 * (t + 1)]).then_inc(s_cast, 16)

        @block.scalar
        def _(scalar):
            if bar is not None:
                scalar.wait_ge(bar, ph)
            for t in range(NT):
                b = t % 2
                scalar.wait_ge(s_ld, 32 + 16 * (t + 1))
                if t >= 1:
                    scalar.wait_ge(s_sum, t)
                last = None
                for k in range(KT):
                    last = scalar.activation(xsq[:, k], xf[:, b, k], AF.Square)
                last.then_inc(s_sq, 1)
                scalar.wait_ge(s_v1, t + 1)
                scalar.activation(rows[0:1, 6 * 512:7 * 512], rows[0:1, 5 * 512:6 * 512], AF.Ln)
                scalar.activation(rows[0:1, 0:512], rows[0:1, 6 * 512:7 * 512], AF.Exp,
                                  scale=-0.5).then_inc(s_sc, 1)

        @block.tensor
        def _(tensor):
            if bar is not None:
                tensor.wait_ge(bar, ph)
            tensor.wait_ge(s_ld, 32)
            for t in range(NT):
                b = t % 2
                tensor.wait_ge(s_cast, 16 * (t + 1))
                tensor.wait_ge(s_sq, t + 1)
                if t >= 1:
                    tensor.wait_ge(s_v1, t)
                for k in range(KT):
                    tensor.matmul(sum_a[:], onesc[:], xb[:, b, k], start=(k == 0), stop=(k == KT - 1))
                for k in range(KT):
                    last = tensor.matmul(sum_b[:], onesc[:], xsq[:, k], start=(k == 0), stop=(k == KT - 1))
                last.then_inc(s_sum, 1)
                tensor.wait_ge(s_row, t + 1)
                if t >= 1:
                    tensor.wait_ge(s_ap, t)
                tensor.matmul(bca[:], selc[:, 0:128], rows[:, 0:512], start=True, stop=True)
                tensor.matmul(bcb[:], selc[:, 0:128], rows[:, 512:1024], start=True, stop=True).then_inc(s_bc, 1)

        @block.vector
        def _(vector):
            if bar is not None:
                vector.wait_ge(bar, ph)
            vector.memset(rows[:], 0.0)
            for t in range(NT):
                b = t % 2
                vector.wait_ge(s_sum, t + 1)
                if t >= 1:
                    vector.wait_ge(s_bc, t)
                r = lambda j: rows[0:1, 512 * j:512 * (j + 1)]
                vector.tensor_scalar_mul(r(2), sum_a[:], 1.0 / D)
                vector.tensor_scalar_mul(r(3), sum_b[:], 1.0 / D)
                vector.tensor_tensor(r(4), r(2), r(2), ALU.mult)
                vector.tensor_tensor(r(5), r(3), r(4), ALU.subtract)
                vector.tensor_scalar_add(r(5), r(5), 1e-5).then_inc(s_v1, 1)
                vector.wait_ge(s_sc, t + 1)
                vector.tensor_tensor(r(7), r(2), r(0), ALU.mult)
                vector.tensor_scalar_mul(r(1), r(7), -1.0).then_inc(s_row, 1)
                vector.wait_ge(s_bc, t + 1)
                if t >= 2:
                    vector.wait_ge(s_st, 16 * (t - 1))
                last = None
                for k in range(KT):
                    vector.tensor_tensor(xn[:, b, k], xf[:, b, k], bca[:], ALU.mult)
                    last = vector.scalar_tensor_tensor(xn[:, b, k], xn[:, b, k], 1.0, bcb[:],
                                                       ALU.mult, ALU.add)
                last.then_inc(s_ap, 1)

    return nc



MF_SEMS = ("s_w", "s_wg", "s_ld", "s_mm", "s_evz", "s_evx", "s_conv", "s_silu",
           "s_dtv", "s_dts", "s_dtd", "s_st", "s_a")


def build_mamba_front(nc=None, block=None, g=None, io=None, bar=None, ph=0):
    standalone = nc is None
    if standalone:
        nc = bass.Bass("TRN2", target_bir_lowering=False)
    def dram(name, shape, dt, kind):
        if io is not None:
            return io[name]
        return nc.dram_tensor(name, shape, dt, kind=kind).ap()
    xn0T = dram("xn0T", [D, SEQ], BF16, "ExternalInput")
    in_proj_w = dram("in_proj_w", [D, 4384], F32, "ExternalInput")
    conv_w = dram("conv_wr", [128, 18, 4], F32, "ExternalInput")
    conv_b = dram("conv_br", [128, 18], F32, "ExternalInput")
    dt_bias = dram("dt_biasr", [32, 1], F32, "ExternalInput")
    A_log = dram("A_logr", [32, 1], F32, "ExternalInput")
    cmask = dram("cmask32", [32, 512], F32, "ExternalInput")

    szT = dram("szT", [2048, SEQ], BF16, "ExternalOutput")
    xcT = dram("xcT", [2048, SEQ], BF16, "ExternalOutput")
    BT = dram("BT", [128, SEQ], BF16, "ExternalOutput")
    CT = dram("CT", [128, SEQ], BF16, "ExternalOutput")
    dtq = dram("dtq", [32, SEQ], F32, "ExternalOutput")
    Acq = dram("Acq", [32, SEQ], F32, "ExternalOutput")

    NT = SEQ // 512
    KT = D // 128
    NB = 6
    MCH = 35

    xnr = xn0T.rearrange("(k p) n -> p k n", p=128)
    wr = in_proj_w.rearrange("(k p) m -> p k m", p=128)
    szr = szT.rearrange("(k p) n -> p k n", p=128)
    xcr = xcT.rearrange("(k p) n -> p k n", p=128)

    def prior_wait(eng, t, m):
        gi = t * MCH + m
        if gi < NB:
            return
        pg = gi - NB
        pt, pm = divmod(pg, MCH)
        if pm < 16:
            eng.wait_ge(S["s_evz"], 16 * pt + pm + 1)
        else:
            eng.wait_ge(S["s_evx"], 19 * pt + (pm - 16) + 1)

    S = {}
    with contextlib.ExitStack() as ctx:
        ent = ctx.enter_context
        wsb = ent(nc.sbuf_tensor("wsb", [128, KT, 4384], BF16))
        xn = ent(nc.sbuf_tensor("xn", [128, 2, KT, 512], BF16))
        xg = ent(nc.sbuf_tensor("xg", [128, 2, KT, 512], BF16))
        stage = ent(nc.sbuf_tensor("stage", [128, 18, 515], BF16))
        conv_y = ent(nc.sbuf_tensor("conv_y", [128, 18, 512], BF16))
        xbcs = ent(nc.sbuf_tensor("xbcs", [128, 18, 512], BF16))
        zs = ent(nc.sbuf_tensor("zs", [128, 16, 512], BF16))
        cwsb = ent(nc.sbuf_tensor("cwsb", [128, 18, 4], F32))
        cbsb = ent(nc.sbuf_tensor("cbsb", [128, 18], F32))
        dtb = ent(nc.sbuf_tensor("dtb", [32, 1], F32))
        alog = ent(nc.sbuf_tensor("alog", [32, 1], F32))
        acol = ent(nc.sbuf_tensor("acol", [32, 2], F32))
        cmsk = ent(nc.sbuf_tensor("cmsk", [32, 512], F32))
        dts = ent(nc.sbuf_tensor("dts", [32, 6, 512], F32))
        acum = ent(nc.sbuf_tensor("acum", [32, 2, 512], F32))
        psum = ent(nc.psum_tensor("psum", [128, NB, 512], F32))
        psdt = ent(nc.psum_tensor("psdt", [32, 512], F32))
        if standalone:
            for name in MF_SEMS:
                S[name] = ent(nc.semaphore(name))
            block = ent(nc.Block())
        else:
            S = g
        s_w, s_wg, s_ld, s_mm = S["s_w"], S["s_wg"], S["s_ld"], S["s_mm"]
        s_evz, s_evx, s_conv, s_silu = S["s_evz"], S["s_evx"], S["s_conv"], S["s_silu"]
        s_dtv, s_dts, s_dtd, s_st, s_a = S["s_dtv"], S["s_dts"], S["s_dtd"], S["s_st"], S["s_a"]

        @block.gpsimd
        def _(gpsimd):
            if bar is not None:
                gpsimd.wait_ge(bar, ph)
            gpsimd.dma_start(out=wsb[:], in_=wr).then_inc(s_wg, 16)
            for t in range(NT):
                b = t % 2
                if t >= 2:
                    gpsimd.wait_ge(s_mm, MCH * (t - 1))
                gpsimd.dma_start(out=xn[:, b], in_=xnr[:, :, 512 * t:512 * (t + 1)]).then_inc(s_ld, 16)

        @block.sync
        def _(sync):
            if bar is not None:
                sync.wait_ge(bar, ph)
            sync.dma_start(out=cwsb[:], in_=conv_w).then_inc(s_w, 16)
            sync.dma_start(out=cbsb[:], in_=conv_b).then_inc(s_w, 16)
            sync.dma_start(out=dtb[:], in_=dt_bias).then_inc(s_w, 16)
            sync.dma_start(out=alog[:], in_=A_log).then_inc(s_w, 16)
            sync.dma_start(out=cmsk[:], in_=cmask).then_inc(s_w, 16)
            for t in range(NT):
                sync.wait_ge(s_evz, 16 * (t + 1))
                sync.dma_start(out=szr[:, :, 512 * t:512 * (t + 1)], in_=zs[:]).then_inc(s_st, 16)
                sync.wait_ge(s_silu, t + 1)
                sync.dma_start(out=xcr[:, :, 512 * t:512 * (t + 1)], in_=xbcs[:, 0:16]).then_inc(s_st, 16)
                sync.dma_start(out=BT[:, 512 * t:512 * (t + 1)], in_=xbcs[:, 16]).then_inc(s_st, 16)
                sync.dma_start(out=CT[:, 512 * t:512 * (t + 1)], in_=xbcs[:, 17]).then_inc(s_st, 16)
                sync.wait_ge(s_dtd, t + 1)
                sync.dma_start(out=dtq[:, 512 * t:512 * (t + 1)], in_=dts[:, 5]).then_inc(s_st, 16)
                sync.dma_start(out=Acq[:, 512 * t:512 * (t + 1)], in_=acum[:, 1]).then_inc(s_st, 16)
            sync.wait_ge(s_st, 16 * 6 * NT)

        @block.tensor
        def _(tensor):
            if bar is not None:
                tensor.wait_ge(bar, ph)
            tensor.wait_ge(s_wg, 16)
            tensor.wait_ge(s_w, 80)
            for t in range(NT):
                b = t % 2
                tensor.wait_ge(s_ld, 16 * (t + 1))
                for m in range(MCH):
                    prior_wait(tensor, t, m)
                    if m == 34:
                        dst = psdt[:]
                        lo, hi = 4352, 4384
                    else:
                        dst = psum[:, m % NB]
                        lo, hi = 128 * m, 128 * (m + 1)
                    last = None
                    for k in range(KT):
                        last = tensor.matmul(dst, wsb[:, k, lo:hi], xn[:, b, k],
                                             start=(k == 0), stop=(k == KT - 1))
                    last.then_inc(s_mm, 1)

        @block.scalar
        def _(scalar):
            if bar is not None:
                scalar.wait_ge(bar, ph)
            scalar.wait_ge(s_w, 80)
            scalar.activation(acol[:, 0:1], alog[:], AF.Exp).then_inc(s_a, 1)
            for t in range(NT):
                if t >= 1:
                    scalar.wait_ge(s_st, 16 * 6 * (t - 1) + 16)
                for m in range(16):
                    scalar.wait_ge(s_mm, t * MCH + m + 1)
                    scalar.activation(zs[:, m], psum[:, m % NB], AF.Silu).then_inc(s_evz, 1)
                scalar.wait_ge(s_conv, t + 1)
                if t >= 1:
                    scalar.wait_ge(s_st, 16 * 6 * (t - 1) + 64)
                scalar.activation(xbcs[:], conv_y[:], AF.Silu).then_inc(s_silu, 1)
                scalar.wait_ge(s_dtv, 2 * t + 1)
                scalar.activation(dts[:, 3], dts[:, 2], AF.Exp, scale=-1.0).then_inc(s_dts, 1)
                scalar.wait_ge(s_dtv, 2 * t + 2)
                scalar.activation(dts[:, 4], dts[:, 3], AF.Ln).then_inc(s_dts, 1)

        @block.vector
        def _(vector):
            if bar is not None:
                vector.wait_ge(bar, ph)
            vector.memset(stage[:, :, 0:3], 0.0)
            vector.wait_ge(s_a, 1)
            vector.tensor_scalar_mul(acol[:, 1:2], acol[:, 0:1], -1.0)
            for t in range(NT):
                for m in range(16, 34):
                    vector.wait_ge(s_mm, t * MCH + m + 1)
                    vector.tensor_copy(stage[:, m - 16, 3:515], psum[:, m % NB]).then_inc(s_evx, 1)
                vector.wait_ge(s_silu, t)
                for j in range(18):
                    vector.tensor_scalar(conv_y[:, j], stage[:, j, 3:515], cwsb[:, j, 3:4],
                                         cbsb[:, j:j + 1], ALU.mult, ALU.add)
                    for i in (2, 1, 0):
                        r = vector.scalar_tensor_tensor(conv_y[:, j], stage[:, j, i:512 + i],
                                                        cwsb[:, j, i:i + 1], conv_y[:, j],
                                                        ALU.mult, ALU.add)
                    if j == 17:
                        r.then_inc(s_conv, 1)
                vector.tensor_copy(stage[:, :, 0:3], stage[:, :, 512:515])
                vector.wait_ge(s_mm, t * MCH + 35)
                if t >= 1:
                    vector.wait_ge(s_st, 16 * 6 * (t - 1) + 96)
                vector.tensor_scalar_add(dts[:, 0], psdt[:], dtb[:]).then_inc(s_evx, 1)
                vector.tensor_scalar_max(dts[:, 1], dts[:, 0], 0.0)
                vector.tensor_scalar_mul(dts[:, 2], dts[:, 0], -1.0)
                vector.tensor_tensor(dts[:, 2], dts[:, 2], dts[:, 0], ALU.max).then_inc(s_dtv, 1)
                vector.wait_ge(s_dts, 2 * t + 1)
                vector.tensor_scalar_add(dts[:, 3], dts[:, 3], 1.0).then_inc(s_dtv, 1)
                vector.wait_ge(s_dts, 2 * t + 2)
                vector.tensor_tensor(dts[:, 5], dts[:, 1], dts[:, 4], ALU.add)
                vector.tensor_scalar_mul(acum[:, 0], dts[:, 5], acol[:, 1:2])
                vector.tensor_tensor_scan(acum[:, 1], cmsk[:], acum[:, 0], 0.0,
                                          ALU.mult, ALU.add).then_inc(s_dtd, 1)

    return nc



SSD_SEMS = ("s_w", "s_wg", "s_wf", "s_pv", "s_pvv", "s_pa", "s_z", "s_dp1", "s_dp2", "s_dp3", "s_ld", "s_ldz", "s_tb", "s_tbe", "s_dtq", "s_dece1", "s_dmm", "s_dece3", "s_sc", "s_scm", "s_d0", "s_dcl", "s_de", "s_m", "s_xt", "s_xe", "s_xtl", "s_yd", "s_yo", "s_yoe", "s_ct", "s_up", "s_rbf", "s_xd", "s_ya", "s_yt", "s_ge", "s_gq", "s_s2", "s_r1", "s_r2", "s_rb", "s_rq", "s_opm", "s_ope", "s_st")


def build_ssd(nc=None, block=None, g=None, io=None, bar=None, ph=0):
    standalone = nc is None
    if standalone:
        nc = bass.Bass("TRN2", target_bir_lowering=False)
    def dram(name, shape, dt, kind):
        if io is not None:
            return io[name]
        return nc.dram_tensor(name, shape, dt, kind=kind).ap()
    xcT = dram("xcT", [2048, SEQ], BF16, "ExternalInput")
    szT = dram("szT", [2048, SEQ], BF16, "ExternalInput")
    BTd = dram("BT", [128, SEQ], BF16, "ExternalInput")
    CTd = dram("CT", [128, SEQ], BF16, "ExternalInput")
    dtq = dram("dtq", [32, SEQ], F32, "ExternalInput")
    Acq = dram("Acq", [32, SEQ], F32, "ExternalInput")
    opw = dram("out_proj_w", [2048, D], F32, "ExternalInput")
    nwr = dram("norm_wr", [128, 16], F32, "ExternalInput")
    Dpr = dram("D_paramr", [32, 1], F32, "ExternalInput")
    sel32 = dram("sel32", [32, 32 * 128], F32, "ExternalInput")
    ident = dram("ident128", [128, 128], F32, "ExternalInput")
    identb = dram("ident128b", [128, 128], BF16, "ExternalInput")
    ident32 = dram("ident32f", [32, 32], F32, "ExternalInput")
    maskTd = dram("maskT", [128, 128], BF16, "ExternalInput")
    onesd = dram("ones_col_bf", [128, 1], BF16, "ExternalInput")

    s0T = dram("s0T", [D, SEQ], BF16, "ExternalOutput")

    NC_ = SEQ // 128
    NG = SEQ // 512
    xcr = xcT.rearrange("(k p) n -> p k n", p=128)
    szr = szT.rearrange("(k p) n -> p k n", p=128)
    opr = opw.rearrange("(k p) m -> p k m", p=128)
    s0r = s0T.rearrange("(k p) n -> p k n", p=128)

    def batch_end(bb):
        c2, k2 = divmod(bb, 5)
        if k2 <= 2:
            return ("s_xe", 17 * c2 + (8, 16, 17)[k2])
        return ("s_ge", 16 * c2 + (8, 16)[k2 - 3])

    def xt_slot(c2, kind, i):
        if kind == "x":
            bb = 5 * c2 + i // 8
            pos = i % 8
        elif kind == "b":
            bb = 5 * c2 + 2
            pos = 0
        else:
            bb = 5 * c2 + 3 + i // 8
            pos = i % 8
        return bb, bb % 2, pos

    with contextlib.ExitStack() as ctx:
        ent = ctx.enter_context
        opwsb = ent(nc.sbuf_tensor("opwsb", [128, 16, D], BF16))
        nwsb = ent(nc.sbuf_tensor("nwsb", [128, 16], F32))
        dpsb = ent(nc.sbuf_tensor("dpsb", [32, 1], F32))
        dbc = ent(nc.sbuf_tensor("dbc", [128, 32], F32))
        selc = ent(nc.sbuf_tensor("selc", [32, 32 * 128], F32))
        idc = ent(nc.sbuf_tensor("idc", [128, 128], F32))
        idcb = ent(nc.sbuf_tensor("idcb", [128, 128], BF16))
        id32 = ent(nc.sbuf_tensor("id32", [32, 32], F32))
        mskc = ent(nc.sbuf_tensor("mskc", [128, 128], BF16))
        onesc = ent(nc.sbuf_tensor("onesc", [128, 1], BF16))
        acg = ent(nc.sbuf_tensor("acg", [32, 512], F32))
        dtg = ent(nc.sbuf_tensor("dtg", [32, 512], F32))
        lndtg = ent(nc.sbuf_tensor("lndtg", [32, 512], F32))
        a2g = ent(nc.sbuf_tensor("a2g", [32, 512], F32))
        e2g = ent(nc.sbuf_tensor("e2g", [32, 512], F32))
        nalg = ent(nc.sbuf_tensor("nalg", [32, 512], F32))
        sc1g = ent(nc.sbuf_tensor("sc1g", [32, 512], F32))
        eacg = ent(nc.sbuf_tensor("eacg", [32, 512], F32))
        z32 = ent(nc.sbuf_tensor("z32", [32, 32], F32))
        z32b = ent(nc.sbuf_tensor("z32b", [32, 128], F32))
        run = ent(nc.sbuf_tensor("run", [128, 2048], F32))
        run_bf = ent(nc.sbuf_tensor("run_bf", [128, 2048], BF16))
        xc_sb = ent(nc.sbuf_tensor("xc_sb", [128, 16, 512], BF16))
        sz_sb = ent(nc.sbuf_tensor("sz_sb", [128, 16, 128], BF16))
        bt_sb = ent(nc.sbuf_tensor("bt_sb", [128, 512], BF16))
        ct_sb = ent(nc.sbuf_tensor("ct_sb", [128, 512], BF16))
        tbs = ent(nc.sbuf_tensor("tbs", [128, 4, 128], F32))
        decq = ent(nc.sbuf_tensor("decq", [128, 128], F32))
        scMq = ent(nc.sbuf_tensor("scMq", [128, 512], BF16))
        bsmq = ent(nc.sbuf_tensor("bsmq", [128, 4, 128], BF16))
        msb = ent(nc.sbuf_tensor("msb", [128, 32, 512], BF16))
        expd = ent(nc.sbuf_tensor("expd", [128, 2, 512], BF16))
        xsm = ent(nc.sbuf_tensor("xsm", [128, 2048], BF16))
        xtl = ent(nc.sbuf_tensor("xtl", [128, 2048], BF16))
        yo_sb = ent(nc.sbuf_tensor("yo_sb", [128, 512], F32))
        xd_sb = ent(nc.sbuf_tensor("xd_sb", [128, 4, 512], BF16))
        y_sb = ent(nc.sbuf_tensor("y_sb", [128, 2048], BF16))
        g_cm = ent(nc.sbuf_tensor("g_cm", [128, 16, 512], BF16))
        gsq = ent(nc.sbuf_tensor("gsq", [128, 16, 128], BF16))
        rows = ent(nc.sbuf_tensor("rows", [32, 2 * 128], F32))
        rstdq = ent(nc.sbuf_tensor("rstdq", [128, 512], F32))
        s0_sb = ent(nc.sbuf_tensor("s0_sb", [128, 8, 512], BF16))
        ps_m = ent(nc.psum_tensor("ps_m", [128, 512], F32))
        ps_s = ent(nc.psum_tensor("ps_s", [128, 512], F32))
        ps_xt = ent(nc.psum_tensor("ps_xt", [128, 2, 8, 128], BF16))
        ps_d0 = ent(nc.psum_tensor("ps_d0", [128, 512], F32))
        ps_yd = ent(nc.psum_tensor("ps_yd", [128, 512], F32))
        ps_yo = ent(nc.psum_tensor("ps_yo", [128, 512], F32))
        ps_op = ent(nc.psum_tensor("ps_op", [128, 512], F32))

        if standalone:
            g = {}
            for name in SSD_SEMS:
                g[name] = ent(nc.semaphore(name))
            block = ent(nc.Block())

        @block.sync
        def _(sync):
            if bar is not None:
                sync.wait_ge(bar, ph)
            for ap_out, ap_in in ((selc[:], sel32), (idc[:], ident), (idcb[:], identb),
                                  (id32[:], ident32), (mskc[:], maskTd), (onesc[:], onesd),
                                  (nwsb[:], nwr), (dpsb[:], Dpr)):
                sync.dma_start(out=ap_out, in_=ap_in).then_inc(g["s_w"], 16)
            for gi in range(NG):
                if gi >= 1:
                    if DBG_SSD >= 8:
                        sync.wait_ge(g["s_opm"], 8 * gi)
                    elif DBG_SSD >= 5:
                        sync.wait_ge(g["s_xt"], 68 * gi)
                    if DBG_SSD >= 7:
                        sync.wait_ge(g["s_ge"], 64 * gi)
                sync.dma_start(out=xc_sb[:], in_=xcr[:, :, 512 * gi:512 * (gi + 1)]).then_inc(g["s_ld"], 16)
                sync.dma_start(out=bt_sb[:], in_=BTd[:, 512 * gi:512 * (gi + 1)]).then_inc(g["s_ld"], 16)
                sync.dma_start(out=ct_sb[:], in_=CTd[:, 512 * gi:512 * (gi + 1)]).then_inc(g["s_ld"], 16)
                if gi >= 1:
                    sync.wait_ge(g["s_pa"], 3 * gi)
                    if DBG_SSD >= 4:
                        sync.wait_ge(g["s_d0"], 32 * gi)
                sync.dma_start(out=dtg[:], in_=dtq[:, 512 * gi:512 * (gi + 1)]).then_inc(g["s_ld"], 16)
                sync.dma_start(out=acg[:], in_=Acq[:, 512 * gi:512 * (gi + 1)]).then_inc(g["s_ld"], 16)
                if DBG_SSD >= 8 and gi >= 1:
                    sync.wait_ge(g["s_ope"], 8 * gi)
                    sync.dma_start(out=s0r[:, :, 512 * (gi - 1):512 * gi],
                                   in_=s0_sb[:]).then_inc(g["s_st"], 16)
            if DBG_SSD >= 8:
                sync.wait_ge(g["s_ope"], 8 * NG)
                sync.dma_start(out=s0r[:, :, 512 * (NG - 1):512 * NG], in_=s0_sb[:]).then_inc(g["s_st"], 16)
                sync.wait_ge(g["s_st"], 16 * NG)
            else:
                if DBG_SSD == 1:
                    sync.wait_ge(g["s_pa"], 3 * NG)
                elif DBG_SSD == 2:
                    sync.wait_ge(g["s_dece3"], NG)
                    sync.wait_ge(g["s_tbe"], NC_)
                elif DBG_SSD == 3:
                    sync.wait_ge(g["s_scm"], NC_)
                elif DBG_SSD == 4:
                    sync.wait_ge(g["s_m"], 32 * NG)
                elif DBG_SSD == 5:
                    sync.wait_ge(g["s_xtl"], 16 * NC_)
                    sync.wait_ge(g["s_xd"], 4 * NC_)
                elif DBG_SSD == 6:
                    sync.wait_ge(g["s_rbf"], NC_ + 1)
                elif DBG_SSD in (6.4, 6.45, 6.5, 7):
                    sync.wait_ge(g["s_ge"], 16 * NC_)
                elif DBG_SSD == 7.1:
                    sync.wait_ge(g["s_s2"], NC_)
                    sync.wait_ge(g["s_r1"], NC_)
                elif DBG_SSD == 7.2:
                    sync.wait_ge(g["s_rq"], NC_)

        @block.gpsimd
        def _(gpsimd):
            if bar is not None:
                gpsimd.wait_ge(bar, ph)
            gpsimd.dma_start(out=opwsb[:], in_=opr).then_inc(g["s_wg"], 16)
            gpsimd.wait_ge(g["s_w"], 129)
            for c in range(NC_):
                if DBG_SSD < 7:
                    break
                if c >= 1:
                    gpsimd.wait_ge(g["s_ge"], 16 * c)
                gpsimd.dma_start(out=sz_sb[:], in_=szr[:, :, 128 * c:128 * (c + 1)]).then_inc(g["s_ldz"], 16)
                if DBG_SSD < 7.05:
                    pass

        @block.tensor
        def _(tensor):
            if bar is not None:
                tensor.wait_ge(bar, ph)
            tensor.wait_ge(g["s_w"], 128)
            tensor.wait_ge(g["s_wf"], 1)
            tensor.wait_ge(g["s_z"], 1)
            tensor.transpose(ps_m[0:1, 384:416], dpsb[:], id32[:]).then_inc(g["s_dp1"], 1)
            tensor.wait_ge(g["s_dp2"], 1)
            tensor.matmul(ps_m[:, 416:448], selc[:, 0:128], z32[:, 0:32],
                          start=True, stop=True).then_inc(g["s_dp3"], 1)
            tensor.wait_ge(g["s_w"], 129)
            for gi in range(NG):
                tensor.wait_ge(g["s_ld"], 80 * (gi + 1))
                tensor.wait_ge(g["s_pa"], 3 * (gi + 1))
                for j in range(4):
                    if DBG_SSD < 2:
                        break
                    c = 4 * gi + j
                    if c >= 1:
                        tensor.wait_ge(g["s_tbe"], c)
                        tensor.wait_ge(g["s_scm"], c)
                    if j == 0 and gi >= 1:
                        tensor.wait_ge(g["s_dece1"], gi)
                    tensor.transpose(ps_m[:, 0:32], nalg[:, 128 * j:128 * (j + 1)], id32[:])
                    tensor.transpose(ps_m[:, 32:64], sc1g[:, 128 * j:128 * (j + 1)], id32[:])
                    tensor.transpose(ps_m[:, 64:96], eacg[:, 128 * j:128 * (j + 1)],
                                     id32[:]).then_inc(g["s_tb"], 1)
                    if DBG_SSD >= 3:
                        tensor.matmul(ps_s[:, 0:128], bt_sb[:, 128 * j:128 * (j + 1)],
                                      ct_sb[:, 128 * j:128 * (j + 1)], start=True,
                                      stop=True).then_inc(g["s_sc"], 1)
                if DBG_SSD < 2:
                    continue
                if gi >= 1:
                    tensor.wait_ge(g["s_dece3"], gi)
                    if DBG_SSD >= 6:
                        tensor.wait_ge(g["s_up"], 16 * gi)
                    if DBG_SSD >= 4:
                        tensor.wait_ge(g["s_de"], 32 * gi)
                    if DBG_SSD >= 7.1:
                        tensor.wait_ge(g["s_r1"], 4 * gi)
                    if DBG_SSD >= 7.2:
                        tensor.wait_ge(g["s_rq"], 4 * gi)
                tensor.wait_ge(g["s_tbe"], 4 * (gi + 1))
                tensor.transpose(ps_m[0:4, 96:128],
                                 eacg.rearrange("p (a b) -> p a b", b=128)[:, :, 127],
                                 id32[:]).then_inc(g["s_dtq"], 1)
                tensor.wait_ge(g["s_dece1"], gi + 1)
                last = None
                for r in range(4):
                    last = tensor.matmul(ps_d0[:, 32 * r:32 * (r + 1)], selc[:, 128 * r:128 * (r + 1)],
                                         z32[:, 0:32], start=True, stop=True)
                last.then_inc(g["s_dmm"], 1)
                tensor.wait_ge(g["s_tbe"], 4 * (gi + 1))
                tensor.wait_ge(g["s_dece3"], gi + 1)
                for h in range(32):
                    if DBG_SSD < 4:
                        break
                    if 32 * gi + h >= 1:
                        tensor.wait_ge(g["s_de"], 32 * gi + h)
                    tensor.matmul(ps_d0[:], selc[:, 128 * h:128 * (h + 1)],
                                  acg[:], start=True, stop=False)
                    tensor.matmul(ps_d0[:], idc[:],
                                  tbs[:, :, h].unsqueeze(2).broadcast_to([128, 4, 128]),
                                  start=False, stop=True).then_inc(g["s_d0"], 1)
                for j in range(4):
                    if DBG_SSD < 5:
                        break
                    c = 4 * gi + j
                    for i in range(16):
                        bb, bank, pos = xt_slot(c, "x", i)
                        if pos == 0 and bb >= 2:
                            if DBG_SSD >= 6.4:
                                if c >= 1 and i == 0:
                                    tensor.wait_ge(g["s_ge"], 16 * c)
                                w = batch_end(bb - 2)
                                tensor.wait_ge(g[w[0]], w[1])
                            elif c >= 1:
                                tensor.wait_ge(g["s_xe"], 17 * (c - 1) + (16 if i < 8 else 17))
                        tensor.transpose(ps_xt[:, bank, pos], xc_sb[:, i, 128 * j:128 * (j + 1)],
                                         idcb[:]).then_inc(g["s_xt"], 1)
                    bb, bank, pos = xt_slot(c, "b", 0)
                    if bb >= 2:
                        if DBG_SSD >= 7:
                            w = batch_end(bb - 2)
                            tensor.wait_ge(g[w[0]], w[1])
                        else:
                            tensor.wait_ge(g["s_xe"], 17 * c + 8)
                    tensor.transpose(ps_xt[:, bank, pos], bt_sb[:, 128 * j:128 * (j + 1)],
                                     idcb[:]).then_inc(g["s_xt"], 1)
                    if DBG_SSD < 6:
                        continue
                    tensor.wait_ge(g["s_m"], 32 * (gi + 1))
                    for q in range(4):
                        tensor.wait_ge(g["s_ya"], 4 * c + q)
                        tensor.wait_ge(g["s_xe"], 17 * c + 4 * (q + 1))
                        last = None
                        for hh in range(8 * q, 8 * q + 8):
                            last = tensor.matmul(ps_yd[:, 64 * (hh - 8 * q):64 * (hh - 8 * q + 1)],
                                                 msb[:, hh, 128 * j:128 * (j + 1)],
                                                 xsm[:, 64 * hh:64 * (hh + 1)],
                                                 start=True, stop=True)
                        last.then_inc(g["s_yd"], 1)
                        tensor.wait_ge(g["s_rbf"], c + 1)
                        tensor.wait_ge(g["s_up"], 4 * c + q)
                        tensor.matmul(ps_yo[:], ct_sb[:, 128 * j:128 * (j + 1)],
                                      run_bf[:, 512 * q:512 * (q + 1)],
                                      start=True, stop=True).then_inc(g["s_yo"], 1)
                        tensor.wait_ge(g["s_yoe"], 4 * c + q + 1)
                        tensor.wait_ge(g["s_xtl"], 16 * c + 4 * (q + 1))
                        tensor.wait_ge(g["s_xe"], 17 * (c + 1))
                        tensor.matmul(ps_yo[:], bsmq[:, j], xtl[:, 512 * q:512 * (q + 1)],
                                      start=True, stop=True).then_inc(g["s_ct"], 1)
                    if DBG_SSD < 6.4:
                        continue
                    tensor.wait_ge(g["s_ya"], 4 * (c + 1))
                    for i in range(16):
                        bb, bank, pos = xt_slot(c, "y", i)
                        if pos == 0:
                            w = batch_end(bb - 2)
                            tensor.wait_ge(g[w[0]], w[1])
                        tensor.transpose(ps_xt[:, bank, pos], y_sb[:, 128 * i:128 * (i + 1)],
                                         idcb[:]).then_inc(g["s_yt"], 1)
                    if DBG_SSD < 7.1:
                        continue
                    tensor.wait_ge(g["s_gq"], 16 * (c + 1))
                    if c >= 1:
                        tensor.wait_ge(g["s_r1"], c)
                        if DBG_SSD >= 7.2:
                            tensor.wait_ge(g["s_rq"], c)
                    last = None
                    for i in range(16):
                        last = tensor.matmul(ps_d0[0:1, 0:128], onesc[:], gsq[:, i],
                                             start=(i == 0), stop=(i == 15))
                    last.then_inc(g["s_s2"], 1)
                    if DBG_SSD >= 7.2:
                        tensor.wait_ge(g["s_r2"], c + 1)
                        if c >= 1:
                            tensor.wait_ge(g["s_rq"], c)
                        tensor.matmul(ps_d0[:, 128:256], selc[:, 0:128], z32b[:, 0:128],
                                      start=True, stop=True).then_inc(g["s_rb"], 1)
                if DBG_SSD < 8:
                    continue
                tensor.wait_ge(g["s_ge"], 64 * (gi + 1))
                tensor.wait_ge(g["s_rq"], 4 * (gi + 1))
                for m in range(8):
                    if 8 * gi + m >= 1:
                        tensor.wait_ge(g["s_ope"], 8 * gi + m)
                    last = None
                    for kt in range(16):
                        last = tensor.matmul(ps_op[:], opwsb[:, kt, 128 * m:128 * (m + 1)],
                                             g_cm[:, kt, :], start=(kt == 0), stop=(kt == 15))
                    last.then_inc(g["s_opm"], 1)

        @block.scalar
        def _(scalar):
            if bar is not None:
                scalar.wait_ge(bar, ph)
            scalar.wait_ge(g["s_w"], 128)
            scalar.wait_ge(g["s_dp1"], 1)
            scalar.copy(z32[0:1, 0:32], ps_m[0:1, 384:416]).then_inc(g["s_dp2"], 1)
            scalar.wait_ge(g["s_dp3"], 1)
            scalar.copy(dbc[:], ps_m[:, 416:448]).then_inc(g["s_w"], 1)
            for gi in range(NG):
                scalar.wait_ge(g["s_ld"], 80 * (gi + 1))
                if gi >= 1:
                    scalar.wait_ge(g["s_pa"], 3 * gi)
                scalar.activation(lndtg[:], dtg[:], AF.Ln).then_inc(g["s_pv"], 1)
                scalar.wait_ge(g["s_pvv"], gi + 1)
                scalar.activation(e2g[:], a2g[:], AF.Exp).then_inc(g["s_pv"], 1)
                scalar.activation(eacg[:], acg[:], AF.Exp).then_inc(g["s_pa"], 1)
                for j in range(4):
                    if DBG_SSD < 2:
                        break
                    c = 4 * gi + j
                    scalar.wait_ge(g["s_tb"], c + 1)
                    if gi >= 1 and j == 0:
                        if DBG_SSD >= 6:
                            scalar.wait_ge(g["s_up"], 16 * gi)
                        if DBG_SSD >= 4:
                            scalar.wait_ge(g["s_d0"], 32 * gi)
                    scalar.copy(tbs[:, j, 0:96], ps_m[:, 0:96]).then_inc(g["s_tbe"], 1)
                if DBG_SSD < 2:
                    continue
                scalar.wait_ge(g["s_dtq"], gi + 1)
                if gi >= 1:
                    scalar.wait_ge(g["s_dmm"], gi)
                scalar.copy(z32[0:4, 0:32], ps_m[0:4, 96:128]).then_inc(g["s_dece1"], 1)
                scalar.wait_ge(g["s_dmm"], gi + 1)
                scalar.copy(decq[:], ps_d0[:, 0:128]).then_inc(g["s_dece3"], 1)
                for h in range(32):
                    if DBG_SSD < 4:
                        break
                    scalar.wait_ge(g["s_dcl"], 32 * gi + h + 1)
                    if 32 * gi + h >= 2:
                        scalar.wait_ge(g["s_m"], 32 * gi + h - 1)
                    scalar.activation(expd[:, h % 2], ps_d0[:], AF.Exp).then_inc(g["s_de"], 1)
                for j in range(4):
                    if DBG_SSD < 7:
                        break
                    c = 4 * gi + j
                    if DBG_SSD < 7.2:
                        continue
                    scalar.wait_ge(g["s_r1"], c + 1)
                    scalar.wait_ge(g["s_z"], 2)
                    if c >= 1:
                        scalar.wait_ge(g["s_rb"], c)
                    scalar.activation(rows[0:1, 128:256], rows[0:1, 0:128], AF.Ln)
                    scalar.activation(z32b[0:1, 0:128], rows[0:1, 128:256], AF.Exp,
                                      scale=-0.5).then_inc(g["s_r2"], 1)
                    scalar.wait_ge(g["s_rb"], c + 1)
                    if gi >= 1 and j == 0:
                        scalar.wait_ge(g["s_ope"], 8 * gi)
                    scalar.copy(rstdq[:, 128 * j:128 * (j + 1)],
                                ps_d0[:, 128:256]).then_inc(g["s_rq"], 1)

        @block.vector
        def _(vector):
            if bar is not None:
                vector.wait_ge(bar, ph)
            vector.wait_ge(g["s_w"], 128)
            vector.wait_ge(g["s_wg"], 16)
            for kt in range(16):
                r = vector.tensor_scalar_mul(opwsb[:, kt, :], opwsb[:, kt, :], nwsb[:, kt:kt + 1])
            r.then_inc(g["s_wf"], 1)
            vector.memset(z32[:], 0.0).then_inc(g["s_z"], 1)
            vector.memset(z32b[:], 0.0).then_inc(g["s_z"], 1)
            vector.memset(run[:], 0.0)
            vector.memset(run_bf[:], 0.0).then_inc(g["s_rbf"], 1)
            for gi in range(NG):
                vector.wait_ge(g["s_ld"], 80 * (gi + 1))
                if gi >= 1:
                    vector.wait_ge(g["s_pa"], 3 * gi)
                sl_ac = acg[:]
                ce = acg[:].rearrange("p (a b) -> p a b", b=128)[:, :, 127]
                vector.tensor_tensor(a2g[:].rearrange("p (a b) -> p a b", a=4),
                                     ce.unsqueeze(2).broadcast_to([32, 4, 128]),
                                     sl_ac.rearrange("p (a b) -> p a b", a=4),
                                     ALU.subtract).then_inc(g["s_pvv"], 1)
                vector.wait_ge(g["s_pv"], 2 * gi + 1)
                vector.tensor_tensor(nalg[:], lndtg[:], sl_ac, ALU.subtract).then_inc(g["s_pa"], 1)
                vector.wait_ge(g["s_pv"], 2 * gi + 2)
                vector.tensor_tensor(sc1g[:], dtg[:], e2g[:], ALU.mult).then_inc(g["s_pa"], 1)
                for j in range(4):
                    if DBG_SSD < 3:
                        break
                    c = 4 * gi + j
                    vector.wait_ge(g["s_sc"], c + 1)
                    if gi >= 1 and j == 0 and DBG_SSD >= 4:
                        vector.wait_ge(g["s_m"], 32 * gi)
                    vector.tensor_tensor(scMq[:, 128 * j:128 * (j + 1)], ps_s[:, 0:128],
                                         mskc[:], ALU.mult).then_inc(g["s_scm"], 1)
                for h in range(32):
                    if DBG_SSD < 4:
                        break
                    vector.wait_ge(g["s_d0"], 32 * gi + h + 1)
                    vector.tensor_scalar_min(ps_d0[:], ps_d0[:],
                                             20.0).then_inc(g["s_dcl"], 1)
                    vector.wait_ge(g["s_de"], 32 * gi + h + 1)
                    vector.wait_ge(g["s_scm"], 4 * (gi + 1))
                    vector.tensor_tensor(msb[:, h], expd[:, h % 2], scMq[:],
                                         ALU.mult).then_inc(g["s_m"], 1)
                for j in range(4):
                    if DBG_SSD < 5:
                        break
                    c = 4 * gi + j
                    for i in range(16):
                        vector.wait_ge(g["s_xt"], 17 * c + 8 * (i // 8 + 1))
                        if c >= 1 and DBG_SSD >= 6:
                            vector.wait_ge(g["s_yd"], 4 * (c - 1) + i // 4 + 1)
                        bb, bank, pos = xt_slot(c, "x", i)
                        vector.tensor_copy(xsm[:, 128 * i:128 * (i + 1)],
                                           ps_xt[:, bank, pos]).then_inc(g["s_xe"], 1)
                        if c >= 1 and DBG_SSD >= 6:
                            vector.wait_ge(g["s_ct"], 4 * (c - 1) + i // 4 + 1)
                        vector.tensor_tensor(
                            xtl[:, 128 * i:128 * (i + 1)].rearrange("p (a b) -> p a b", a=2),
                            xsm[:, 128 * i:128 * (i + 1)].rearrange("p (a b) -> p a b", a=2),
                            tbs[:, j, 32 + 2 * i:32 + 2 * i + 2].unsqueeze(2).broadcast_to([128, 2, 64]),
                            ALU.mult).then_inc(g["s_xtl"], 1)
                    vector.wait_ge(g["s_xt"], 17 * c + 17)
                    if c >= 4 and DBG_SSD >= 6:
                        vector.wait_ge(g["s_ct"], 4 * (c - 4) + 4)
                    bb, bank, pos = xt_slot(c, "b", 0)
                    vector.tensor_copy(bsmq[:, j], ps_xt[:, bank, pos]).then_inc(g["s_xe"], 1)
                    for q in range(4):
                        vector.tensor_tensor(
                            xd_sb[:, q].rearrange("p (a b) -> p a b", a=8),
                            xsm[:, 512 * q:512 * (q + 1)].rearrange("p (a b) -> p a b", a=8),
                            dbc[:, 8 * q:8 * (q + 1)].unsqueeze(2).broadcast_to([128, 8, 64]),
                            ALU.mult).then_inc(g["s_xd"], 1)
                    for q in range(4):
                        if DBG_SSD < 6:
                            break
                        vector.wait_ge(g["s_yo"], 4 * c + q + 1)
                        vector.tensor_tensor(
                            yo_sb[:].rearrange("p (a b) -> p a b", a=8),
                            ps_yo[:].rearrange("p (a b) -> p a b", a=8),
                            tbs[:, j, 64 + 8 * q:64 + 8 * q + 8].unsqueeze(2).broadcast_to([128, 8, 64]),
                            ALU.mult).then_inc(g["s_yoe"], 1)
                        vector.wait_ge(g["s_yd"], 4 * c + q + 1)
                        vector.wait_ge(g["s_xd"], 4 * c + q + 1)
                        vector.tensor_tensor(y_sb[:, 512 * q:512 * (q + 1)], ps_yd[:], yo_sb[:], ALU.add)
                        vector.tensor_tensor(y_sb[:, 512 * q:512 * (q + 1)], y_sb[:, 512 * q:512 * (q + 1)],
                                             xd_sb[:, q], ALU.add).then_inc(g["s_ya"], 1)
                        vector.wait_ge(g["s_ct"], 4 * c + q + 1)
                        vector.tensor_tensor(
                            run[:, 512 * q:512 * (q + 1)].rearrange("p (a b) -> p a b", a=8),
                            run[:, 512 * q:512 * (q + 1)].rearrange("p (a b) -> p a b", a=8),
                            decq[:, 32 * j + 8 * q:32 * j + 8 * (q + 1)].unsqueeze(2).broadcast_to([128, 8, 64]),
                            ALU.mult)
                        vector.tensor_tensor(run[:, 512 * q:512 * (q + 1)], run[:, 512 * q:512 * (q + 1)],
                                             ps_yo[:], ALU.add).then_inc(g["s_up"], 1)
                    if DBG_SSD >= 6:
                        vector.tensor_copy(run_bf[:], run[:]).then_inc(g["s_rbf"], 1)
                    if DBG_SSD < 6.4:
                        continue
                    if DBG_SSD >= 7:
                        vector.wait_ge(g["s_ldz"], 16 * (c + 1))
                    for i in range(16):
                        vector.wait_ge(g["s_yt"], 16 * c + 8 * (i // 8 + 1))
                        if gi >= 1 and j == 0 and i == 0 and DBG_SSD >= 8:
                            vector.wait_ge(g["s_opm"], 8 * gi)
                        bb, bank, pos = xt_slot(c, "y", i)
                        if DBG_SSD >= 7:
                            vector.tensor_tensor(g_cm[:, i, 128 * j:128 * (j + 1)],
                                                 ps_xt[:, bank, pos],
                                                 sz_sb[:, i, :],
                                                 ALU.mult).then_inc(g["s_ge"], 1)
                        elif DBG_SSD >= 6.5:
                            vector.tensor_copy(g_cm[:, i, 128 * j:128 * (j + 1)],
                                               ps_xt[:, bank, pos]).then_inc(g["s_ge"], 1)
                        elif DBG_SSD >= 6.45:
                            bbx, bankx, posx = xt_slot(c, "x", i)
                            vector.tensor_copy(g_cm[:, i, 128 * j:128 * (j + 1)],
                                               ps_xt[:, bankx, posx]).then_inc(g["s_ge"], 1)
                        else:
                            vector.tensor_copy(g_cm[:, i, 128 * j:128 * (j + 1)],
                                               mskc[:]).then_inc(g["s_ge"], 1)
                    if DBG_SSD < 7.1:
                        continue
                    for i in range(16):
                        vector.wait_ge(g["s_yt"], 16 * c + 8 * (i // 8 + 1))
                        if c >= 1 and i == 0:
                            vector.wait_ge(g["s_s2"], c)
                        vector.tensor_tensor(gsq[:, i], g_cm[:, i, 128 * j:128 * (j + 1)],
                                             g_cm[:, i, 128 * j:128 * (j + 1)],
                                             ALU.mult).then_inc(g["s_gq"], 1)
                    vector.wait_ge(g["s_s2"], c + 1)
                    if c >= 1 and DBG_SSD >= 7.2:
                        vector.wait_ge(g["s_r2"], c)
                    vector.tensor_scalar(rows[0:1, 0:128], ps_d0[0:1, 0:128],
                                         1.0 / 2048.0, 1e-5, ALU.mult, ALU.add).then_inc(g["s_r1"], 1)
                for m in range(8):
                    if DBG_SSD < 8:
                        break
                    vector.wait_ge(g["s_opm"], 8 * gi + m + 1)
                    vector.wait_ge(g["s_rq"], 4 * (gi + 1))
                    if m == 0:
                        vector.wait_ge(g["s_st"], 16 * gi)
                    vector.tensor_tensor(s0_sb[:, m], ps_op[:], rstdq[:],
                                         ALU.mult).then_inc(g["s_ope"], 1)

    return nc



L0F_SEMS = ("s_w", "s_wg", "s_ld", "s_ldg", "s_mix", "s_mm", "s_ev", "s_vmm", "s_f0", "s_g0", "s_cmb", "s_sum", "s_v1", "s_sc", "s_row", "s_bc", "s_ap", "s_st")


def build_l0ffn(nc=None, block=None, g=None, io=None, bar=None, ph=0):
    standalone = nc is None
    if standalone:
        nc = bass.Bass("TRN2", target_bir_lowering=False)
    def dram(name, shape, dt, kind):
        if io is not None:
            return io[name]
        return nc.dram_tensor(name, shape, dt, kind=kind).ap()
    xT = dram("xT", [D, SEQ], F32, "ExternalInput")
    xn0T = dram("xn0T", [D, SEQ], BF16, "ExternalInput")
    s0T = dram("s0T", [D, SEQ], BF16, "ExternalInput")
    WrT = dram("WrT", [D, D], F32, "ExternalInput")
    WkT = dram("WkT", [D, D], F32, "ExternalInput")
    WvT = dram("WvT", [D, D], F32, "ExternalInput")
    mur = dram("murr", [128, 8], F32, "ExternalInput")
    muk = dram("mukr", [128, 8], F32, "ExternalInput")
    ones_col = dram("ones_col_bf", [128, 1], BF16, "ExternalInput")
    sel32 = dram("sel32", [32, 32 * 128], F32, "ExternalInput")

    x1T = dram("x1T", [D, SEQ], F32, "ExternalOutput")
    xn1T = dram("xn1T", [D, SEQ], BF16, "ExternalOutput")

    NT = SEQ // 512
    KT = 8
    xr_ = xT.rearrange("(k p) n -> p k n", p=128)
    xnr = xn0T.rearrange("(k p) n -> p k n", p=128)
    s0r = s0T.rearrange("(k p) n -> p k n", p=128)
    x1r = x1T.rearrange("(k p) n -> p k n", p=128)
    xn1r = xn1T.rearrange("(k p) n -> p k n", p=128)
    wrr = WrT.rearrange("(k p) m -> p k m", p=128)
    wkr = WkT.rearrange("(k p) m -> p k m", p=128)
    wvr = WvT.rearrange("(k p) m -> p k m", p=128)

    with contextlib.ExitStack() as ctx:
        ent = ctx.enter_context
        wr_sb = ent(nc.sbuf_tensor("wr_sb", [128, KT, D], BF16))
        wk_sb = ent(nc.sbuf_tensor("wk_sb", [128, KT, D], BF16))
        wv_sb = ent(nc.sbuf_tensor("wv_sb", [128, KT, D], BF16))
        mur_sb = ent(nc.sbuf_tensor("mur_sb", [128, 8], F32))
        muk_sb = ent(nc.sbuf_tensor("muk_sb", [128, 8], F32))
        onesc = ent(nc.sbuf_tensor("onesc", [128, 1], BF16))
        selc = ent(nc.sbuf_tensor("selc", [32, 256], F32))
        xs = ent(nc.sbuf_tensor("xs", [128, 2, KT, 513], BF16))
        xf = ent(nc.sbuf_tensor("xf", [128, 1, KT, 512], F32))
        s0_sb = ent(nc.sbuf_tensor("s0_sb", [128, 1, KT, 512], BF16))
        dd = ent(nc.sbuf_tensor("dd", [128, KT, 512], BF16))
        xr = ent(nc.sbuf_tensor("xrb", [128, KT, 512], BF16))
        xk = ent(nc.sbuf_tensor("xkb", [128, KT, 512], BF16))
        r_sb = ent(nc.sbuf_tensor("r_sb", [128, KT, 512], BF16))
        ka_sb = ent(nc.sbuf_tensor("ka_sb", [128, KT, 512], BF16))
        krl = ent(nc.sbuf_tensor("krl", [128, 512], F32))
        f0 = ent(nc.sbuf_tensor("f0", [128, KT, 512], BF16))
        g0 = ent(nc.sbuf_tensor("g0", [128, KT, 512], BF16))
        x1 = ent(nc.sbuf_tensor("x1", [128, KT, 512], F32))
        x1b = ent(nc.sbuf_tensor("x1b", [128, KT, 512], BF16))
        xn1 = ent(nc.sbuf_tensor("xn1", [128, KT, 512], BF16))
        rows = ent(nc.sbuf_tensor("rows", [32, 5 * 512], F32))
        ps = ent(nc.psum_tensor("ps", [128, 4, 512], F32))
        sum_a = ent(nc.psum_tensor("sum_a", [1, 512], F32))
        sum_b = ent(nc.psum_tensor("sum_b", [1, 512], F32))
        bca = ent(nc.psum_tensor("bca", [128, 512], F32))
        bcb = ent(nc.psum_tensor("bcb", [128, 512], F32))
        if standalone:
            g = {}
            for name in L0F_SEMS:
                g[name] = ent(nc.semaphore(name))
            block = ent(nc.Block())


        @block.gpsimd
        def _(gpsimd):
            if bar is not None:
                gpsimd.wait_ge(bar, ph)
            gpsimd.dma_start(out=wr_sb[:], in_=wrr).then_inc(g["s_wg"], 16)
            gpsimd.dma_start(out=wk_sb[:], in_=wkr).then_inc(g["s_wg"], 16)
            gpsimd.dma_start(out=wv_sb[:], in_=wvr).then_inc(g["s_wg"], 16)
            for t in range(NT):
                if t >= 1:
                    gpsimd.wait_ge(g["s_cmb"], t)
                gpsimd.dma_start(out=xf[:, 0], in_=xr_[:, :, 512 * t:512 * (t + 1)]).then_inc(g["s_ldg"], 16)
                gpsimd.dma_start(out=s0_sb[:, 0], in_=s0r[:, :, 512 * t:512 * (t + 1)]).then_inc(g["s_ldg"], 16)

        @block.sync
        def _(sync):
            if bar is not None:
                sync.wait_ge(bar, ph)
            sync.dma_start(out=mur_sb[:], in_=mur).then_inc(g["s_w"], 16)
            sync.dma_start(out=muk_sb[:], in_=muk).then_inc(g["s_w"], 16)
            sync.dma_start(out=onesc[:], in_=ones_col).then_inc(g["s_w"], 16)
            sync.dma_start(out=selc[:], in_=sel32[:, 0:256]).then_inc(g["s_w"], 16)
            for t in range(NT):
                b = t % 2
                if t >= 2:
                    sync.wait_ge(g["s_mix"], 3 * (t - 1))
                sync.dma_start(out=xs[:, b, :, 1:513],
                               in_=xnr[:, :, 512 * t:512 * (t + 1)]).then_inc(g["s_ld"], 16)
                if t >= 1:
                    sync.wait_ge(g["s_cmb"], t)
                    sync.dma_start(out=x1r[:, :, 512 * (t - 1):512 * t], in_=x1[:]).then_inc(g["s_st"], 16)
                    sync.wait_ge(g["s_ap"], t)
                    sync.dma_start(out=xn1r[:, :, 512 * (t - 1):512 * t], in_=xn1[:]).then_inc(g["s_st"], 16)
            sync.wait_ge(g["s_cmb"], NT)
            sync.dma_start(out=x1r[:, :, 512 * (NT - 1):512 * NT], in_=x1[:]).then_inc(g["s_st"], 16)
            sync.wait_ge(g["s_ap"], NT)
            sync.dma_start(out=xn1r[:, :, 512 * (NT - 1):512 * NT], in_=xn1[:]).then_inc(g["s_st"], 16)
            sync.wait_ge(g["s_st"], 16 * 2 * NT)

        @block.tensor
        def _(tensor):
            if bar is not None:
                tensor.wait_ge(bar, ph)
            tensor.wait_ge(g["s_wg"], 48)
            tensor.wait_ge(g["s_w"], 64)
            for t in range(NT):
                b = t % 2
                tensor.wait_ge(g["s_mix"], 3 * t + 3)
                for br, (wsb, rhs) in enumerate(((wr_sb, xr), (wk_sb, xk))):
                    for m in range(8):
                        ci = 24 * t + 8 * br + m
                        if ci >= 4:
                            tensor.wait_ge(g["s_ev"], ci - 3)
                        last = None
                        for k in range(KT):
                            last = tensor.matmul(ps[:, ci % 4], wsb[:, k, 128 * m:128 * (m + 1)],
                                                 rhs[:, k], start=(k == 0), stop=(k == KT - 1))
                        last.then_inc(g["s_mm"], 1)
                tensor.wait_ge(g["s_ev"], 24 * t + 16)
                for m in range(8):
                    ci = 24 * t + 16 + m
                    if ci >= 4:
                        tensor.wait_ge(g["s_ev"], ci - 3)
                    last = None
                    for k in range(KT):
                        last = tensor.matmul(ps[:, ci % 4], wv_sb[:, k, 128 * m:128 * (m + 1)],
                                             ka_sb[:, k], start=(k == 0), stop=(k == KT - 1))
                    last.then_inc(g["s_mm"], 1)
                tensor.wait_ge(g["s_v1"], 3 * t + 2)
                if t >= 1:
                    tensor.wait_ge(g["s_row"], t)
                for k in range(KT):
                    tensor.matmul(sum_a[:], onesc[:], x1b[:, k], start=(k == 0), stop=(k == KT - 1))
                for k in range(KT):
                    last = tensor.matmul(sum_b[:], onesc[:], g0[:, k], start=(k == 0), stop=(k == KT - 1))
                last.then_inc(g["s_sum"], 1)
                tensor.wait_ge(g["s_row"], t + 1)
                if t >= 1:
                    tensor.wait_ge(g["s_ap"], t)
                tensor.matmul(bca[:], selc[:, 0:128], rows[:, 0:512], start=True, stop=True)
                tensor.matmul(bcb[:], selc[:, 0:128], rows[:, 512:1024], start=True,
                              stop=True).then_inc(g["s_bc"], 1)

        @block.scalar
        def _(scalar):
            if bar is not None:
                scalar.wait_ge(bar, ph)
            for t in range(NT):
                for m in range(8):
                    ci = 24 * t + m
                    scalar.wait_ge(g["s_mm"], ci + 1)
                    if t >= 1 and m == 0:
                        scalar.wait_ge(g["s_ev"], 24 * t)
                    scalar.activation(r_sb[:, m], ps[:, ci % 4], AF.Sigmoid).then_inc(g["s_ev"], 1)
                for m in range(8):
                    ci = 24 * t + 8 + m
                    scalar.wait_ge(g["s_mm"], ci + 1)
                    scalar.activation(krl[:], ps[:, ci % 4], AF.Relu)
                    scalar.activation(ka_sb[:, m], krl[:], AF.Square).then_inc(g["s_ev"], 1)
                scalar.wait_ge(g["s_ev"], 24 * (t + 1))
                if t >= 1:
                    scalar.wait_ge(g["s_sum"], t)
                scalar.activation(g0[:], f0[:], AF.Silu).then_inc(g["s_g0"], 1)
                scalar.wait_ge(g["s_cmb"], t + 1)
                scalar.activation(g0[:], x1[:], AF.Square).then_inc(g["s_v1"], 1)
                scalar.wait_ge(g["s_v1"], 3 * t + 3)
                scalar.activation(rows[0:1, 4 * 512:5 * 512], rows[0:1, 3 * 512:4 * 512], AF.Ln)
                scalar.activation(rows[0:1, 0:512], rows[0:1, 4 * 512:5 * 512], AF.Exp,
                                  scale=-0.5).then_inc(g["s_sc"], 1)

        @block.vector
        def _(vector):
            if bar is not None:
                vector.wait_ge(bar, ph)
            vector.memset(xs[:, 0, :, 0:1], 0.0)
            vector.memset(rows[:], 0.0)
            for t in range(NT):
                b = t % 2
                vector.wait_ge(g["s_ld"], 16 * (t + 1))
                r1 = None
                for k in range(KT):
                    r1 = vector.tensor_tensor(dd[:, k], xs[:, b, k, 1:513], xs[:, b, k, 0:512],
                                              ALU.subtract)
                r1.then_inc(g["s_mix"], 1)
                for k in range(KT):
                    r1 = vector.scalar_tensor_tensor(xr[:, k], dd[:, k], mur_sb[:, k:k + 1],
                                                     xs[:, b, k, 0:512], ALU.mult, ALU.add)
                r1.then_inc(g["s_mix"], 1)
                for k in range(KT):
                    r2 = vector.scalar_tensor_tensor(xk[:, k], dd[:, k], muk_sb[:, k:k + 1],
                                                     xs[:, b, k, 0:512], ALU.mult, ALU.add)
                if t + 1 < NT:
                    nb = (t + 1) % 2
                    r2 = vector.tensor_tensor(xs[:, nb, :, 0:1], xs[:, b, :, 512:513],
                                              xs[:, b, :, 512:513], ALU.max)
                r2.then_inc(g["s_mix"], 1)
                for m in range(8):
                    ci = 24 * t + 16 + m
                    vector.wait_ge(g["s_mm"], ci + 1)
                    vector.tensor_tensor(f0[:, m], ps[:, ci % 4], r_sb[:, m],
                                         ALU.mult).then_inc(g["s_ev"], 1)
                vector.wait_ge(g["s_g0"], t + 1)
                vector.wait_ge(g["s_ldg"], 32 * (t + 1))
                if t >= 1:
                    vector.wait_ge(g["s_st"], 32 * t - 16)
                    vector.wait_ge(g["s_sc"], t)
                for k in range(KT):
                    vector.tensor_tensor(x1b[:, k], s0_sb[:, 0, k], g0[:, k], ALU.mult)
                    r3 = vector.tensor_tensor(x1[:, k], xf[:, 0, k], x1b[:, k], ALU.add)
                r3.then_inc(g["s_cmb"], 1)
                for k in range(KT):
                    r4 = vector.tensor_copy(x1b[:, k], x1[:, k])
                r4.then_inc(g["s_v1"], 1)
                vector.wait_ge(g["s_sum"], t + 1)
                if t >= 1:
                    vector.wait_ge(g["s_bc"], t)
                rr = lambda j: rows[0:1, 512 * j:512 * (j + 1)]
                vector.tensor_scalar_mul(rr(2), sum_a[:], 1.0 / D)
                vector.tensor_scalar_mul(rr(3), sum_b[:], 1.0 / D)
                vector.tensor_tensor(rr(4), rr(2), rr(2), ALU.mult)
                vector.tensor_tensor(rr(3), rr(3), rr(4), ALU.subtract)
                vector.tensor_scalar_add(rr(3), rr(3), 1e-5).then_inc(g["s_v1"], 1)
                vector.wait_ge(g["s_sc"], t + 1)
                vector.tensor_tensor(rr(4), rr(2), rr(0), ALU.mult)
                vector.tensor_scalar_mul(rr(1), rr(4), -1.0).then_inc(g["s_row"], 1)
                vector.wait_ge(g["s_bc"], t + 1)
                if t >= 1:
                    vector.wait_ge(g["s_st"], 32 * t)
                for k in range(KT):
                    vector.tensor_tensor(xn1[:, k], x1[:, k], bca[:], ALU.mult)
                    r5 = vector.scalar_tensor_tensor(xn1[:, k], xn1[:, k], 1.0, bcb[:],
                                                     ALU.mult, ALU.add)
                r5.then_inc(g["s_ap"], 1)

    return nc



L1F_SEMS = ("s_w", "s_wg", "s_wf", "s_ld", "s_xg", "s_mm", "s_eva", "s_evd", "s_if", "s_bc", "s_bce", "s_fiv", "s_fis", "s_fisc", "s_fid", "s_st")


def build_l1front(nc=None, block=None, g=None, io=None, bar=None, ph=0):
    standalone = nc is None
    if standalone:
        nc = bass.Bass("TRN2", target_bir_lowering=False)
    def dram(name, shape, dt, kind):
        if io is not None:
            return io[name]
        return nc.dram_tensor(name, shape, dt, kind=kind).ap()
    xn1T = dram("xn1T", [D, SEQ], BF16, "ExternalInput")
    qwT = dram("qwT", [D, D], F32, "ExternalInput")
    kwT = dram("kwT", [D, D], F32, "ExternalInput")
    vwT = dram("vwT", [D, D], F32, "ExternalInput")
    iwT = dram("iwT", [D, 64], F32, "ExternalInput")
    fwT = dram("fwT", [D, 64], F32, "ExternalInput")
    gmr = dram("gmr", [128, 8], F32, "ExternalInput")
    btr = dram("btr", [128, 8], F32, "ExternalInput")
    qbr = dram("qbr", [128, 8], F32, "ExternalInput")
    kbr2 = dram("kbr2", [128, 8], F32, "ExternalInput")
    vbr2 = dram("vbr2", [128, 8], F32, "ExternalInput")
    kbrow = dram("kbrow", [32, D], F32, "ExternalInput")
    vbrow = dram("vbrow", [32, D], F32, "ExternalInput")
    ibr = dram("ibr", [64, 1], F32, "ExternalInput")
    fbr = dram("fbr", [64, 1], F32, "ExternalInput")
    c44 = dram("c44", [64, 1], F32, "ExternalInput")
    c44n = dram("c44n", [64, 1], F32, "ExternalInput")
    sel32 = dram("sel32", [32, 32 * 128], F32, "ExternalInput")

    qT = dram("qT", [D, SEQ], BF16, "ExternalOutput")
    kT = dram("kT", [D, SEQ], BF16, "ExternalOutput")
    vT = dram("vT", [D, SEQ], BF16, "ExternalOutput")
    ksm = dram("ksm", [SEQ, D], BF16, "ExternalOutput")
    vsm = dram("vsm", [SEQ, D], BF16, "ExternalOutput")
    atil = dram("atil", [64, SEQ], F32, "ExternalOutput")
    btil = dram("btil", [64, SEQ], F32, "ExternalOutput")
    decd = dram("decd", [64, SEQ // 128], F32, "ExternalOutput")

    NT = SEQ // 512
    KT = 8
    NCH = 40
    xnr = xn1T.rearrange("(k p) n -> p k n", p=128)
    qwr = qwT.rearrange("(k p) m -> p k m", p=128)
    kwr = kwT.rearrange("(k p) m -> p k m", p=128)
    vwr = vwT.rearrange("(k p) m -> p k m", p=128)
    iwr = iwT.rearrange("(k p) m -> p k m", p=128)
    fwr = fwT.rearrange("(k p) m -> p k m", p=128)
    qspr = qT.rearrange("(k p) n -> p k n", p=128)
    kspr = kT.rearrange("(k p) n -> p k n", p=128)
    vspr = vT.rearrange("(k p) n -> p k n", p=128)
    ksmr = ksm.rearrange("(a p) d -> p a d", p=128)
    vsmr = vsm.rearrange("(a p) d -> p a d", p=128)

    with contextlib.ExitStack() as ctx:
        ent = ctx.enter_context
        qw_sb = ent(nc.sbuf_tensor("qw_sb", [128, KT, D], BF16))
        kw_sb = ent(nc.sbuf_tensor("kw_sb", [128, KT, D], BF16))
        vw_sb = ent(nc.sbuf_tensor("vw_sb", [128, KT, D], BF16))
        iw_sb = ent(nc.sbuf_tensor("iw_sb", [128, KT, 64], BF16))
        fw_sb = ent(nc.sbuf_tensor("fw_sb", [128, KT, 64], BF16))
        gm_sb = ent(nc.sbuf_tensor("gm_sb", [128, 8], F32))
        bt_sb = ent(nc.sbuf_tensor("bt_sb", [128, 8], F32))
        qb_sb = ent(nc.sbuf_tensor("qb_sb", [128, 8], F32))
        kb_sb = ent(nc.sbuf_tensor("kb_sb", [128, 8], F32))
        vb_sb2 = ent(nc.sbuf_tensor("vb_sb2", [128, 8], F32))
        kbr_sb = ent(nc.sbuf_tensor("kbr_sb", [32, D], F32))
        vbr_sb = ent(nc.sbuf_tensor("vbr_sb", [32, D], F32))
        ib_sb = ent(nc.sbuf_tensor("ib_sb", [64, 1], F32))
        fb_sb = ent(nc.sbuf_tensor("fb_sb", [64, 1], F32))
        c44_sb = ent(nc.sbuf_tensor("c44_sb", [64, 1], F32))
        c44n_sb = ent(nc.sbuf_tensor("c44n_sb", [64, 1], F32))
        selc = ent(nc.sbuf_tensor("selc", [32, 256], F32))
        kbb = ent(nc.sbuf_tensor("kbb", [128, D], F32))
        vbb = ent(nc.sbuf_tensor("vbb", [128, D], F32))
        xn = ent(nc.sbuf_tensor("xn", [128, 2, KT, 512], BF16))
        xg = ent(nc.sbuf_tensor("xg", [128, 2, KT, 512], BF16))
        qsp = ent(nc.sbuf_tensor("qsp", [128, 8, 512], BF16))
        ksp = ent(nc.sbuf_tensor("ksp", [128, 8, 512], BF16))
        vsp = ent(nc.sbuf_tensor("vsp", [128, 8, 512], BF16))
        ksm_sb = ent(nc.sbuf_tensor("ksm_sb", [128, 4, D], BF16))
        vsm_sb = ent(nc.sbuf_tensor("vsm_sb", [128, 4, D], BF16))
        fi = ent(nc.sbuf_tensor("fi", [64, 8, 512], F32))
        carry = ent(nc.sbuf_tensor("carry", [64, 4], F32))
        decs = ent(nc.sbuf_tensor("decs", [64, 16], F32))
        zrow = ent(nc.sbuf_tensor("zrow", [64, 512], F32))
        ps = ent(nc.psum_tensor("ps", [128, 4, 512], F32))
        ps_if = ent(nc.psum_tensor("ps_if", [64, 2, 512], F32))
        ps_bc = ent(nc.psum_tensor("ps_bc", [128, 2, 512], F32))
        if standalone:
            g = {}
            for name in L1F_SEMS:
                g[name] = ent(nc.semaphore(name))
            block = ent(nc.Block())

        def bank_wait(eng, t, ci):
            gi = NCH * t + ci
            if gi < 4:
                return
            pg = gi - 4
            pt, pc = divmod(pg, NCH)
            if pc < 24:
                eng.wait_ge(g["s_eva"], 24 * pt + pc + 1)
            else:
                eng.wait_ge(g["s_evd"], 16 * pt + (pc - 24) + 1)

        @block.gpsimd
        def _(gpsimd):
            if bar is not None:
                gpsimd.wait_ge(bar, ph)
            gpsimd.dma_start(out=qw_sb[:], in_=qwr).then_inc(g["s_wg"], 16)
            gpsimd.dma_start(out=kw_sb[:], in_=kwr).then_inc(g["s_wg"], 16)
            gpsimd.dma_start(out=vw_sb[:], in_=vwr).then_inc(g["s_wg"], 16)
            gpsimd.dma_start(out=iw_sb[:], in_=iwr).then_inc(g["s_wg"], 16)
            gpsimd.dma_start(out=fw_sb[:], in_=fwr).then_inc(g["s_wg"], 16)

        @block.sync
        def _(sync):
            if bar is not None:
                sync.wait_ge(bar, ph)
            for ap_out, ap_in in ((gm_sb[:], gmr), (bt_sb[:], btr),
                                  (qb_sb[:], qbr), (kb_sb[:], kbr2), (vb_sb2[:], vbr2),
                                  (kbr_sb[:], kbrow),
                                  (vbr_sb[:], vbrow), (ib_sb[:], ibr), (fb_sb[:], fbr),
                                  (c44_sb[:], c44), (c44n_sb[:], c44n),
                                  (selc[:], sel32[:, 0:256])):
                sync.dma_start(out=ap_out, in_=ap_in).then_inc(g["s_w"], 16)
            for t in range(NT + 1):
                if t < NT:
                    b = t % 2
                    if t >= 2:
                        sync.wait_ge(g["s_mm"], NCH * (t - 1))
                    sync.dma_start(out=xn[:, b], in_=xnr[:, :, 512 * t:512 * (t + 1)]).then_inc(g["s_ld"], 16)
                if t >= 1:
                    sync.wait_ge(g["s_eva"], 24 * t)
                    sync.wait_ge(g["s_evd"], 16 * t)
                    sync.dma_start(out=qspr[:, :, 512 * (t - 1):512 * t], in_=qsp[:]).then_inc(g["s_st"], 16)
                    sync.dma_start(out=kspr[:, :, 512 * (t - 1):512 * t], in_=ksp[:]).then_inc(g["s_st"], 16)
                    sync.dma_start(out=vspr[:, :, 512 * (t - 1):512 * t], in_=vsp[:]).then_inc(g["s_st"], 16)
                    sync.dma_start(out=ksmr[:, 4 * (t - 1):4 * t], in_=ksm_sb[:]).then_inc(g["s_st"], 16)
                    sync.dma_start(out=vsmr[:, 4 * (t - 1):4 * t], in_=vsm_sb[:]).then_inc(g["s_st"], 16)
                    sync.wait_ge(g["s_fiv"], t)
                    sync.dma_start(out=atil[:, 512 * (t - 1):512 * t], in_=fi[:, 5]).then_inc(g["s_st"], 16)
                    sync.dma_start(out=btil[:, 512 * (t - 1):512 * t], in_=fi[:, 6]).then_inc(g["s_st"], 16)
                    sync.dma_start(out=decd[:, 4 * (t - 1):4 * t], in_=decs[:, 12:16]).then_inc(g["s_st"], 16)
            sync.wait_ge(g["s_st"], 16 * 8 * NT)

        @block.tensor
        def _(tensor):
            if bar is not None:
                tensor.wait_ge(bar, ph)
            tensor.wait_ge(g["s_wg"], 80)
            tensor.wait_ge(g["s_wf"], 1)
            tensor.wait_ge(g["s_w"], 192)
            tensor.matmul(ps_bc[:, 0], selc[:, 0:128], kbr_sb[:, 0:512], start=True, stop=True)
            tensor.matmul(ps_bc[:, 1], selc[:, 0:128], kbr_sb[:, 512:1024],
                          start=True, stop=True).then_inc(g["s_bc"], 1)
            tensor.wait_ge(g["s_bce"], 1)
            tensor.matmul(ps_bc[:, 0], selc[:, 0:128], vbr_sb[:, 0:512], start=True, stop=True)
            tensor.matmul(ps_bc[:, 1], selc[:, 0:128], vbr_sb[:, 512:1024],
                          start=True, stop=True).then_inc(g["s_bc"], 2)
            for t in range(NT):
                b = t % 2
                tensor.wait_ge(g["s_xg"], t + 1)
                ci = 0
                for wsb in (qw_sb, kw_sb, vw_sb):
                    for m in range(8):
                        bank_wait(tensor, t, ci)
                        last = None
                        for k in range(KT):
                            last = tensor.matmul(ps[:, (NCH * t + ci) % 4],
                                                 wsb[:, k, 128 * m:128 * (m + 1)],
                                                 xg[:, b, k], start=(k == 0), stop=(k == KT - 1))
                        last.then_inc(g["s_mm"], 1)
                        ci += 1
                for wsb in (kw_sb, vw_sb):
                    for a in range(4):
                        for half in range(2):
                            bank_wait(tensor, t, ci)
                            last = None
                            for k in range(KT):
                                last = tensor.matmul(ps[:, (NCH * t + ci) % 4],
                                                     xg[:, b, k, 128 * a:128 * (a + 1)],
                                                     wsb[:, k, 512 * half:512 * (half + 1)],
                                                     start=(k == 0), stop=(k == KT - 1))
                            last.then_inc(g["s_mm"], 1)
                            ci += 1
                if t >= 1:
                    tensor.wait_ge(g["s_fid"], t)
                for k in range(KT):
                    tensor.matmul(ps_if[:, 0], iw_sb[:, k], xg[:, b, k],
                                  start=(k == 0), stop=(k == KT - 1))
                for k in range(KT):
                    last = tensor.matmul(ps_if[:, 1], fw_sb[:, k], xg[:, b, k],
                                         start=(k == 0), stop=(k == KT - 1))
                last.then_inc(g["s_if"], 1)

        @block.scalar
        def _(scalar):
            if bar is not None:
                scalar.wait_ge(bar, ph)
            scalar.wait_ge(g["s_w"], 192)
            scalar.mul(kb_sb[:], kb_sb[:], 0.25)
            scalar.wait_ge(g["s_bc"], 1)
            scalar.copy(kbb[:, 0:512], ps_bc[:, 0])
            scalar.mul(kbb[:, 0:512], kbb[:, 0:512], 0.25)
            scalar.copy(kbb[:, 512:1024], ps_bc[:, 1])
            scalar.mul(kbb[:, 512:1024], kbb[:, 512:1024], 0.25).then_inc(g["s_bce"], 1)
            scalar.wait_ge(g["s_bc"], 2)
            scalar.copy(vbb[:, 0:512], ps_bc[:, 0])
            scalar.copy(vbb[:, 512:1024], ps_bc[:, 1]).then_inc(g["s_bce"], 2)
            for t in range(NT):
                for ci in range(24):
                    scalar.wait_ge(g["s_mm"], NCH * t + ci + 1)
                    if t >= 1 and ci == 0:
                        scalar.wait_ge(g["s_st"], 16 * 8 * t)
                    tau = ci % 8
                    dst = (qsp, ksp, vsp)[ci // 8]
                    bias = (qb_sb, kb_sb, vb_sb2)[ci // 8]
                    scalar.add(dst[:, tau], ps[:, (NCH * t + ci) % 4],
                               bias[:, tau:tau + 1]).then_inc(g["s_eva"], 1)
                scalar.wait_ge(g["s_fis"], 3 * t + 1)
                scalar.activation(fi[:, 5], fi[:, 7], AF.Exp,
                                  bias=c44_sb[:]).then_inc(g["s_fisc"], 1)
                scalar.wait_ge(g["s_fis"], 3 * t + 2)
                scalar.activation(fi[:, 6], fi[:, 7], AF.Exp, scale=-1.0,
                                  bias=c44n_sb[:]).then_inc(g["s_fisc"], 1)
                scalar.wait_ge(g["s_fis"], 3 * t + 3)
                scalar.activation(decs[:, 12:16], decs[:, 8:12], AF.Exp).then_inc(g["s_fiv"], 1)

        @block.vector
        def _(vector):
            if bar is not None:
                vector.wait_ge(bar, ph)
            vector.wait_ge(g["s_wg"], 32)
            vector.tensor_scalar_mul(kw_sb[:], kw_sb[:], 0.25).then_inc(g["s_wf"], 1)
            vector.memset(carry[:], 0.0)
            vector.memset(zrow[:], 0.0)
            for t in range(NT):
                b2_ = t % 2
                vector.wait_ge(g["s_ld"], 16 * (t + 1))
                vector.wait_ge(g["s_w"], 192)
                if t >= 2:
                    vector.wait_ge(g["s_mm"], NCH * (t - 1))
                for k in range(KT):
                    op = vector.scalar_tensor_tensor(
                        xg[:, b2_, k].unsqueeze(1), xn[:, b2_, k].unsqueeze(1),
                        gm_sb[:, k:k + 1],
                        bt_sb[:, k:k + 1].unsqueeze(2).broadcast_to([128, 1, 512]),
                        ALU.mult, ALU.add)
                    if k == KT - 1:
                        op.then_inc(g["s_xg"], 1)
                for ci in range(24, 40):
                    vector.wait_ge(g["s_mm"], NCH * t + ci + 1)
                    if t >= 1 and ci == 24:
                        vector.wait_ge(g["s_st"], 16 * 8 * t)
                    pss = ps[:, (NCH * t + ci) % 4]
                    a = (ci - 24) % 8 // 2
                    half = (ci - 24) % 2
                    if ci < 32:
                        vector.tensor_tensor(ksm_sb[:, a, 512 * half:512 * (half + 1)], pss,
                                             kbb[:, 512 * half:512 * (half + 1)],
                                             ALU.add).then_inc(g["s_evd"], 1)
                    else:
                        vector.tensor_tensor(vsm_sb[:, a, 512 * half:512 * (half + 1)], pss,
                                             vbb[:, 512 * half:512 * (half + 1)],
                                             ALU.add).then_inc(g["s_evd"], 1)
                vector.wait_ge(g["s_if"], t + 1)
                vector.tensor_scalar_add(fi[:, 0], ps_if[:, 0], ib_sb[:])
                vector.tensor_scalar_add(fi[:, 1], ps_if[:, 1], fb_sb[:]).then_inc(g["s_fid"], 1)
                vector.tensor_tensor_scan(fi[:, 2], fi[:, 1], zrow[:], carry[:, 0:1],
                                          ALU.add, ALU.add)
                vector.tensor_copy(decs[:, 0:1], carry[:, 1:2])
                vector.tensor_copy(carry[:, 0:1], fi[:, 2, 511:512])
                vector.tensor_tensor(fi[:, 3], fi[:, 0], fi[:, 2], ALU.subtract)
                vector.tensor_tensor_scan(fi[:, 4], zrow[:], fi[:, 3], carry[:, 1:2],
                                          ALU.add, ALU.max)
                gce = fi[:, 4].rearrange("p (c l) -> p c l", l=128)[:, :, 127]
                for cc in range(4):
                    vector.tensor_tensor(decs[:, 1 + cc:2 + cc], gce[:, cc:cc + 1],
                                         gce[:, cc:cc + 1], ALU.max)
                vector.tensor_copy(carry[:, 1:2], fi[:, 4, 511:512])
                if t >= 1:
                    vector.wait_ge(g["s_fiv"], t)
                    vector.wait_ge(g["s_fisc"], 2 * t)
                gceb = gce.unsqueeze(2).broadcast_to([64, 4, 128])
                vector.tensor_tensor(fi[:, 7].rearrange("p (c l) -> p c l", l=128),
                                     fi[:, 3].rearrange("p (c l) -> p c l", l=128),
                                     gceb, ALU.subtract).then_inc(g["s_fis"], 1)
                vector.wait_ge(g["s_fisc"], 2 * t + 1)
                vector.tensor_tensor(fi[:, 7].rearrange("p (c l) -> p c l", l=128),
                                     fi[:, 4].rearrange("p (c l) -> p c l", l=128),
                                     gceb, ALU.subtract)
                vector.tensor_scalar_max(fi[:, 7], fi[:, 7],
                                         -124.0).then_inc(g["s_fis"], 1)
                vector.wait_ge(g["s_fisc"], 2 * t + 2)
                vector.tensor_tensor(decs[:, 8:12], decs[:, 0:4], decs[:, 1:5],
                                     ALU.subtract).then_inc(g["s_fis"], 1)

    return nc



ML_SEMS = ("s_w", "s_wg", "s_ld", "s_ldg", "s_ab", "s_abe", "s_dt", "s_dte", "s_dmm", "s_dece", "s_sb", "s_sbe", "s_ske", "s_ktl", "s_pv", "s_den", "s_scl", "s_yint", "s_ctm", "s_up", "s_yb", "s_hn", "s_ht", "s_hte", "s_opm", "s_ope", "s_st", "s_dbg", "s_dnr", "s_rcp", "s_z0")


def build_mlstm(nc=None, block=None, g=None, io=None, bar=None, ph=0):
    standalone = nc is None
    if standalone:
        nc = bass.Bass("TRN2", target_bir_lowering=False)
    def dram(name, shape, dt, kind):
        if io is not None:
            return io[name]
        return nc.dram_tensor(name, shape, dt, kind=kind).ap()
    qT = dram("qT", [D, SEQ], BF16, "ExternalInput")
    kT = dram("kT", [D, SEQ], BF16, "ExternalInput")
    vT = dram("vT", [D, SEQ], BF16, "ExternalInput")
    ksmd = dram("ksm", [SEQ, D], BF16, "ExternalInput")
    vsmd = dram("vsm", [SEQ, D], BF16, "ExternalInput")
    atil = dram("atil", [64, SEQ], F32, "ExternalInput")
    btil = dram("btil", [64, SEQ], F32, "ExternalInput")
    decd = dram("decd", [64, SEQ // 128], F32, "ExternalInput")
    owT = dram("owT", [D, D], F32, "ExternalInput")
    obr = dram("obr", [128, 8], F32, "ExternalInput")
    xn1T = dram("xn1T", [D, SEQ], BF16, "ExternalInput")
    sel32 = dram("sel32", [32, 32 * 128], F32, "ExternalInput")
    id64 = dram("ident64f", [64, 64], F32, "ExternalInput")
    pm64 = dram("perm64", [64, 64], F32, "ExternalInput")
    c44q = dram("c44_128", [128, 1], F32, "ExternalInput")
    idf = dram("ident128", [128, 128], F32, "ExternalInput")
    mask8 = dram("mask8", [128, 8 * 128], BF16, "ExternalInput")

    s1T = dram("s1T", [D, SEQ], BF16, "ExternalOutput")
    if DBG_ML:
        d_abT = nc.dram_tensor("d_abT", [128, 128], F32, kind="ExternalOutput").ap()
        d_abTb = nc.dram_tensor("d_abTb", [128, 64], BF16, kind="ExternalOutput").ap()
        d_ssb = nc.dram_tensor("d_ssb", [128, 128], BF16, kind="ExternalOutput").ap()
        d_skb = nc.dram_tensor("d_skb", [128, 128], BF16, kind="ExternalOutput").ap()
        d_dn = nc.dram_tensor("d_dn", [128, 64], F32, kind="ExternalOutput").ap()
        d_dn2 = nc.dram_tensor("d_dn2", [128, 64], F32, kind="ExternalOutput").ap()
        d_csp = nc.dram_tensor("d_csp", [128, 272], F32, kind="ExternalOutput").ap()
        d_cspb = nc.dram_tensor("d_cspb", [128, 272], BF16, kind="ExternalOutput").ap()
        d_hsm = nc.dram_tensor("d_hsm", [128, 1024], F32, kind="ExternalOutput").ap()
        d_hcm = nc.dram_tensor("d_hcm", [128, 8, 512], BF16, kind="ExternalOutput").ap()
        d_vtl = nc.dram_tensor("d_vtl", [128, D], BF16, kind="ExternalOutput").ap()
        d_dbq = nc.dram_tensor("d_dbq", [128, 256], F32, kind="ExternalOutput").ap()

    NC_ = SEQ // 128
    NG = SEQ // 512
    qsp_view = qT.rearrange("(tau g e) n -> g e tau n", g=4, e=16)
    ksp_view = kT.rearrange("(tau g e) n -> g e tau n", g=4, e=16)
    vsp_view = vT.rearrange("(tau g e) n -> g e tau n", g=4, e=16)
    ksmr = ksmd.rearrange("(a p) d -> p a d", p=128)
    vsmr = vsmd.rearrange("(a p) d -> p a d", p=128)
    owr = owT.rearrange("(k p) m -> p k m", p=128)
    xnr = xn1T.rearrange("(k p) n -> p k n", p=128)
    s1r = s1T.rearrange("(k p) n -> p k n", p=128)

    with contextlib.ExitStack() as ctx:
        ent = ctx.enter_context
        ow_sb = ent(nc.sbuf_tensor("ow_sb", [128, 8, D], BF16))
        ob_sb = ent(nc.sbuf_tensor("ob_sb", [128, 8], F32))
        selc = ent(nc.sbuf_tensor("selc", [32, 512], F32))
        id64c = ent(nc.sbuf_tensor("id64c", [64, 64], F32))
        pm64c = ent(nc.sbuf_tensor("pm64c", [64, 64], F32))
        c44c = ent(nc.sbuf_tensor("c44c", [128, 1], F32))
        idfc = ent(nc.sbuf_tensor("idfc", [128, 128], F32))
        m8c = ent(nc.sbuf_tensor("m8c", [128, 8 * 128], BF16))
        qsp = ent(nc.sbuf_tensor("qsp", [128, 2, 16, 128], BF16))
        ksp = ent(nc.sbuf_tensor("ksp", [128, 2, 16, 128], BF16))
        vsp = ent(nc.sbuf_tensor("vsp", [128, 2, 16, 128], BF16))
        ksm_sb = ent(nc.sbuf_tensor("ksm_sb", [128, 2, D], BF16))
        vsm_sb = ent(nc.sbuf_tensor("vsm_sb", [128, 2, D], BF16))
        vtl = ent(nc.sbuf_tensor("vtl", [128, D], BF16))
        ktl = ent(nc.sbuf_tensor("ktl", [128, D], BF16))
        absb = ent(nc.sbuf_tensor("absb", [64, 2, 2, 512], F32))
        decg = ent(nc.sbuf_tensor("decg", [64, 2, 4], F32))
        zdec = ent(nc.sbuf_tensor("zdec", [32, 64], F32))
        abT = ent(nc.sbuf_tensor("abT", [128, 128], F32))
        abTb = ent(nc.sbuf_tensor("abTb", [128, 64], BF16))
        abTs = ent(nc.sbuf_tensor("abTs", [128, 64], F32))
        rcpb = ent(nc.sbuf_tensor("rcpb", [128, 64], F32))
        rcpb2 = ent(nc.sbuf_tensor("rcpb2", [128, 64], F32))
        decbq = ent(nc.sbuf_tensor("decbq", [128, 256], F32))
        ssb = ent(nc.sbuf_tensor("ssb", [128, 64, 128], BF16))
        skb = ent(nc.sbuf_tensor("skb", [128, 64, 128], BF16))
        csp = ent(nc.sbuf_tensor("csp", [128, 272], F32))
        cspb = ent(nc.sbuf_tensor("cspb", [128, 272], BF16))
        dnall = ent(nc.sbuf_tensor("dnall", [128, 64], F32))
        dnneg = ent(nc.sbuf_tensor("dnneg", [128, 64], F32))
        hsm = ent(nc.sbuf_tensor("hsm", [128, 1024], F32))
        hcm = ent(nc.sbuf_tensor("hcm", [128, 8, 512], BF16))
        tmp512 = ent(nc.sbuf_tensor("tmp512", [128, 512], F32))
        s1_sb = ent(nc.sbuf_tensor("s1_sb", [128, 8, 512], BF16))
        xng = ent(nc.sbuf_tensor("xng", [128, 8, 512], BF16))
        ps_sb = ent(nc.psum_tensor("ps_sb", [128, 2, 512], F32))
        ps_y = ent(nc.psum_tensor("ps_y", [128, 4, 512], F32))
        ps_c = ent(nc.psum_tensor("ps_c", [128, 512], F32))
        ps_m = ent(nc.psum_tensor("ps_m", [128, 512], F32))
        if standalone:
            g = {}
            for name in ML_SEMS:
                g[name] = ent(nc.semaphore(name))
            block = ent(nc.Block())

        def batch_end6(bb):
            c2, k2 = divmod(bb, 34)
            if k2 < 16:
                return ("s_sbe", 16 * c2 + k2 + 1)
            if k2 < 32:
                return ("s_ske", 16 * c2 + (k2 - 16) + 1)
            return ("s_hte", 2 * c2 + (k2 - 32) + 1)

        def aview(sigma, qr, w):
            return abT[:, 0:64].rearrange("p (t2 four) -> p t2 four", four=4)[
                :, 4 * qr:4 * qr + 4, sigma].unsqueeze(2).broadcast_to([128, 4, w])

        @block.gpsimd
        def _(gpsimd):
            if bar is not None:
                gpsimd.wait_ge(bar, ph)
            gpsimd.dma_start(out=ow_sb[:], in_=owr).then_inc(g["s_wg"], 16)
            for c in range(NC_):
                cb = c % 2
                if c >= 2:
                    gpsimd.wait_ge(g["s_pv"], c - 1)
                    gpsimd.wait_ge(g["s_ktl"], 2 * (c - 1))
                    gpsimd.wait_ge(g["s_yint"], 4 * (c - 1))
                    gpsimd.wait_ge(g["s_ctm"], 4 * (c - 1))
                for dst, view in ((qsp, qsp_view), (ksp, ksp_view), (vsp, vsp_view)):
                    for gg in range(4):
                        gpsimd.dma_start(out=dst[32 * gg:32 * gg + 16, cb],
                                         in_=view[gg, :, :, 128 * c:128 * (c + 1)]).then_inc(g["s_ldg"], 16)
                gpsimd.dma_start(out=ksm_sb[:, cb], in_=ksmr[:, c]).then_inc(g["s_ldg"], 16)
                gpsimd.dma_start(out=vsm_sb[:, cb], in_=vsmr[:, c]).then_inc(g["s_ldg"], 16)

        @block.sync
        def _(sync):
            if bar is not None:
                sync.wait_ge(bar, ph)
            for ap_out, ap_in in ((ob_sb[:], obr), (selc[:], sel32[:, 0:512]),
                                  (id64c[:], id64), (pm64c[:], pm64), (c44c[:], c44q),
                                  (idfc[:], idf), (m8c[:], mask8)):
                sync.dma_start(out=ap_out, in_=ap_in).then_inc(g["s_w"], 16)
            for gi in range(NG):
                gb = gi % 2
                if gi >= 2:
                    sync.wait_ge(g["s_hn"], 16 * (gi - 1))
                    sync.wait_ge(g["s_up"], 16 * (gi - 1))
                sync.dma_start(out=absb[0:64, 0, gb], in_=atil[:, 512 * gi:512 * (gi + 1)]).then_inc(g["s_ld"], 16)
                sync.dma_start(out=absb[0:64, 1, gb], in_=btil[:, 512 * gi:512 * (gi + 1)]).then_inc(g["s_ld"], 16)
                sync.dma_start(out=decg[:, gb], in_=decd[:, 4 * gi:4 * (gi + 1)]).then_inc(g["s_ld"], 16)
                if gi >= 1:
                    sync.wait_ge(g["s_ope"], 8 * gi)
                    sync.dma_start(out=s1r[:, :, 512 * (gi - 1):512 * gi], in_=s1_sb[:]).then_inc(g["s_st"], 16)
                sync.dma_start(out=xng[:], in_=xnr[:, :, 512 * gi:512 * (gi + 1)]).then_inc(g["s_ld"], 16)
            sync.wait_ge(g["s_ope"], 8 * NG)
            sync.dma_start(out=s1r[:, :, 512 * (NG - 1):512 * NG], in_=s1_sb[:]).then_inc(g["s_st"], 16)
            if DBG_ML:
                nd = 0
                for ap_out, src in ((d_abT, abT[:]), (d_abTb, abTb[:]),
                                    (d_ssb, ssb[:, 63, :]), (d_skb, skb[:, 63, :]),
                                    (d_dn, dnall[:]), (d_dn2, rcpb[:]),
                                    (d_csp, csp[:]), (d_cspb, cspb[:]),
                                    (d_hsm, hsm[:]), (d_hcm, hcm[:]),
                                    (d_vtl, vtl[:]), (d_dbq, decbq[:])):
                    sync.dma_start(out=ap_out, in_=src).then_inc(g["s_dbg"], 16)
                    nd += 1
                sync.wait_ge(g["s_dbg"], 16 * nd)
            sync.wait_ge(g["s_st"], 16 * NG)

        @block.tensor
        def _(tensor):
            if bar is not None:
                tensor.wait_ge(bar, ph)
            tensor.wait_ge(g["s_w"], 112)
            tensor.wait_ge(g["s_wg"], 16)
            tensor.wait_ge(g["s_z0"], 1)
            for gi in range(NG):
                gb = gi % 2
                tensor.wait_ge(g["s_ld"], 64 * gi + 48)
                if gi >= 1:
                    tensor.wait_ge(g["s_dece"], gi)
                    tensor.wait_ge(g["s_up"], 16 * gi)
                    tensor.wait_ge(g["s_ope"], 8 * gi)
                tensor.transpose(ps_m[0:4, 128:192], decg[:, gb], id64c[:]).then_inc(g["s_dt"], 1)
                tensor.wait_ge(g["s_dte"], gi + 1)
                last = None
                for r in range(4):
                    last = tensor.matmul(ps_m[:, 192 + 64 * r:256 + 64 * r],
                                         selc[:, 128 * r:128 * (r + 1)], zdec[:, 0:64],
                                         start=True, stop=True)
                last.then_inc(g["s_dmm"], 1)
                for j in range(4):
                    c = 4 * gi + j
                    cb = c % 2
                    tensor.wait_ge(g["s_ldg"], 224 * (c + 1))
                    if j == 0:
                        tensor.wait_ge(g["s_dece"], gi + 1)
                    if c >= 1:
                        tensor.wait_ge(g["s_abe"], c)
                    tensor.transpose(ps_m[:, 0:64], absb[0:64, 0, gb, 128 * j:128 * (j + 1)], id64c[:])
                    tensor.transpose(ps_m[:, 128:192], absb[0:64, 1, gb, 128 * j:128 * (j + 1)],
                                     pm64c[:])
                    tensor.transpose(ps_m[:, 64:128], absb[0:64, 1, gb, 128 * j:128 * (j + 1)],
                                     id64c[:]).then_inc(g["s_ab"], 1)
                    bb0 = 34 * c
                    for kind in range(2):
                        lsp = vsp if kind == 0 else ksp
                        for sg in range(4):
                            tensor.drain()
                            for qr in range(4):
                                bb = bb0 + 16 * kind + 4 * sg + qr
                                if bb >= 2:
                                    w = batch_end6(bb - 2)
                                    tensor.wait_ge(g[w[0]], w[1])
                                last = None
                                for tt_ in range(4):
                                    tau = 4 * qr + tt_
                                    tp = (32 * sg, 0) if sg > 0 else None
                                    last = tensor.matmul(
                                        ps_sb[:, bb % 2, 128 * tt_:128 * (tt_ + 1)],
                                        lsp[32 * sg:32 * sg + 16, cb, tau],
                                        qsp[32 * sg:32 * sg + 16, cb, tau],
                                        start=True, stop=True, tile_position=tp)
                                last.then_inc(g["s_sb"], 1)
                    tensor.wait_ge(g["s_abe"], c + 1)
                    for sg in range(4):
                        tensor.drain()
                        tensor.wait_ge(g["s_sbe"], 16 * c + 4 * (sg + 1))
                        if c >= 1:
                            tensor.wait_ge(g["s_hn"], 4 * (c - 1) + sg + 1)
                            if sg == 0:
                                tensor.wait_ge(g["s_pv"], c)
                        if sg == 0:
                            tensor.wait_ge(g["s_ktl"], 2 * c + 1)
                        last = None
                        for tau in range(16):
                            h = 4 * tau + sg
                            last = tensor.matmul(ps_y[:, sg, 16 * tau:16 * (tau + 1)],
                                                 ssb[:, 16 * sg + tau, :],
                                                 ktl[:, 16 * h:16 * (h + 1)],
                                                 start=(tau == 0), stop=False)
                        if sg == 3:
                            last.then_inc(g["s_pv"], 1)
                        tensor.wait_ge(g["s_ske"], 16 * c + 4 * (sg + 1))
                        last = None
                        for tau in range(16):
                            h = 4 * tau + sg
                            last = tensor.matmul(ps_y[:, sg, 256 + tau:257 + tau],
                                                 skb[:, 16 * sg + tau, :],
                                                 abTb[:, h:h + 1],
                                                 start=False, stop=False)
                        if sg == 3:
                            last.then_inc(g["s_den"], 1)
                        tensor.drain()
                        tensor.wait_ge(g["s_scl"], 4 * c + sg + 1)
                        last = None
                        for tau in range(16):
                            tp = (32 * sg, 0) if sg > 0 else None
                            tensor.matmul(ps_y[:, sg, 16 * tau:16 * (tau + 1)],
                                          qsp[32 * sg:32 * sg + 16, cb, tau],
                                          cspb[32 * sg:32 * sg + 16, 17 * tau:17 * tau + 16],
                                          start=False, stop=False, tile_position=tp)
                            last = tensor.matmul(ps_y[:, sg, 256 + tau:257 + tau],
                                                 qsp[32 * sg:32 * sg + 16, cb, tau],
                                                 cspb[32 * sg:32 * sg + 16, 17 * tau + 16:17 * tau + 17],
                                                 start=False, stop=(tau == 15), tile_position=tp)
                        last.then_inc(g["s_yint"], 1)
                    tensor.wait_ge(g["s_ktl"], 2 * (c + 1))
                    tensor.wait_ge(g["s_up"], 4 * c)
                    for sg in range(4):
                        tensor.drain()
                        last = None
                        for tau in range(16):
                            h = 4 * tau + sg
                            tp = (0, 32 * sg) if sg > 0 else None
                            last = tensor.matmul(ps_c[32 * sg:32 * sg + 16, 17 * tau:17 * tau + 16],
                                                 vtl[:, 16 * h:16 * (h + 1)],
                                                 ksm_sb[:, cb, 16 * h:16 * (h + 1)],
                                                 start=True, stop=True, tile_position=tp)
                            last = tensor.matmul(ps_c[32 * sg:32 * sg + 16, 17 * tau + 16:17 * tau + 17],
                                                 ksm_sb[:, cb, 16 * h:16 * (h + 1)],
                                                 abTb[:, h:h + 1],
                                                 start=True, stop=True, tile_position=tp)
                        last.then_inc(g["s_ctm"], 1)
                    tensor.drain()
                    tensor.wait_ge(g["s_hn"], 4 * (c + 1))
                    for tb in range(2):
                        bb = 34 * c + 32 + tb
                        w = batch_end6(bb - 2)
                        tensor.wait_ge(g[w[0]], w[1])
                        last = None
                        for i in range(4):
                            last = tensor.transpose(
                                ps_sb[:, bb % 2, 128 * i:128 * (i + 1)],
                                hsm[:, 512 * tb + 128 * i:512 * tb + 128 * (i + 1)], idfc[:])
                        last.then_inc(g["s_ht"], 1)
                tensor.wait_ge(g["s_hte"], 8 * (gi + 1))
                tensor.wait_ge(g["s_abe"], 4 * (gi + 1))
                tensor.wait_ge(g["s_dece"], gi + 1)
                for m in range(8):
                    if 8 * gi + m >= 1:
                        tensor.wait_ge(g["s_ope"], 8 * gi + m)
                    last = None
                    for k in range(8):
                        last = tensor.matmul(ps_m[:], ow_sb[:, k, 128 * m:128 * (m + 1)],
                                             hcm[:, k, :], start=(k == 0), stop=(k == 7))
                    last.then_inc(g["s_opm"], 1)

        @block.scalar
        def _(scalar):
            if bar is not None:
                scalar.wait_ge(bar, ph)
            scalar.wait_ge(g["s_w"], 112)
            for gi in range(NG):
                scalar.wait_ge(g["s_dt"], gi + 1)
                if gi >= 1:
                    scalar.wait_ge(g["s_dmm"], gi)
                scalar.copy(zdec[0:4, 0:64], ps_m[0:4, 128:192]).then_inc(g["s_dte"], 1)
                scalar.wait_ge(g["s_dmm"], gi + 1)
                if gi >= 1:
                    scalar.wait_ge(g["s_scl"], 16 * gi)
                scalar.copy(decbq[:], ps_m[:, 192:448]).then_inc(g["s_dece"], 1)
                for j in range(4):
                    c = 4 * gi + j
                    scalar.wait_ge(g["s_ab"], c + 1)
                    if c >= 1:
                        scalar.wait_ge(g["s_hn"], 4 * c)
                        scalar.wait_ge(g["s_ktl"], 2 * c)
                        scalar.wait_ge(g["s_sbe"], 16 * c)
                        scalar.wait_ge(g["s_den"], c)
                        scalar.wait_ge(g["s_ctm"], 4 * c)
                    scalar.copy(abT[:], ps_m[:, 0:128])
                    scalar.copy(abTs[:], ps_m[:, 128:192])
                    scalar.copy(abTb[:], ps_m[:, 0:64]).then_inc(g["s_abe"], 1)
                    scalar.wait_ge(g["s_dnr"], c + 1)
                    scalar.activation(rcpb[:], dnneg[:], AF.Ln)
                    scalar.activation(rcpb[:], rcpb[:], AF.Exp,
                                      scale=-1.0).then_inc(g["s_rcp"], 1)

        @block.vector
        def _(vector):
            if bar is not None:
                vector.wait_ge(bar, ph)
            vector.wait_ge(g["s_w"], 112)
            vector.memset(csp[:], 0.0)
            vector.memset(cspb[:], 0.0)
            vector.memset(zdec[:], 0.0).then_inc(g["s_z0"], 1)
            for gi in range(NG):
                gb = gi % 2
                for j in range(4):
                    c = 4 * gi + j
                    cb = c % 2
                    vector.wait_ge(g["s_dece"], gi + 1)
                    for sg in range(4):
                        if c >= 1:
                            vector.wait_ge(g["s_yint"], 4 * (c - 1) + sg + 1)
                        cslice = csp[32 * sg:32 * sg + 16, :]
                        decview = decbq[32 * sg:32 * sg + 16, 64 * j:64 * (j + 1)].rearrange(
                            "p (t2 four) -> p t2 four", four=4)[:, :, sg].unsqueeze(2) \
                            .broadcast_to([16, 16, 17])
                        vector.tensor_tensor(
                            cspb[32 * sg:32 * sg + 16, :].rearrange("p (t2 s) -> p t2 s", s=17),
                            cslice.rearrange("p (t2 s) -> p t2 s", s=17),
                            decview, ALU.mult).then_inc(g["s_scl"], 1)
                        vector.tensor_tensor(cslice.rearrange("p (t2 s) -> p t2 s", s=17),
                                             cslice.rearrange("p (t2 s) -> p t2 s", s=17),
                                             decview, ALU.mult)
                    vector.wait_ge(g["s_abe"], c + 1)
                    vector.wait_ge(g["s_ldg"], 224 * (c + 1))
                    if c >= 1:
                        vector.wait_ge(g["s_ctm"], 4 * c)
                        vector.wait_ge(g["s_pv"], c)
                    vector.tensor_tensor(ktl[:].rearrange("p (h e) -> p h e", e=16),
                                         ksm_sb[:, cb].rearrange("p (h e) -> p h e", e=16),
                                         abT[:, 0:64].unsqueeze(2).broadcast_to([128, 64, 16]),
                                         ALU.mult).then_inc(g["s_ktl"], 1)
                    vector.tensor_tensor(vtl[:].rearrange("p (h e) -> p h e", e=16),
                                         vsm_sb[:, cb].rearrange("p (h e) -> p h e", e=16),
                                         abT[:, 0:64].unsqueeze(2).broadcast_to([128, 64, 16]),
                                         ALU.mult).then_inc(g["s_ktl"], 1)
                    for k2 in range(16):
                        bb = 34 * c + k2
                        vector.wait_ge(g["s_sb"], 32 * c + k2 + 1)
                        if c >= 1 and k2 == 0:
                            vector.wait_ge(g["s_pv"], c)
                        sg, qr = divmod(k2, 4)
                        dst = ssb[:, 16 * sg + 4 * qr:16 * sg + 4 * qr + 4, :]
                        vector.tensor_tensor(dst,
                                             ps_sb[:, bb % 2].rearrange("p (a b) -> p a b", a=4),
                                             m8c[:, 0:512].rearrange("p (a b) -> p a b", a=4),
                                             ALU.mult).then_inc(g["s_sbe"], 1)
                    for k2 in range(16):
                        bb = 34 * c + 16 + k2
                        vector.wait_ge(g["s_sb"], 32 * c + 16 + k2 + 1)
                        if c >= 1 and k2 == 0:
                            vector.wait_ge(g["s_den"], c)
                        sg, qr = divmod(k2, 4)
                        dst = skb[:, 16 * sg + 4 * qr:16 * sg + 4 * qr + 4, :]
                        vector.tensor_tensor(dst,
                                             ps_sb[:, bb % 2].rearrange("p (a b) -> p a b", a=4),
                                             m8c[:, 0:512].rearrange("p (a b) -> p a b", a=4),
                                             ALU.mult).then_inc(g["s_ske"], 1)
                    for sg in range(4):
                        vector.wait_ge(g["s_yint"], 4 * c + sg + 1)
                        vector.tensor_scalar_add(dnall[:, 16 * sg:16 * (sg + 1)],
                                                 ps_y[:, sg, 256:272], 0.0)
                    vector.wait_ge(g["s_ctm"], 4 * (c + 1))
                    for sg in range(4):
                        cslice = csp[32 * sg:32 * sg + 16, :]
                        vector.tensor_tensor(cslice.rearrange("p (t2 s) -> p t2 s", s=17),
                                             cslice.rearrange("p (t2 s) -> p t2 s", s=17),
                                             ps_c[32 * sg:32 * sg + 16, 0:272].rearrange(
                                                 "p (t2 s) -> p t2 s", s=17),
                                             ALU.add).then_inc(g["s_up"], 1)
                    vector.tensor_scalar_mul(dnneg[:], dnall[:], -1.0)
                    for _n in range(4):
                        vector.engine_nop()
                    vector.tensor_tensor(dnneg[:], dnneg[:], dnall[:], ALU.max)
                    for _n in range(4):
                        vector.engine_nop()
                    vector.tensor_tensor(dnneg[:], dnneg[:], abTs[:], ALU.mult)
                    for _n in range(4):
                        vector.engine_nop()
                    vector.tensor_scalar_max(dnneg[:], dnneg[:], 1.0).then_inc(g["s_dnr"], 1)
                    vector.wait_ge(g["s_rcp"], c + 1)
                    vector.tensor_tensor(rcpb2[:], rcpb[:], abTs[:], ALU.mult)
                    for _n in range(5):
                        vector.engine_nop()
                    for sg in range(4):
                        vector.tensor_tensor(
                            hsm[:].rearrange("p (t2 four e) -> p t2 four e", four=4, e=16)[:, :, sg, :],
                            ps_y[:, sg, 0:256].rearrange("p (t2 e) -> p t2 e", e=16),
                            rcpb2[:, 16 * sg:16 * (sg + 1)].unsqueeze(2).broadcast_to([128, 16, 16]),
                            ALU.mult).then_inc(g["s_hn"], 1)
                    for tb in range(2):
                        vector.wait_ge(g["s_ht"], 2 * c + tb + 1)
                        bb = 34 * c + 32 + tb
                        last = None
                        for i in range(4):
                            last = vector.tensor_copy(hcm[:, 4 * tb + i, 128 * j:128 * (j + 1)],
                                                      ps_sb[:, bb % 2, 128 * i:128 * (i + 1)])
                        last.then_inc(g["s_hte"], 1)
                for m in range(8):
                    vector.wait_ge(g["s_opm"], 8 * gi + m + 1)
                    vector.wait_ge(g["s_ld"], 64 * (gi + 1))
                    if m == 0 and gi >= 1:
                        vector.wait_ge(g["s_st"], 16 * gi)
                    vector.tensor_scalar_add(tmp512[:], ps_m[:], ob_sb[:, m:m + 1])
                    vector.tensor_tensor(s1_sb[:, m], tmp512[:], xng[:, m],
                                         ALU.add).then_inc(g["s_ope"], 1)

    return nc





QF_SEMS = ("s_w", "s_l1", "s_l2", "s_l3", "s_l4", "s_ab1", "s_ab3", "s_p1m", "s_p3m", "s_r2", "s_r4", "s_qs", "s_tt", "s_z", "s_bc", "s_gb", "s_gl", "s_c1", "s_c2", "s_lx", "s_lx1", "s_mmh", "s_he", "s_om", "s_sl", "s_oe", "s_st")


def build_qffn(nc=None, block=None, g=None, io=None, bar=None, ph=0):
    standalone = nc is None
    if standalone:
        nc = bass.Bass("TRN2", target_bir_lowering=False)
    def dram(name, shape, dt, kind):
        if io is not None:
            return io[name]
        return nc.dram_tensor(name, shape, dt, kind=kind).ap()
    x1T = dram("x1T", [D, SEQ], F32, "ExternalInput")
    xn1T = dram("xn1T", [D, SEQ], BF16, "ExternalInput")
    s1T = dram("s1T", [D, SEQ], BF16, "ExternalInput")
    W1T = dram("W1T", [D, 4096], F32, "ExternalInput")
    W2T = dram("W2T", [4096, D], F32, "ExternalInput")
    b1r = dram("b1r", [128, 32], F32, "ExternalInput")
    b2r = dram("b2r", [128, 8], F32, "ExternalInput")
    ones_f = dram("ones_f", [128, 1], F32, "ExternalInput")
    eps1 = dram("eps1", [1, 1], F32, "ExternalInput")
    sel32 = dram("sel32", [32, 32 * 128], F32, "ExternalInput")

    x2T = dram("x2T", [D, SEQ], F32, "ExternalOutput")

    NT2 = SEQ // 256
    NEL = float(D * 4096)
    w1r = W1T.rearrange("(k p) f -> p k f", p=128)
    w2r = W2T.rearrange("(k p) f -> p k f", p=128)
    xnr = xn1T.rearrange("(k p) n -> p k n", p=128)
    x1r = x1T.rearrange("(k p) n -> p k n", p=128)
    s1r = s1T.rearrange("(k p) n -> p k n", p=128)
    x2r = x2T.rearrange("(k p) n -> p k n", p=128)

    with contextlib.ExitStack() as ctx:
        ent = ctx.enter_context
        wq1 = ent(nc.sbuf_tensor("wq1", [128, 8, 4096], BF16))
        wq2 = ent(nc.sbuf_tensor("wq2", [128, 32, 1024], BF16))
        wpc = ent(nc.sbuf_tensor("wpc", [128, 2, 2048], F32))
        wi32 = ent(nc.sbuf_tensor("wi32", [128, 2048], mybir.dt.int32))
        qs = ent(nc.sbuf_tensor("qs", [128, 2], F32))
        gbc = ent(nc.sbuf_tensor("gbc", [128, 2], F32))
        z32q = ent(nc.sbuf_tensor("z32q", [32, 2], F32))
        gtmp = ent(nc.sbuf_tensor("gtmp", [1, 4], F32))
        selc = ent(nc.sbuf_tensor("selc", [32, 128], F32))
        onec = ent(nc.sbuf_tensor("onec", [128, 1], F32))
        epsc = ent(nc.sbuf_tensor("epsc", [1, 1], F32))
        b1_sb = ent(nc.sbuf_tensor("b1_sb", [128, 32], F32))
        b2_sb = ent(nc.sbuf_tensor("b2_sb", [128, 8], F32))
        xnt = ent(nc.sbuf_tensor("xnt", [128, 2, 8, 256], BF16))
        x1t = ent(nc.sbuf_tensor("x1t", [128, 8, 256], F32))
        s1t = ent(nc.sbuf_tensor("s1t", [128, 8, 256], BF16))
        hsb = ent(nc.sbuf_tensor("hsb", [128, 32, 256], BF16))
        htmp = ent(nc.sbuf_tensor("htmp", [128, 256], F32))
        otmp = ent(nc.sbuf_tensor("otmp", [128, 256], F32))
        g1b = ent(nc.sbuf_tensor("g1b", [128, 8, 256], BF16))
        tmp2 = ent(nc.sbuf_tensor("tmp2", [128, 8, 256], BF16))
        ps_h = ent(nc.psum_tensor("ps_h", [128, 2, 512], F32))
        ps_o = ent(nc.psum_tensor("ps_o", [128, 2, 512], F32))
        ps_q = ent(nc.psum_tensor("ps_q", [128, 512], F32))
        if standalone:
            g = {}
            for name in QF_SEMS:
                g[name] = ent(nc.semaphore(name))
            block = ent(nc.Block())

        @block.sync
        def _(sync):
            if bar is not None:
                sync.wait_ge(bar, ph)
            for ap_out, ap_in in ((b1_sb[:], b1r), (b2_sb[:], b2r),
                                  (onec[:], ones_f), (epsc[:], eps1),
                                  (selc[:], sel32[:, 0:128])):
                sync.dma_start(out=ap_out, in_=ap_in).then_inc(g["s_w"], 16)
            for p in range(16):
                if p >= 2:
                    sync.wait_ge(g["s_p1m"], 16 * (p - 1))
                sync.dma_start(out=wpc[:, p % 2],
                               in_=w1r[:, :, 256 * p:256 * (p + 1)]).then_inc(g["s_l1"], 16)
            for p in range(16):
                sync.wait_ge(g["s_p1m"], 256 if p < 2 else 256)
                if p >= 2:
                    sync.wait_ge(g["s_r2"], p - 1)
                sync.dma_start(out=wpc[:, p % 2],
                               in_=w1r[:, :, 256 * p:256 * (p + 1)]).then_inc(g["s_l2"], 16)
            for p in range(16):
                if p < 2:
                    sync.wait_ge(g["s_r2"], 15 + p)
                else:
                    sync.wait_ge(g["s_p3m"], 16 * (p - 1))
                sync.dma_start(out=wpc[:, p % 2],
                               in_=w2r[:, 2 * p:2 * (p + 1), :]).then_inc(g["s_l3"], 16)
            for p in range(16):
                sync.wait_ge(g["s_p3m"], 256)
                if p >= 2:
                    sync.wait_ge(g["s_r4"], p - 1)
                sync.dma_start(out=wpc[:, p % 2],
                               in_=w2r[:, 2 * p:2 * (p + 1), :]).then_inc(g["s_l4"], 16)
            for t in range(NT2):
                b = t % 2
                if t >= 2:
                    sync.wait_ge(g["s_mmh"], 256 * (t - 1))
                sync.dma_start(out=xnt[:, b],
                               in_=xnr[:, :, 256 * t:256 * (t + 1)]).then_inc(g["s_lx"], 16)
                sync.wait_ge(g["s_oe"], 8 * t)
                if t >= 1:
                    sync.dma_start(out=x2r[:, :, 256 * (t - 1):256 * t],
                                   in_=x1t[:]).then_inc(g["s_st"], 16)
                    sync.wait_ge(g["s_st"], 16 * t)
                sync.dma_start(out=x1t[:], in_=x1r[:, :, 256 * t:256 * (t + 1)]).then_inc(g["s_lx1"], 16)
                sync.dma_start(out=s1t[:], in_=s1r[:, :, 256 * t:256 * (t + 1)]).then_inc(g["s_lx1"], 16)
            sync.wait_ge(g["s_oe"], 8 * NT2)
            sync.dma_start(out=x2r[:, :, 256 * (NT2 - 1):256 * NT2],
                           in_=x1t[:]).then_inc(g["s_st"], 16)
            sync.wait_ge(g["s_st"], 16 * NT2)

        @block.tensor
        def _(tensor):
            if bar is not None:
                tensor.wait_ge(bar, ph)
            tensor.wait_ge(g["s_w"], 80)
            for p in range(16):
                tensor.wait_ge(g["s_ab1"], p + 1)
                for i in range(16):
                    mm = tensor.matmul(ps_q[:, 0:1],
                                       wpc[:, p % 2].rearrange("p (a m) -> p a m", m=128)[:, i],
                                       onec[:],
                                       start=(p == 0 and i == 0), stop=(p == 15 and i == 15))
                    mm.then_inc(g["s_p1m"], 1)
            tensor.wait_ge(g["s_qs"], 1)
            tensor.matmul(ps_q[0:1, 1:2], qs[:, 0:1], onec[:],
                          start=True, stop=True).then_inc(g["s_tt"], 1)
            tensor.wait_ge(g["s_z"], 1)
            tensor.matmul(ps_q[:, 2:3], selc[:], z32q[:, 0:1],
                          start=True, stop=True).then_inc(g["s_bc"], 1)
            for p in range(16):
                tensor.wait_ge(g["s_ab3"], p + 1)
                if p == 0:
                    tensor.wait_ge(g["s_gb"], 1)
                for i in range(16):
                    mm = tensor.matmul(ps_q[:, 0:1],
                                       wpc[:, p % 2].rearrange("p (a m) -> p a m", m=128)[:, i],
                                       onec[:],
                                       start=(p == 0 and i == 0), stop=(p == 15 and i == 15))
                    mm.then_inc(g["s_p3m"], 1)
            tensor.wait_ge(g["s_qs"], 2)
            tensor.matmul(ps_q[0:1, 1:2], qs[:, 1:2], onec[:],
                          start=True, stop=True).then_inc(g["s_tt"], 2)
            tensor.wait_ge(g["s_z"], 2)
            tensor.matmul(ps_q[:, 2:3], selc[:], z32q[:, 1:2],
                          start=True, stop=True).then_inc(g["s_bc"], 2)
            tensor.wait_ge(g["s_r2"], 16)
            tensor.wait_ge(g["s_r4"], 16)
            for t in range(NT2):
                b = t % 2
                tensor.wait_ge(g["s_lx"], 16 * (t + 1))
                for m in range(32):
                    gm = 32 * t + m
                    if gm >= 2:
                        tensor.wait_ge(g["s_he"], gm - 1)
                    last = None
                    for k in range(8):
                        last = tensor.matmul(ps_h[:, m % 2, 0:256],
                                             wq1[:, k, 128 * m:128 * (m + 1)],
                                             xnt[:, b, k], start=(k == 0), stop=(k == 7))
                    last.then_inc(g["s_mmh"], 8)
                tensor.wait_ge(g["s_he"], 32 * (t + 1))
                for mo in range(8):
                    go = 8 * t + mo
                    if go >= 2:
                        tensor.wait_ge(g["s_oe"], go - 1)
                    last = None
                    for k in range(32):
                        last = tensor.matmul(ps_o[:, mo % 2, 0:256],
                                             wq2[:, k, 128 * mo:128 * (mo + 1)],
                                             hsb[:, k], start=(k == 0), stop=(k == 31))
                    last.then_inc(g["s_om"], 1)

        @block.scalar
        def _(scalar):
            if bar is not None:
                scalar.wait_ge(bar, ph)
            scalar.wait_ge(g["s_w"], 80)
            for w_i in range(2):
                scalar.wait_ge(g["s_p1m" if w_i == 0 else "s_p3m"], 256)
                scalar.copy(qs[:, w_i:w_i + 1], ps_q[:, 0:1]).then_inc(g["s_qs"], 1)
                scalar.wait_ge(g["s_tt"], w_i + 1)
                scalar.mul(gtmp[0:1, 0:1], ps_q[0:1, 1:2], 1.0 / NEL)
                scalar.add(gtmp[0:1, 0:1], gtmp[0:1, 0:1], epsc[:])
                scalar.activation(gtmp[0:1, 0:1], gtmp[0:1, 0:1], AF.Ln)
                scalar.activation(gtmp[0:1, 1:2], gtmp[0:1, 0:1], AF.Exp,
                                  scale=-1.0).then_inc(g["s_gl"], 1)
                scalar.wait_ge(g["s_bc"], w_i + 1)
                scalar.copy(gbc[:, w_i:w_i + 1], ps_q[:, 2:3]).then_inc(g["s_gb"], 1)
                for p in range(16):
                    scalar.wait_ge(g["s_c1"], 16 * w_i + p + 1)
                    scalar.copy(wpc[:, p % 2], wi32[:]).then_inc(g["s_c2"], 1)
            for t in range(NT2):
                for m in range(32):
                    scalar.wait_ge(g["s_mmh"], 256 * t + 8 * (m + 1))
                    scalar.add(htmp[:], ps_h[:, m % 2, 0:256], b1_sb[:, m:m + 1])
                    scalar.activation(hsb[:, m], htmp[:], AF.Relu).then_inc(g["s_he"], 1)
                for mo in range(8):
                    scalar.wait_ge(g["s_om"], 8 * t + mo + 1)
                    scalar.add(otmp[:], ps_o[:, mo % 2, 0:256], b2_sb[:, mo:mo + 1])
                    scalar.activation(g1b[:, mo], otmp[:], AF.Silu).then_inc(g["s_sl"], 1)

        @block.vector
        def _(vector):
            if bar is not None:
                vector.wait_ge(bar, ph)
            vector.wait_ge(g["s_w"], 80)
            vector.memset(z32q[:], 0.0)
            for w_i in range(2):
                l_sem = "s_l1" if w_i == 0 else "s_l3"
                a_sem = "s_ab1" if w_i == 0 else "s_ab3"
                for p in range(16):
                    vector.wait_ge(g[l_sem], 16 * (p + 1))
                    v = wpc[:, p % 2].unsqueeze(1)
                    vector.scalar_tensor_tensor(v, v, -1.0, v, ALU.mult,
                                                ALU.max).then_inc(g[a_sem], 1)
                vector.wait_ge(g["s_gl"], w_i + 1)
                vector.tensor_tensor(z32q[0:1, w_i:w_i + 1], gtmp[0:1, 1:2],
                                     gtmp[0:1, 1:2], ALU.max).then_inc(g["s_z"], 1)
                l2_sem = "s_l2" if w_i == 0 else "s_l4"
                r_sem = "s_r2" if w_i == 0 else "s_r4"
                def minmax(p2):
                    wbp = wpc[:, p2 % 2]
                    vector.tensor_scalar_min(wbp, wbp, 1.0)
                    if w_i == 0:
                        dst = wq1[:, :, 256 * p2:256 * (p2 + 1)]
                        src = wbp.rearrange("p (a f) -> p a f", f=256)
                    else:
                        dst = wq2[:, 2 * p2:2 * (p2 + 1), :]
                        src = wbp.rearrange("p (a f) -> p a f", f=1024)
                    vector.tensor_scalar_max(dst, src, -1.0).then_inc(g[r_sem], 1)

                for p in range(16):
                    vector.wait_ge(g[l2_sem], 16 * (p + 1))
                    if p == 0:
                        vector.wait_ge(g["s_gb"], w_i + 1)
                    wb = wpc[:, p % 2]
                    if p >= 1:
                        vector.wait_ge(g["s_c2"], 16 * w_i + p)
                    vector.tensor_scalar_mul(wb, wb, gbc[:, w_i:w_i + 1])
                    if p >= 1:
                        minmax(p - 1)
                    vector.tensor_copy(wi32[:], wb).then_inc(g["s_c1"], 1)
                vector.wait_ge(g["s_c2"], 16 * w_i + 16)
                minmax(15)
            for t in range(NT2):
                vector.wait_ge(g["s_lx1"], 32 * (t + 1))
                for mo in range(9):
                    if mo < 8:
                        vector.wait_ge(g["s_sl"], 8 * t + mo + 1)
                        vector.tensor_tensor(tmp2[:, mo], g1b[:, mo], s1t[:, mo], ALU.mult)
                    if mo >= 1:
                        vector.tensor_tensor(x1t[:, mo - 1], x1t[:, mo - 1], tmp2[:, mo - 1],
                                             ALU.add).then_inc(g["s_oe"], 1)

    return nc


_TALLY = [None]
_PATCHED = [False]


def _install_tally():
    if _PATCHED[0]:
        return
    nc2 = bass.Bass("TRN2", target_bir_lowering=False)
    captured = []
    with contextlib.ExitStack() as c2:
        t2 = c2.enter_context(nc2.sbuf_tensor("t2", [1, 4], F32))
        d2 = nc2.dram_tensor("d2", [1, 4], F32, kind="ExternalInput").ap()
        b2 = c2.enter_context(nc2.Block())

        @b2.vector
        def _(vector):
            captured.append(vector.memset(t2[:], 0.0))

        @b2.sync
        def _(sync):
            captured.append(sync.dma_start(out=t2[:], in_=d2))
    classes = {type(op) for op in captured}
    for cls in classes:
        orig = cls.then_inc

        def make(orig):
            def wrapper(self, sem, n):
                rec = _TALLY[0]
                if rec is not None:
                    tally, names = rec
                    nm = names.get(id(sem))
                    if nm is not None:
                        tally[nm] = tally.get(nm, 0) + n
                return orig(self, sem, n)
            return wrapper
        cls.then_inc = make(orig)
    _PATCHED[0] = True


FUSED_PHASES = None


def _fused_phases():
    return [
        (build_pre, PRE_SEMS, ("s_cast",)),
        (build_mamba_front, MF_SEMS, ("s_ld", "s_wg")),
        (build_ssd, SSD_SEMS, ("s_ldz", "s_wg")),
        (build_l0ffn, L0F_SEMS, ("s_ldg", "s_wg")),
        (build_l1front, L1F_SEMS, ("s_wg",)),
        (build_mlstm, ML_SEMS, ("s_ldg", "s_wg")),
        (build_qffn, QF_SEMS, ()),
    ]


class _NCProxy:
    def __init__(self, nc, prefix):
        self._nc = nc
        self._p = prefix

    def sbuf_tensor(self, name, *a, **k):
        return self._nc.sbuf_tensor(self._p + name, *a, **k)

    def psum_tensor(self, name, *a, **k):
        return self._nc.psum_tensor(self._p + name, *a, **k)

    def __getattr__(self, a):
        return getattr(self._nc, a)


def build_fused(debug_mids=False):
    _install_tally()
    nc = bass.Bass("TRN2", target_bir_lowering=False)
    io = {}

    def ext(name, shape, dt):
        io[name] = nc.dram_tensor(name, shape, dt, kind="ExternalInput").ap()

    def mid(name, shape, dt):
        kind = "ExternalOutput" if debug_mids else "Internal"
        io[name] = nc.dram_tensor(name, shape, dt, kind=kind).ap()

    ext("xT", [D, SEQ], F32)
    for name, shape in (("ones_col_bf", [128, 1]), ("ident128b", [128, 128]),
                        ("maskT", [128, 128]), ("mask8", [128, 8 * 128])):
        ext(name, shape, BF16)
    for name, shape in (("sel32", [32, 32 * 128]), ("ident128", [128, 128]),
                        ("ident32f", [32, 32]), ("ident64f", [64, 64]),
                        ("perm64", [64, 64]), ("c44", [64, 1]), ("c44n", [64, 1]),
                        ("c44_128", [128, 1]), ("ones_f", [128, 1]), ("eps1", [1, 1]),
                        ("in_proj_w", [D, 4384]), ("conv_wr", [128, 18, 4]),
                        ("conv_br", [128, 18]), ("dt_biasr", [32, 1]),
                        ("A_logr", [32, 1]), ("cmask32", [32, 512]),
                        ("out_proj_w", [2048, D]), ("norm_wr", [128, 16]),
                        ("D_paramr", [32, 1]),
                        ("WrT", [D, D]), ("WkT", [D, D]), ("WvT", [D, D]),
                        ("murr", [128, 8]), ("mukr", [128, 8]),
                        ("gmr", [128, 8]), ("btr", [128, 8]),
                        ("qwT", [D, D]), ("kwT", [D, D]), ("vwT", [D, D]),
                        ("iwT", [D, 64]), ("fwT", [D, 64]),
                        ("qbr", [128, 8]), ("kbr2", [128, 8]), ("vbr2", [128, 8]),
                        ("kbrow", [32, D]), ("vbrow", [32, D]),
                        ("ibr", [64, 1]), ("fbr", [64, 1]),
                        ("owT", [D, D]), ("obr", [128, 8]),
                        ("W1T", [D, 4096]), ("W2T", [4096, D]),
                        ("b1r", [128, 32]), ("b2r", [128, 8])):
        ext(name, shape, F32)
    for name, shape, dt in (("xn0T", [D, SEQ], BF16), ("szT", [2048, SEQ], BF16),
                            ("xcT", [2048, SEQ], BF16), ("BT", [128, SEQ], BF16),
                            ("CT", [128, SEQ], BF16), ("dtq", [32, SEQ], F32),
                            ("Acq", [32, SEQ], F32), ("s0T", [D, SEQ], BF16),
                            ("x1T", [D, SEQ], F32), ("xn1T", [D, SEQ], BF16),
                            ("qT", [D, SEQ], BF16), ("kT", [D, SEQ], BF16),
                            ("vT", [D, SEQ], BF16), ("ksm", [SEQ, D], BF16),
                            ("vsm", [SEQ, D], BF16), ("atil", [64, SEQ], F32),
                            ("btil", [64, SEQ], F32), ("decd", [64, SEQ // 128], F32),
                            ("s1T", [D, SEQ], BF16)):
        mid(name, shape, dt)
    io["x2T"] = nc.dram_tensor("x2T", [D, SEQ], F32, kind="ExternalOutput").ap()

    phases = _fused_phases()
    npool = max(len(sems) for _, sems, _ in phases)
    with contextlib.ExitStack() as ctx:
        ent = ctx.enter_context
        pool = [ent(nc.semaphore(f"pool{i}")) for i in range(npool)]
        bar = ent(nc.semaphore("bar"))
        done = {e: ent(nc.semaphore(f"done_{e}")) for e in
                ("sync", "gpsimd", "tensor", "scalar")}
        block = ent(nc.Block())
        for ph, (fn, names, swdge) in enumerate(phases):
            g = {}
            pi = 0
            for n in names:
                if n in swdge:
                    g[n] = ent(nc.semaphore(f"dg{ph}_{n}"))
                else:
                    g[n] = pool[pi]
                    pi += 1
            tally = {}
            _TALLY[0] = (tally, {id(g[n]): n for n in names})
            fn(nc=_NCProxy(nc, f"p{ph}_"), block=block, g=g, io=io, bar=bar, ph=ph)
            _TALLY[0] = None

            last = ph == len(phases) - 1

            @block.sync
            def _(sync):
                sync.sem_inc(done["sync"], 1)

            @block.gpsimd
            def _(gpsimd):
                gpsimd.sem_inc(done["gpsimd"], 1)

            @block.tensor
            def _(tensor):
                tensor.sem_inc(done["tensor"], 1)

            @block.scalar
            def _(scalar):
                scalar.sem_inc(done["scalar"], 1)

            @block.vector
            def _(vector, tally=dict(tally), g=dict(g), names=tuple(names),
                  swdge=tuple(swdge), last=last, ph=ph):
                for n in sorted(tally):
                    vector.wait_ge(g[n], tally[n])
                for e in ("sync", "gpsimd", "tensor", "scalar"):
                    vector.wait_ge(done[e], ph + 1)
                if not last:
                    for n in names:
                        if n not in swdge:
                            vector.sem_clear(g[n])
                    vector.sem_inc(bar, 1)
    return nc



def kernel(**inputs):
    consts = host_consts()
    AC = np.ascontiguousarray
    x = inputs["L0_attn_u"]

    cmask = np.ones((32, 512), np.float32)
    cmask[:, ::128] = 0.0

    ipw = AC(inputs["L0_attn_in_proj_w"].astype(np.float32))
    conv_wr = AC(inputs["L0_attn_conv_w"].reshape(18, 128, 4).transpose(1, 0, 2).astype(np.float32))
    conv_br = AC(inputs["L0_attn_conv_b"].reshape(18, 128).T.astype(np.float32))
    dt_biasr = AC(inputs["L0_attn_dt_bias"].reshape(32, 1).astype(np.float32))
    A_logr = AC(inputs["L0_attn_A_log"].reshape(32, 1).astype(np.float32))
    D_paramr = AC(inputs["L0_attn_D_param"].reshape(32, 1).astype(np.float32))
    norm_wr = AC(inputs["L0_attn_norm_w"].reshape(16, 128).T.astype(np.float32))
    opw = AC(inputs["L0_attn_out_proj_w"].astype(np.float32))
    WrT = AC(inputs["L0_ffn_W_r"].T.astype(np.float32))
    WkT = AC(inputs["L0_ffn_W_k"].T.astype(np.float32))
    WvT = AC(inputs["L0_ffn_W_v"].T.astype(np.float32))
    murr = AC(inputs["L0_ffn_mu_r"].reshape(8, 128).T.astype(np.float32))
    mukr = AC(inputs["L0_ffn_mu_k"].reshape(8, 128).T.astype(np.float32))
    gmr = AC(inputs["L1_attn_gamma"].reshape(8, 128).T.astype(np.float32))
    btr = AC(inputs["L1_attn_beta"].reshape(8, 128).T.astype(np.float32))
    qwT = AC(inputs["L1_attn_qw"].T.astype(np.float32))
    kwT = AC(inputs["L1_attn_kw"].T.astype(np.float32))
    vwT = AC(inputs["L1_attn_vw"].T.astype(np.float32))
    iwT = AC(inputs["L1_attn_iw"].T.astype(np.float32))
    fwT = AC(inputs["L1_attn_fw"].T.astype(np.float32))
    qbr = AC(inputs["L1_attn_qb"].reshape(8, 128).T.astype(np.float32))
    kbr2 = AC(inputs["L1_attn_kb"].reshape(8, 128).T.astype(np.float32))
    vbr2 = AC(inputs["L1_attn_vb"].reshape(8, 128).T.astype(np.float32))
    kbrow = np.zeros((32, D), np.float32)
    kbrow[0] = inputs["L1_attn_kb"]
    vbrow = np.zeros((32, D), np.float32)
    vbrow[0] = inputs["L1_attn_vb"]
    ibr = AC(inputs["L1_attn_ib"].reshape(64, 1).astype(np.float32))
    fbr = AC(inputs["L1_attn_fb"].reshape(64, 1).astype(np.float32))
    owT = AC(inputs["L1_attn_ow"].T.astype(np.float32))
    obr = AC(inputs["L1_attn_ob"].reshape(8, 128).T.astype(np.float32))
    W1T = AC(inputs["L1_ffn_W1"].T.astype(np.float32))
    W2T = AC(inputs["L1_ffn_W2"].T.astype(np.float32))
    b1r = AC(inputs["L1_ffn_b1"].reshape(32, 128).T.astype(np.float32))
    b2r = AC(inputs["L1_ffn_b2"].reshape(8, 128).T.astype(np.float32))

    cores = list(range(NCORE))
    xT = [AC(x[b].T) for b in cores]

    shared = dict(
        ones_col_bf=consts["ones_col_bf"], sel32=consts["sel32"],
        ident128=consts["ident128"], ident128b=consts["ident128b"],
        ident32f=consts["ident32f"], ident64f=consts["ident64f"],
        perm64=consts["perm64"], maskT=consts["maskT"], mask8=consts["mask8"],
        c44=consts["c44"], c44n=consts["c44n"], c44_128=consts["c44_128"],
        ones_f=consts["ones_f"], eps1=consts["eps1"],
        in_proj_w=ipw, conv_wr=conv_wr, conv_br=conv_br, dt_biasr=dt_biasr,
        A_logr=A_logr, cmask32=cmask, out_proj_w=opw, norm_wr=norm_wr,
        D_paramr=D_paramr, WrT=WrT, WkT=WkT, WvT=WvT, murr=murr, mukr=mukr,
        gmr=gmr, btr=btr, qwT=qwT, kwT=kwT, vwT=vwT, iwT=iwT, fwT=fwT,
        qbr=qbr, kbr2=kbr2, vbr2=vbr2, kbrow=kbrow, vbrow=vbrow, ibr=ibr,
        fbr=fbr, owT=owT, obr=obr, W1T=W1T, W2T=W2T, b1r=b1r, b2r=b2r,
    )
    nc = build_fused()
    res = run_bass_kernel_spmd(nc, [dict(xT=xT[b], **shared) for b in cores],
                               core_ids=cores)
    out = np.stack([np.asarray(res.results[b]["x2T"], np.float32).T for b in cores])
    return out
